# version 21
# baseline (speedup 1.0000x reference)
import math
from contextlib import ExitStack

import numpy as np
import ml_dtypes

import concourse.bass as bass
import concourse.mybir as mybir
from concourse.bass_utils import run_bass_kernel_spmd

F32 = mybir.dt.float32
BF16 = mybir.dt.bfloat16
AF = mybir.ActivationFunctionType
ALU = mybir.AluOpType
AX = mybir.AxisListType
MAGIC = float(np.float32(1.5 * 2**23))
NEG = -1e9
EPS_LN = 1e-5
EPS_RMS = float(np.finfo(np.float32).eps)
QMAX = 127.0

NCORES = 8
SEQ = 2
N = 2048
R = SEQ * N
D = 1024
DC = D // 128
H = 8
E = 64
F0 = 4096
FC0 = F0 // 128
HID = 2048
HC = HID // 128
QK = 128
I1 = 2816
IC1 = I1 // 128


class Prog:

    ENGINES = ("sync", "tensor", "vector", "scalar", "gpsimd")
    QUEUES = ("qs", "qg", "qa", "qv", "q5", "q6")

    def __init__(self, nc, ctx):
        self.nc = nc
        self.ops = []
        self.join_pos = set()
        self.sem = {}
        self.count = {e: 0 for e in self.ENGINES}
        self.dma_count = {q: 0 for q in self.QUEUES}
        for e in self.ENGINES:
            self.sem[e] = ctx.enter_context(nc.semaphore("sm_" + e))
        for q in self.QUEUES:
            self.sem[q] = ctx.enter_context(nc.semaphore("sm_" + q))
        self.last_write = {}
        self.readers = {}

    def _record(self, eng, fn, reads, writes, waits, tsem_name, tamt):
        w = list(waits)
        for b in reads:
            t = self.last_write.get(b)
            if t:
                w.append(t)
        for b in writes:
            t = self.last_write.get(b)
            if t:
                w.append(t)
            w.extend(self.readers.get(b, ()))
        if tsem_name in self.dma_count:
            self.dma_count[tsem_name] += tamt
            tok = (tsem_name, self.dma_count[tsem_name])
        else:
            self.count[eng] += tamt
            tok = (eng, self.count[eng])
        self.ops.append((eng, fn, tuple(w), self.sem[tsem_name], tamt))
        for b in reads:
            self.readers.setdefault(b, []).append(tok)
        for b in writes:
            self.last_write[b] = tok
            self.readers[b] = []
        return tok

    def op(self, eng, fn, reads=(), writes=(), waits=()):
        return self._record(eng, fn, reads, writes, waits, eng, 1)

    def peek_raw(self, bufs):
        out = []
        for b in bufs:
            t = self.last_write.get(b)
            if t:
                out.append(t)
        return out

    def peek_war(self, bufs):
        out = []
        for b in bufs:
            t = self.last_write.get(b)
            if t:
                out.append(t)
            out.extend(self.readers.get(b, ()))
        return out

    def dma(self, queue, eng, fn, reads=(), writes=(), waits=()):
        return self._record(eng, fn, reads, writes, waits, queue, 16)

    def barrier_token(self, eng):
        return (eng, self.count[eng])

    def join(self):
        self.join_pos.add(len(self.ops))
        toks = [(e, self.count[e]) for e in self.ENGINES] + \
               [(q, self.dma_count[q]) for q in self.QUEUES]
        for e in self.ENGINES:
            self.op(e, lambda h: h.nop(), waits=list(toks))

    def dma_token(self, queue):
        return (queue, self.dma_count[queue])

    def emit(self, block):
        qname = {}
        for q in self.QUEUES:
            qname[id(self.sem[q])] = q
        dma_running = {q: 0 for q in self.QUEUES}
        resolved = []
        for eng, fn, waits, tsem, tamt in self.ops:
            w = []
            for (s, c) in waits:
                if s in dma_running:
                    c = dma_running[s]
                if c > 0:
                    w.append((s, c))
            is_q = qname.get(id(tsem))
            resolved.append([eng, fn, w, tsem, tamt, is_q])
            if is_q:
                dma_running[is_q] += tamt
        boundaries = {q: [0] for q in self.QUEUES}
        running = {q: 0 for q in self.QUEUES}
        for pos, rec in enumerate(resolved):
            if pos in self.join_pos:
                for q in self.QUEUES:
                    if running[q] > boundaries[q][-1]:
                        boundaries[q].append(running[q])
            eng, fn, w, tsem, tamt, is_q = rec
            if is_q:
                if w and running[is_q] > boundaries[is_q][-1]:
                    boundaries[is_q].append(running[is_q])
                running[is_q] += tamt
            elif w:
                for q in self.QUEUES:
                    if running[q] > boundaries[q][-1]:
                        pass
        for q in self.QUEUES:
            if running[q] > boundaries[q][-1]:
                boundaries[q].append(running[q])

        def round_up(q, c):
            import bisect
            b = boundaries[q]
            i = bisect.bisect_left(b, c)
            return b[i] if i < len(b) else b[-1]

        per_eng = {e: [] for e in self.ENGINES}
        for eng, fn, w, tsem, tamt, is_q in resolved:
            w = [(s, round_up(s, c) if s in self.QUEUES else c) for (s, c) in w]
            per_eng[eng].append((fn, w, tsem, tamt, bool(is_q)))
        for eng in self.ENGINES:
            steps = per_eng[eng]
            if not steps:
                continue
            sems = self.sem

            def body(handle, steps=steps):
                seen = {}
                for fn, w, tsem, tamt, isdma in steps:
                    for (s, c) in w:
                        if isdma or seen.get(s, 0) < c:
                            handle.wait_ge(sems[s], c)
                            seen[s] = max(seen.get(s, 0), c)
                    fn(handle).then_inc(tsem, tamt)

            getattr(block, eng)(body)


class ConstBank:

    def __init__(self, P, nc, ctx, n=16, pfx=""):
        self.P = P
        self.tile = ctx.enter_context(
            nc.sbuf_tensor(pfx + "konst", [128, n], F32))
        self.vals = {}
        self.pfx = pfx

    def get(self, val):
        val = float(val)
        if val not in self.vals:
            i = len(self.vals)
            self.vals[val] = i
            self.P.op("vector",
                      lambda v, i=i, val=val: v.memset(
                          self.tile[:, i:i + 1], val),
                      writes=[(self.pfx + "konst", i)])
        return self.tile[:, self.vals[val]:self.vals[val] + 1]

    def key(self, val):
        return (self.pfx + "konst", self.vals[float(val)])


def k_loop(P, psum_ap, pairs, reads, writes, waits=()):
    n = len(pairs)
    first_waits = list(waits) + P.peek_raw(reads) + P.peek_war(writes)
    for k, (l, r) in enumerate(pairs):
        P.op("tensor",
             lambda t, o=psum_ap, l=l, r=r, st=(k == 0), sp=(k == n - 1):
             t.matmul(o, l, r, start=st, stop=sp),
             reads=(reads if k == n - 1 else ()),
             writes=(writes if k == n - 1 else ()),
             waits=(first_waits if k == 0 else ()))



def _host_masks():
    j = np.arange(128)[:, None]
    i = np.arange(512)[None, :]
    mask_T = np.stack([np.where(d * 128 + j <= i, 0.0, NEG)
                       for d in range(4)]).astype(np.float32)
    ii = np.arange(128)[:, None]
    jj = np.arange(512)[None, :]
    mask_IJ = np.stack([np.where(jj <= v * 128 + ii, 0.0, NEG)
                        for v in range(4)]).astype(np.float32)
    return mask_T, mask_IJ


def _host_selpair():
    sel = np.zeros((8, 8 * 128), np.float32)
    for p in range(4):
        sel[2 * p, p * 128:p * 128 + 64] = 1.0
        sel[2 * p + 1, p * 128 + 64:p * 128 + 128] = 1.0
    return sel


def _cols(vec, nch):
    return np.ascontiguousarray(np.asarray(vec, np.float32)
                                .reshape(nch, 128).T)



def emit_phW(nc, P, ctx, d, pfx="phW_", part=0):



    sb = lambda n, s, dt=F32: ctx.enter_context(nc.sbuf_tensor(pfx + n, s, dt))
    psn = lambda n, s, dt=F32: ctx.enter_context(nc.psum_tensor(pfx + n, s, dt))
    fb = [sb(f"fb{i}", [128, 4096]) for i in range(2)]
    bb = [sb(f"bb{i}", [128, 4096], BF16) for i in range(2)]

    def cast_stream(src, dst, nrows, ncols):
        for c in range(nrows // 128):
            s = c % 2
            P.dma("qs" if s == 0 else "qa", "sync",
                  lambda e, o=fb[s], i=src[c * 128:(c + 1) * 128, :],
                         n_=ncols: e.dma_start(out=o[:, 0:n_], in_=i),
                  writes=[f"fb{s}"])
            P.op("vector",
                 lambda v, o=bb[s], i=fb[s], n_=ncols:
                 v.tensor_copy(o[:, 0:n_], i[:, 0:n_]),
                 reads=[f"fb{s}"], writes=[f"bb{s}"])
            P.dma("qg", "gpsimd",
                  lambda e, o=dst[c * 128:(c + 1) * 128, :], i=bb[s],
                         n_=ncols: e.dma_start(out=o, in_=i[:, 0:n_]),
                  reads=[f"bb{s}"])

    if part in (0, 1):
        cast_stream(d["w1"], d["w1b"], D, F0)
        cast_stream(d["w2"], d["w2b"], F0, D)
        cast_stream(d["Wh"], d["Whb"], D, 2 * HID)
        cast_stream(d["Wo"], d["Wob"], HID, D)
    if part == 1:
        return

    ident = sb("ident", [128, 128])
    ones1 = sb("ones1", [1, 128])
    P.dma("qs", "sync", lambda e: e.dma_start(out=ident[:], in_=d["identf"]),
          writes=["ident"])
    P.op("vector", lambda v: v.memset(ones1[:], 1.0), writes=["ones1"])

    nat = sb("nat", [128, 4096])
    scol = sb("scol", [128, 1])
    ps_r = psn("ps_r", [1, 128])
    srow = {w: sb(f"srow_{w}", [1, 4096]) for w in ("wg", "wu", "wd")}
    sgcol = sb("sgcol", [128, IC1]); sucol = sb("sucol", [128, IC1])
    coltile = {"wg": sgcol, "wu": sucol}
    for wname, nchunks, ncols in (("wg", IC1, D), ("wu", IC1, D),
                                  ("wd", DC, I1)):
        for c in range(nchunks):
            P.dma("qv", "sync",
                  lambda e, i=d[wname][c * 128:(c + 1) * 128, :], n_=ncols:
                  e.dma_start(out=nat[:, 0:n_], in_=i), writes=["nat"])
            P.op("vector",
                 lambda v, n_=ncols: v.reduce_max(
                     scol[:], nat[:, 0:n_], axis=AX.X,
                     apply_absolute_value=True),
                 reads=["nat"], writes=["scol"])
            P.op("vector",
                 lambda v: v.tensor_scalar(scol[:], scol[:], 1.0 / QMAX,
                                           None, ALU.mult),
                 reads=["scol"], writes=["scol"])
            if wname in coltile:
                P.op("vector",
                     lambda v, o=coltile[wname], cc=c: v.tensor_copy(
                         o[:, cc:cc + 1], scol[:]),
                     reads=["scol"], writes=[("wcol", wname, c)])
            P.op("tensor",
                 lambda t: t.transpose(ps_r[:], scol[:, 0:1], ident[:]),
                 reads=["scol", "ident"], writes=["ps_r"])
            P.op("vector",
                 lambda v, o=srow[wname], cc=c:
                 v.tensor_copy(o[:, cc * 128:(cc + 1) * 128], ps_r[:]),
                 reads=["ps_r"], writes=[f"srow_{wname}_{c}"])
    P.dma("qg", "gpsimd",
          lambda e: e.dma_start(out=d["sgc"], in_=sgcol[:]),
          reads=[("wcol", "wg", c) for c in range(IC1)])
    P.dma("qg", "gpsimd",
          lambda e: e.dma_start(out=d["suc"], in_=sucol[:]),
          reads=[("wcol", "wu", c) for c in range(IC1)])

    rrow = sb("rrow", [1, 4096])
    ps_b = psn("ps_b", [128, 512])
    bc = sb("bc", [128, 4096])
    sbc = sb("sbc", [128, 4096])
    qf = sb("qf", [128, 4096])
    qb = sb("qb", [128, 4096], BF16)

    for wname, tname, oname, scaled in (("wg", "wgT", "wgTi", False),
                                        ("wu", "wuT", "wuTi", False),
                                        ("wd", "wdT", "wdTq", True)):
        ncols = I1 if wname != "wd" else D
        nchunks = DC if wname != "wd" else IC1
        nsch = IC1 if wname != "wd" else DC
        P.op("vector",
             lambda v, s=srow[wname], n_=ncols:
             v.reciprocal(rrow[:, 0:n_], s[:, 0:n_]),
             reads=[f"srow_{wname}_{c}" for c in range(nsch)],
             writes=["rrow"])
        for st in range((ncols + 511) // 512):
            lo, hi = st * 512, min((st + 1) * 512, ncols)
            P.op("tensor",
                 lambda t, lo=lo, hi=hi: t.matmul(
                     ps_b[:, 0:hi - lo], ones1[:], rrow[0:1, lo:hi],
                     start=True, stop=True),
                 reads=["rrow", "ones1"], writes=["ps_b"])
            P.op("vector",
                 lambda v, lo=lo, hi=hi:
                 v.tensor_copy(bc[:, lo:hi], ps_b[:, 0:hi - lo]),
                 reads=["ps_b"], writes=[("bc", st)])
        if scaled:
            for st in range((ncols + 511) // 512):
                lo, hi = st * 512, min((st + 1) * 512, ncols)
                P.op("tensor",
                     lambda t, lo=lo, hi=hi, s=srow[wname]: t.matmul(
                         ps_b[:, 0:hi - lo], ones1[:], s[0:1, lo:hi],
                         start=True, stop=True),
                     reads=[f"srow_{wname}_{c}" for c in range(nsch)]
                     + ["ones1"], writes=["ps_b"])
                P.op("vector",
                     lambda v, lo=lo, hi=hi:
                     v.tensor_copy(sbc[:, lo:hi], ps_b[:, 0:hi - lo]),
                     reads=["ps_b"], writes=[("sbc", st)])
        bkeys = [("bc", st) for st in range((ncols + 511) // 512)]
        skeys = [("sbc", st) for st in range((ncols + 511) // 512)] \
            if scaled else []
        for c in range(nchunks):
            P.dma("q5", "sync",
                  lambda e, i=d[tname][c * 128:(c + 1) * 128, :], n_=ncols:
                  e.dma_start(out=qf[:, 0:n_], in_=i), writes=["qf"])
            P.op("vector",
                 lambda v, n_=ncols:
                 v.tensor_mul(qf[:, 0:n_], qf[:, 0:n_], bc[:, 0:n_]),
                 reads=["qf"] + bkeys, writes=["qf"])
            P.op("vector",
                 lambda v, n_=ncols: v.tensor_scalar(
                     qf[:, 0:n_], qf[:, 0:n_], MAGIC, None, ALU.add),
                 reads=["qf"], writes=["qf"])
            P.op("vector",
                 lambda v, n_=ncols: v.tensor_scalar(
                     qf[:, 0:n_], qf[:, 0:n_], MAGIC, None, ALU.subtract),
                 reads=["qf"], writes=["qf"])
            P.op("vector",
                 lambda v, n_=ncols: v.tensor_scalar(
                     qf[:, 0:n_], qf[:, 0:n_], -QMAX - 1.0, QMAX,
                     ALU.max, ALU.min),
                 reads=["qf"], writes=["qf"])
            if scaled:
                P.op("vector",
                     lambda v, n_=ncols:
                     v.tensor_mul(qf[:, 0:n_], qf[:, 0:n_], sbc[:, 0:n_]),
                     reads=["qf"] + skeys, writes=["qf"])
            P.op("scalar",
                 lambda a_, n_=ncols: a_.copy(qb[:, 0:n_], qf[:, 0:n_]),
                 reads=["qf"], writes=["qb"])
            P.dma("qg", "gpsimd",
                  lambda e, o=d[oname][c * 128:(c + 1) * 128, :], n_=ncols:
                  e.dma_start(out=o, in_=qb[:, 0:n_]), reads=["qb"])

    P.op("gpsimd", lambda g: g.engine_nop(),
         waits=[P.dma_token("qg"), P.dma_token("qs")])
    return


def emit_ph0(nc, P, ctx, d, pfx="ph0_"):
    x0 = d["x0"]; identb = d["identb"]; identf = d["identf"]
    xnfm = d["xnfm"]; tsc = d["tsc"]
    sb = lambda n, s, dt=F32: ctx.enter_context(nc.sbuf_tensor(pfx + n, s, dt))
    psn = lambda n, s, dt=F32: ctx.enter_context(nc.psum_tensor(pfx + n, s, dt))
    xt = [sb(f"xt{i}", [128, D]) for i in range(2)]
    sq_ = [sb(f"sq{u}", [128, D]) for u in range(2)]
    mcol_ = [sb(f"mcol{u}", [128, 1]) for u in range(2)]
    vcol_ = [sb(f"vcol{u}", [128, 1]) for u in range(2)]
    rstd_ = [sb(f"rstd{u}", [128, 1]) for u in range(2)]
    r2_ = [sb(f"r2{u}", [128, 1]) for u in range(2)]
    xn_ = [sb(f"xn{u}", [128, D]) for u in range(2)]
    xnb_ = [sb(f"xnb{u}", [128, D], BF16) for u in range(2)]
    idb = sb("idb", [128, 128], BF16)
    idf = sb("idf", [128, 128], F32)
    tcols = sb("tcols", [128, 32])
    fm = [sb(f"fm{c}", [128, R], BF16) for c in range(DC)]
    pstr_ = [psn(f"pstr{u}", [128, 128], BF16) for u in range(2)]
    ps_t = psn("ps_t", [32, 128], F32)
    K = ConstBank(P, nc, ctx, pfx=pfx)
    b_ln = K.get(EPS_LN); k_ln = K.key(EPS_LN)
    b_rms = K.get(EPS_RMS); k_rms = K.key(EPS_RMS)

    P.dma("qs", "sync", lambda e: e.dma_start(out=idb[:], in_=identb),
          writes=["idb"])
    P.dma("qs", "sync", lambda e: e.dma_start(out=idf[:], in_=identf),
          writes=["idf"])

    def tile(r, s):
        sq = sq_[s]; mcol = mcol_[s]; vcol = vcol_[s]; rstd = rstd_[s]
        r2 = r2_[s]; xn = xn_[s]; xnb = xnb_[s]; b = xt[s]; pstr = pstr_[s]
        _k = lambda nm: (nm, s)
        P.dma("qs" if s == 0 else "qa", "sync",
              lambda e: e.dma_start(out=b[:], in_=x0[r * 128:(r + 1) * 128, :]),
              writes=[f"xt{s}"])
        P.op("vector", lambda v: v.reduce_sum(mcol[:], b[:], axis=AX.X),
             reads=[f"xt{s}"], writes=[_k("mcol")])
        P.op("vector", lambda v: v.tensor_scalar(
            mcol[:], mcol[:], 1.0 / D, None, ALU.mult),
            reads=[_k("mcol")], writes=[_k("mcol")])
        P.op("vector", lambda v: v.tensor_scalar(
            xn[:], b[:], mcol[:], None, ALU.subtract),
            reads=[f"xt{s}", _k("mcol")], writes=[_k("xn")])
        P.op("scalar", lambda a: a.activation(sq[:], xn[:], AF.Square),
             reads=[_k("xn")], writes=[_k("sq")])
        P.op("vector", lambda v: v.reduce_sum(vcol[:], sq[:], axis=AX.X),
             reads=[_k("sq")], writes=[_k("vcol")])
        P.op("scalar", lambda a: a.activation(
            rstd[:], vcol[:], AF.Sqrt, bias=b_ln, scale=1.0 / D),
            reads=[_k("vcol"), k_ln], writes=[_k("rstd")])
        P.op("vector", lambda v: v.reciprocal(rstd[:], rstd[:]),
             reads=[_k("rstd")], writes=[_k("rstd")])
        P.op("vector", lambda v: v.tensor_scalar(
            xn[:], xn[:], rstd[:], None, ALU.mult),
            reads=[_k("xn"), _k("rstd")], writes=[_k("xn")])
        P.op("scalar", lambda a: a.activation(sq[:], xn[:], AF.Square),
             reads=[_k("xn")], writes=[_k("sq")])
        P.op("vector", lambda v: v.reduce_sum(r2[:], sq[:], axis=AX.X),
             reads=[_k("sq")], writes=[_k("r2")])
        P.op("scalar", lambda a: a.activation(
            r2[:], r2[:], AF.Sqrt, bias=b_rms, scale=1.0 / D),
            reads=[_k("r2"), k_rms], writes=[_k("r2")])
        P.op("vector", lambda v: v.reciprocal(r2[:], r2[:]),
             reads=[_k("r2")], writes=[_k("r2")])
        P.op("vector", lambda v: v.tensor_copy(tcols[:, r:r + 1], r2[:]),
             reads=[_k("r2")], writes=[("tc", r)])
        P.op("scalar", lambda a: a.copy(xnb[:], xn[:]),
             reads=[_k("xn")], writes=[_k("xnb")])
        for c in range(DC):
            P.op("tensor",
                 lambda t, cc=c: t.transpose(
                     pstr[:], xnb[:, cc * 128:(cc + 1) * 128], idb[:]),
                 reads=[_k("xnb"), "idb"], writes=[_k("pstr")])
            P.op("scalar",
                 lambda a, cc=c: a.copy(
                     fm[cc][:, r * 128:(r + 1) * 128], pstr[:]),
                 reads=[_k("pstr")], writes=[("fm", c, r)])

    for r in range(R // 128):
        tile(r, r % 2)
    P.op("tensor", lambda t: t.transpose(ps_t[:], tcols[:], idf[:]),
         reads=[("tc", r) for r in range(32)] + ["idf"], writes=["ps_t"])
    trow = sb("trow", [32, 128], F32)
    P.op("vector", lambda v: v.tensor_copy(trow[:], ps_t[:]),
         reads=["ps_t"], writes=["trow"])
    P.dma("qg", "gpsimd", lambda e: e.dma_start(out=tsc, in_=trow[:]),
          reads=["trow"])
    for c in range(DC):
        P.dma("qg", "gpsimd",
              lambda e, cc=c: e.dma_start(out=xnfm[cc], in_=fm[cc][:]),
              reads=[("fm", c, r) for r in range(32)])
    P.op("gpsimd", lambda g: g.engine_nop(), waits=[P.dma_token("qg")])
    return

def emit_ph2(nc, P, ctx, d, pfx="ph2_"):
    TS = 512
    NS = R // TS

    """L0 FFN: h=xn@w1+b1 -> q256 -> exact PWL gelu -> q256 -> @w2+b2 -> q256;
    x1 = x1p + f0."""
    xnfm = d["xnfm"]
    x1p = d["x1p"]
    w1b = d["w1b"]
    w2b = d["w2b"]
    b1c = d["b1c"]
    b2 = d["b2"]
    x1 = d["x1"]
    sb = lambda n, s, dt=F32: ctx.enter_context(nc.sbuf_tensor(pfx + n, s, dt))
    psn = lambda n, s, dt=F32: ctx.enter_context(nc.psum_tensor(pfx + n, s, dt))
    w1s = [sb(f"w1s{c}", [128, F0], BF16) for c in range(DC)]
    w2s = [sb(f"w2s{c}", [128, D], BF16) for c in range(FC0)]
    b1t = sb("b1t", [128, FC0])
    ones1 = sb("ones1", [1, 128])
    b2bc = sb("b2bc", [128, D])
    xns = [sb(f"xns{c}", [128, TS], BF16) for c in range(DC)]
    g = [sb(f"g{c}", [128, TS], BF16) for c in range(FC0)]
    XQ4_ = [sb(f"XQ4{u}", [128, TS]) for u in range(2)]
    I4_ = [sb(f"I4{u}", [128, TS]) for u in range(2)]
    YLO_ = [sb(f"YLO{u}", [128, TS]) for u in range(2)]
    YHI_ = [sb(f"YHI{u}", [128, TS]) for u in range(2)]
    YY_ = [sb(f"YY{u}", [128, TS]) for u in range(2)]
    x1t = sb("x1t", [128, D]); outt = sb("outt", [128, D])
    ps_h_ = [psn(f"ps_h{u}", [128, TS]) for u in range(2)]
    ps_f = psn("ps_f", [128, D])
    ps_bc = psn("ps_bc", [128, 512])
    K = ConstBank(P, nc, ctx, pfx=pfx)
    b_q = K.get(0.25); k_q = K.key(0.25)

    for c in range(DC):
        P.dma("qs", "sync",
              lambda e, cc=c: e.dma_start(
                  out=w1s[cc][:], in_=w1b[cc * 128:(cc + 1) * 128, :]),
              writes=[("w1", c)])
    for c in range(FC0):
        P.dma("qs", "sync",
              lambda e, cc=c: e.dma_start(
                  out=w2s[cc][:], in_=w2b[cc * 128:(cc + 1) * 128, :]),
              writes=[("w2", c)])
    P.dma("qs", "sync", lambda e: e.dma_start(out=b1t[:], in_=b1c),
          writes=["b1t"])
    P.dma("qs", "sync", lambda e: e.dma_start(out=b2bc[0:1, :], in_=b2),
          writes=["b2row"])
    P.op("vector", lambda v: v.memset(ones1[:], 1.0), writes=["ones1"])
    for dstrip in range(2):
        P.op("tensor", lambda t, ds=dstrip: t.matmul(
            ps_bc[:], ones1[:], b2bc[0:1, ds * 512:(ds + 1) * 512],
            start=True, stop=True),
            reads=["b2row", "ones1"], writes=["ps_bc"])
        P.op("vector", lambda v, ds=dstrip: v.tensor_copy(
            b2bc[:, ds * 512:(ds + 1) * 512], ps_bc[:]),
            reads=["ps_bc"], writes=[("b2bc", dstrip)])

    w1keys = [("w1", c) for c in range(DC)]
    w2keys = [("w2", c) for c in range(FC0)]
    for s in range(NS):
        for c in range(DC):
            P.dma("qs", "sync",
                  lambda e, cc=c, ss=s: e.dma_start(
                      out=xns[cc][:],
                      in_=xnfm[cc][:, ss * TS:(ss + 1) * TS]),
                  writes=[("xns", c)])
        for fc in range(FC0):
            u = fc % 2
            XQ4 = XQ4_[u]; I4 = I4_[u]
            YLO = YLO_[u]; YHI = YHI_[u]; YY = YY_[u]
            ps_h = ps_h_[u]
            kx = lambda nm, u=u: (nm, u)
            k_loop(P, ps_h[:],
                   [(w1s[c][:, fc * 128:(fc + 1) * 128], xns[c][:])
                    for c in range(DC)],
                   reads=w1keys + [("xns", c) for c in range(DC)] + ["b1t"],
                   writes=[kx("ps_h")])
            P.op("vector", lambda v, f=fc, XQ4=XQ4, ps_h=ps_h: v.tensor_scalar(
                XQ4[:], ps_h[:], b1t[:, f:f + 1], 256.0, ALU.add, ALU.mult),
                reads=[kx("ps_h"), "b1t"], writes=[kx("XQ4")])
            P.op("gpsimd", lambda g_, XQ4=XQ4: g_.tensor_scalar(
                XQ4[:], XQ4[:], MAGIC, None, ALU.add),
                reads=[kx("XQ4")], writes=[kx("XQ4")])
            P.op("vector", lambda v, XQ4=XQ4: v.tensor_scalar(
                XQ4[:], XQ4[:], MAGIC, 2.0 ** -6, ALU.subtract, ALU.mult),
                reads=[kx("XQ4")], writes=[kx("XQ4")])
            P.op("vector", lambda v, XQ4=XQ4, I4=I4: v.tensor_scalar(
                I4[:], XQ4[:], MAGIC - 63.0 / 128.0, None, ALU.add),
                reads=[kx("XQ4")], writes=[kx("I4")])
            P.op("vector", lambda v, I4=I4: v.tensor_scalar(
                I4[:], I4[:], MAGIC, None, ALU.subtract),
                reads=[kx("I4")], writes=[kx("I4")])
            P.op("gpsimd", lambda g_, XQ4=XQ4, I4=I4: g_.tensor_sub(
                XQ4[:], XQ4[:], I4[:]),
                reads=[kx("XQ4"), kx("I4")], writes=[kx("XQ4")])
            P.op("scalar", lambda a_, I4=I4, YLO=YLO: a_.activation(
                YLO[:], I4[:], AF.Gelu, scale=0.25),
                reads=[kx("I4")], writes=[kx("YLO")])
            P.op("scalar", lambda a_, I4=I4, YHI=YHI: a_.activation(
                YHI[:], I4[:], AF.Gelu, bias=b_q, scale=0.25),
                reads=[kx("I4"), k_q], writes=[kx("YHI")])
            P.op("gpsimd", lambda g_, YHI=YHI, YLO=YLO: g_.tensor_sub(
                YHI[:], YHI[:], YLO[:]),
                reads=[kx("YHI"), kx("YLO")], writes=[kx("YHI")])
            P.op("vector", lambda v, YY=YY, XQ4=XQ4, YHI=YHI: v.tensor_mul(
                YY[:], XQ4[:], YHI[:]),
                reads=[kx("XQ4"), kx("YHI")], writes=[kx("YY")])
            P.op("gpsimd", lambda g_, YY=YY, YLO=YLO: g_.tensor_add(
                YY[:], YY[:], YLO[:]),
                reads=[kx("YY"), kx("YLO")], writes=[kx("YY")])
            P.op("vector", lambda v, YY=YY: v.tensor_scalar(
                YY[:], YY[:], 256.0, MAGIC, ALU.mult, ALU.add),
                reads=[kx("YY")], writes=[kx("YY")])
            P.op("scalar", lambda a_, f=fc, YY=YY: a_.activation(
                g[f][:], YY[:], AF.Copy, bias=-MAGIC * 2.0 ** -8,
                scale=2.0 ** -8),
                reads=[kx("YY")], writes=[("g", fc)])
        for tile in range(TS // 128):
            rr = s * TS + tile * 128
            P.dma("q5", "sync",
                  lambda e, r0=rr: e.dma_start(out=x1t[:],
                                               in_=x1p[r0:r0 + 128, :]),
                  writes=["x1t"])
            for dstrip in range(2):
                k_loop(P, ps_f[:, dstrip * 512:(dstrip + 1) * 512],
                       [(g[fc][:, tile * 128:(tile + 1) * 128],
                         w2s[fc][:, dstrip * 512:(dstrip + 1) * 512])
                        for fc in range(FC0)],
                       reads=[("g", fc) for fc in range(FC0)] + w2keys,
                       writes=[("ps_f", dstrip)])
                P.op("vector", lambda v, ds=dstrip: v.tensor_tensor(
                    outt[:, ds * 512:(ds + 1) * 512],
                    ps_f[:, ds * 512:(ds + 1) * 512],
                    b2bc[:, ds * 512:(ds + 1) * 512], ALU.add),
                    reads=[("ps_f", dstrip), ("b2bc", dstrip)],
                    writes=[("outt", dstrip)])
                P.op("vector", lambda v, ds=dstrip: v.tensor_scalar(
                    outt[:, ds * 512:(ds + 1) * 512],
                    outt[:, ds * 512:(ds + 1) * 512],
                    256.0, MAGIC, ALU.mult, ALU.add),
                    reads=[("outt", dstrip)], writes=[("outt", dstrip)])
                P.op("vector", lambda v, ds=dstrip: v.tensor_scalar(
                    outt[:, ds * 512:(ds + 1) * 512],
                    outt[:, ds * 512:(ds + 1) * 512],
                    MAGIC, 2.0 ** -8, ALU.subtract, ALU.mult),
                    reads=[("outt", dstrip)], writes=[("outt", dstrip)])
                P.op("vector", lambda v, ds=dstrip: v.tensor_tensor(
                    outt[:, ds * 512:(ds + 1) * 512],
                    outt[:, ds * 512:(ds + 1) * 512],
                    x1t[:, ds * 512:(ds + 1) * 512], ALU.add),
                    reads=[("outt", dstrip), "x1t"],
                    writes=[("outt", dstrip)])
            P.dma("qg", "gpsimd",
                  lambda e, r0=rr: e.dma_start(out=x1[r0:r0 + 128, :],
                                               in_=outt[:]),
                  reads=[("outt", 0), ("outt", 1)])
    P.op("gpsimd", lambda g_: g_.engine_nop(), waits=[P.dma_token("qg")])
    return


def emit_ph1(nc, P, ctx, d, pfx="ph1_"):

    xnfm = d["xnfm"]
    tsc = d["tsc"]
    x0 = d["x0"]
    wq_d = d["wq"]
    wk_d = d["wk"]
    wv_d = d["wv"]
    wo_d = d["wo"]
    wgates_d = d["wgates"]
    rmsc_d = d["rmsc"]
    maskIJ_d = d["maskIJ"]
    maskT_d = d["maskT"]
    identf_d = d["identf"]
    erow_d = d["erow"]
    ones_d = d["onesrow"]
    x1p = d["x1p"]
    mrow_hbm = d["mrow_scr"]
    crow_hbm = d["crow_scr"]

    sb = lambda n, s, dt=F32: ctx.enter_context(nc.sbuf_tensor(pfx + n, s, dt))
    psn = lambda n, s, dt=F32: ctx.enter_context(nc.psum_tensor(pfx + n, s, dt))
    wqb = [sb(f"wqb{c}", [128, 512], BF16) for c in range(DC)]
    wkb = [sb(f"wkb{c}", [128, 512], BF16) for c in range(DC)]
    wvb = [sb(f"wvb{c}", [128, 512], BF16) for c in range(DC)]
    wob = [sb(f"wob{p}", [128, D], BF16) for p in range(4)]
    wgb = [sb(f"wgb{c}", [128, H], BF16) for c in range(DC)]
    wld2 = sb("wld2", [128, D])
    rmsc = sb("s_rmsc", [128, DC])
    maskIJ = [sb(f"mIJ{v}", [128, 512], BF16) for v in range(4)]
    maskT = [sb(f"mT{v}", [128, 512], BF16) for v in range(4)]
    identf = sb("s_identf", [128, 128])
    erow = sb("s_erow", [1, 128])
    onesr = sb("onesr", [1, 128])
    xn_sb = sb("xn_sb", [128, N], BF16)
    t_fm = [sb(f"t_fm{c}", [128, N], BF16) for c in range(DC)]
    tscrow = sb("tscrow", [1, N])
    tscb = sb("tscb", [128, N])
    qt = [sb(f"qt{u}", [65, N], BF16) for u in range(2)]
    kt = [sb(f"kt{u}", [65, N], BF16) for u in range(2)]
    v_tm = [sb(f"v_tm{j}", [128, H * 65], BF16) for j in range(16)]
    gates = sb("gates", [128, 16 * H])
    sums = sb("sums", [128, 16 * H])
    expdump_ = [sb(f"expdump{u}", [128, N], BF16) for u in range(2)]
    p_T = [sb(f"p_T{j}", [128, 512], BF16) for j in range(16)]
    outsc = [sb(f"outsc{p}", [128, N], BF16) for p in range(4)]
    mtmp_ = [sb(f"mtmp{u}", [128, 4]) for u in range(2)]
    mcol_ = [sb(f"mcol{u}", [128, 1]) for u in range(2)]
    mb16_ = [sb(f"mb16{u}", [128, 1], BF16) for u in range(2)]
    mcols = sb("mcols", [128, 16])
    negm_ = [sb(f"negm{u}", [128, 1]) for u in range(2)]
    ctmp = sb("ctmp", [128, 16])
    rs = sb("rs", [128, 1])
    mrow = [sb("mrow0", [1, N])] * 2
    crow = [sb(f"crow{u}", [1, N]) for u in range(2)]
    mrows16 = sb("mrows16", [16, 128])
    cb2 = sb("cb2", [128, 512])
    x0t = sb("x0t", [128, D])
    outt = sb("outt", [128, D])
    PS1 = psn("PS1", [128, 2048])
    PS2 = psn("PS2", [128, 2048])

    bank = lambda ps, k: ps[:, k * 512:(k + 1) * 512]

    for c in range(DC):
        for (wd_, dst, key, scale) in ((wq_d, wqb, "wq", 0.125),
                                       (wk_d, wkb, "wk", None),
                                       (wv_d, wvb, "wv", None)):
            P.dma("qs", "sync", lambda e, cc=c, w_=wd_: e.dma_start(
                out=wld2[:, 0:512], in_=w_[cc * 128:(cc + 1) * 128, :]),
                writes=["wld2"])
            if scale is None:
                P.op("vector", lambda v, cc=c, d_=dst: v.tensor_copy(
                    d_[cc][:], wld2[:, 0:512]), reads=["wld2"], writes=[(key, c)])
            else:
                P.op("vector", lambda v, cc=c, d_=dst, sc=scale:
                     v.tensor_scalar(d_[cc][:], wld2[:, 0:512], sc, None, ALU.mult),
                     reads=["wld2"], writes=[(key, c)])
        P.dma("qs", "sync", lambda e, cc=c: e.dma_start(
            out=wld2[:, 0:H], in_=wgates_d[cc * 128:(cc + 1) * 128, :]),
            writes=["wld2"])
        P.op("vector", lambda v, cc=c: v.tensor_copy(wgb[cc][:], wld2[:, 0:H]),
             reads=["wld2"], writes=[("wg", c)])
    for p in range(4):
        P.dma("qs", "sync", lambda e, pp=p: e.dma_start(
            out=wld2[:], in_=wo_d[pp]), writes=["wld2"])
        P.op("vector", lambda v, pp=p: v.tensor_copy(wob[pp][:], wld2[:]),
             reads=["wld2"], writes=[("wo", p)])
    P.dma("qs", "sync", lambda e: e.dma_start(out=rmsc[:], in_=rmsc_d),
          writes=["rmsc"])
    for v_ in range(4):
        P.dma("qs", "sync", lambda e, vv=v_: e.dma_start(
            out=maskIJ[vv][:], in_=maskIJ_d[vv]), writes=[("mIJ", v_)])
        P.dma("qs", "sync", lambda e, vv=v_: e.dma_start(
            out=maskT[vv][:], in_=maskT_d[vv]), writes=[("mT", v_)])
    P.dma("qs", "sync", lambda e: e.dma_start(out=identf[:], in_=identf_d),
          writes=["identf"])
    P.dma("qs", "sync", lambda e: e.dma_start(out=erow[:], in_=erow_d),
          writes=["erow"])
    P.dma("qs", "sync", lambda e: e.dma_start(out=onesr[:], in_=ones_d),
          writes=["onesr"])

    wqk = [("wq", c) for c in range(DC)]
    wkk = [("wk", c) for c in range(DC)]
    wvk = [("wv", c) for c in range(DC)]
    wgk = [("wg", c) for c in range(DC)]
    tfk = [("t", c) for c in range(DC)]

    for s in range(SEQ):
        P.dma("qs", "sync", lambda e, ss=s: e.dma_start(
            out=tscrow[:],
            in_=tsc.rearrange("a b -> (a b)")[None, ss * N:(ss + 1) * N]),
            writes=["tscrow"])
        for k in range(4):
            P.op("tensor", lambda t, kk=k: t.matmul(
                bank(PS2, 3), onesr[:], tscrow[0:1, kk * 512:(kk + 1) * 512],
                start=True, stop=True),
                reads=["tscrow", "onesr"],
                waits=P.peek_war(["B23"]), writes=["B23"])
            P.op("vector", lambda v, kk=k: v.tensor_copy(
                tscb[:, kk * 512:(kk + 1) * 512], bank(PS2, 3)),
                reads=["B23"], writes=[("tscb", k)])
        tscbk = [("tscb", k) for k in range(4)]
        for c in range(DC):
            P.dma("qs", "sync", lambda e, cc=c, ss=s: e.dma_start(
                out=xn_sb[:], in_=xnfm[cc][:, ss * N:(ss + 1) * N]),
                writes=["xn_sb"])
            P.op("vector", lambda v, cc=c: v.scalar_tensor_tensor(
                t_fm[cc][:], xn_sb[:], rmsc[:, cc:cc + 1], tscb[:],
                ALU.mult, ALU.mult),
                reads=["xn_sb", "rmsc"] + tscbk, writes=[("t", c)])
        for jt in range(16):
            k_loop(P, bank(PS2, 3),
                   [(t_fm[c][:, jt * 128:(jt + 1) * 128], wvb[c][:])
                    for c in range(DC)],
                   reads=tfk + wvk, writes=["B23"])
            P.op("vector", lambda v, j=jt: v.tensor_copy(
                v_tm[j][:].rearrange("p (h n) -> p h n", n=65)[:, :, 0:64],
                bank(PS2, 3).rearrange("p (h n) -> p h n", n=64)),
                reads=["B23"], writes=[("v", jt)])
            P.op("vector", lambda v, j=jt: v.memset(
                v_tm[j][:].rearrange("p (h n) -> p h n", n=65)[:, :, 64:65],
                1.0), writes=[("v1", jt)])
            k_loop(P, PS2[:, 1536:1544],
                   [(t_fm[c][:, jt * 128:(jt + 1) * 128], wgb[c][:])
                    for c in range(DC)],
                   reads=tfk + wgk, writes=["B23"])
            P.op("scalar", lambda a, j=jt: a.activation(
                gates[:, j * H:(j + 1) * H], PS2[:, 1536:1544], AF.Sigmoid),
                reads=["B23"], writes=[("gates", jt)])
        P.op("vector", lambda v: v.memset(sums[:], 0.0),
             writes=[("sums", h_, i_) for h_ in range(H) for i_ in range(16)])

        for pair in range(4):
            for hh in range(2):
                h = 2 * pair + hh
                Q, Kt = qt[hh], kt[hh]
                qk_r = [("qt", hh, I) for I in range(4)] + [("qtaug", hh)]
                kk_r = [("kt", hh, I) for I in range(4)] + [("kt1", hh)]
                for I in range(4):
                    k_loop(P, bank(PS2, 2)[0:64, :],
                           [(wqb[c][:, h * 64:(h + 1) * 64],
                             t_fm[c][:, I * 512:(I + 1) * 512])
                            for c in range(DC)],
                           reads=wqk + tfk, writes=["B22"])
                    P.op("vector", lambda v, II=I, Q=Q: v.tensor_copy(
                        Q[0:64, II * 512:(II + 1) * 512],
                        bank(PS2, 2)[0:64, :]),
                        reads=["B22"], writes=[("qt", hh, I)])
                    k_loop(P, bank(PS2, 2)[0:64, :],
                           [(wkb[c][:, h * 64:(h + 1) * 64],
                             t_fm[c][:, I * 512:(I + 1) * 512])
                            for c in range(DC)],
                           reads=wkk + tfk, writes=["B22"])
                    P.op("vector", lambda v, II=I, Kt=Kt: v.tensor_copy(
                        Kt[0:64, II * 512:(II + 1) * 512],
                        bank(PS2, 2)[0:64, :]),
                        reads=["B22"], writes=[("kt", hh, I)])
                P.op("vector", lambda v, Kt=Kt: v.memset(Kt[64:65, :], 1.0),
                     writes=[("kt1", hh)])
                for it in range(16):
                    W4 = it // 4 + 1
                    up = it % 2
                    PSS = PS1 if up == 0 else PS2
                    skey = (lambda j, up=up: ("B1", j) if up == 0
                            else ["B20", "B21", "B22", "B23"][j])
                    mtmp = mtmp_[up]; mcol = mcol_[up]
                    mb16 = mb16_[up]; negm = negm_[up]
                    expdump = expdump_[up]
                    for jb4 in range(W4):
                        P.op("tensor", lambda t, i_=it, j_=jb4, Q=Q, Kt=Kt,
                                    PSS=PSS:
                             t.matmul(bank(PSS, j_),
                                      Q[0:64, i_ * 128:(i_ + 1) * 128],
                                      Kt[0:64, j_ * 512:(j_ + 1) * 512],
                                      start=True, stop=True),
                             reads=qk_r[:4] + kk_r[:4],
                             waits=P.peek_war([skey(jb4)]),
                             writes=[skey(jb4)])
                    P.op("vector", lambda v, i_=it, j_=W4 - 1, PSS=PSS:
                         v.tensor_tensor(
                             bank(PSS, j_), bank(PSS, j_), maskIJ[i_ % 4][:],
                             ALU.add),
                         reads=[skey(W4 - 1), ("mIJ", it % 4)],
                         writes=[skey(W4 - 1)])
                    for jb4 in range(W4):
                        P.op("vector", lambda v, j_=jb4, PSS=PSS, mtmp=mtmp:
                             v.reduce_max(
                                 mtmp[:, j_:j_ + 1], bank(PSS, j_), axis=AX.X),
                             reads=[skey(jb4)], writes=[("mtmp", up, jb4)])
                    P.op("vector", lambda v, W=W4, mtmp=mtmp, mcol=mcol:
                         v.reduce_max(mcol[:], mtmp[:, 0:W], axis=AX.X),
                         reads=[("mtmp", up, j) for j in range(W4)],
                         writes=[("mcol", up)])
                    P.op("vector", lambda v, mb16=mb16, mcol=mcol:
                         v.tensor_copy(mb16[:], mcol[:]),
                         reads=[("mcol", up)], writes=[("mb16", up)])
                    P.op("vector", lambda v, i_=it, mb16=mb16: v.tensor_copy(
                        mcols[:, i_:i_ + 1], mb16[:]),
                        reads=[("mb16", up)], writes=[("mcols", it)])
                    P.op("vector", lambda v, i_=it, negm=negm: v.tensor_scalar(
                        negm[:], mcols[:, i_:i_ + 1], -1.0, None, ALU.mult),
                        reads=[("mcols", it)], writes=[("negm", up)])
                    P.op("scalar", lambda a, i_=it, W=W4, h_=h, PSS=PSS,
                                negm=negm, expdump=expdump: a.activation(
                        expdump[:, 0:W * 512], PSS[:, 0:W * 512], AF.Exp,
                        bias=negm[:], scale=1.0,
                        accum_out=sums[:, h_ * 16 + i_:h_ * 16 + i_ + 1]),
                        reads=[skey(j) for j in range(W4)] + [("negm", up)],
                        writes=[("expdump", up), ("sums", h, it)])
                P.op("tensor", lambda t: t.transpose(
                    PS2[0:16, 1536:1664], mcols[:], identf[:]),
                    reads=[("mcols", i_) for i_ in range(16)] + ["identf"],
                    writes=["B23"])
                P.op("vector", lambda v: v.tensor_copy(mrows16[:], PS2[0:16, 1536:1664]),
                     reads=["B23"], writes=["mrows16"])
                P.dma("qg", "gpsimd", lambda e: e.dma_start(
                    out=mrow_hbm, in_=mrows16[:]), reads=["mrows16"],
                    writes=["mrow_hbm"])
                P.dma("qg", "gpsimd", lambda e, u=hh: e.dma_start(
                    out=mrow[u][:],
                    in_=mrow_hbm.rearrange("a b -> (a b)")[None, :]),
                    reads=["mrow_hbm"], writes=[("mrowsb", hh)])
                for I in range(4):
                    P.op("tensor", lambda t, II=I, u=hh: t.matmul(
                        PS2[0:65, 1536:2048], erow[0:1, 0:65],
                        mrow[u][0:1, II * 512:(II + 1) * 512],
                        start=True, stop=True),
                        reads=[("mrowsb", hh), "erow"],
                        waits=P.peek_war(["B23"]), writes=["B23"])
                    P.op("vector", lambda v, II=I, Q=Q: v.tensor_scalar(
                        Q[64:65, II * 512:(II + 1) * 512],
                        PS2[64:65, 1536:2048], -1.0, None, ALU.mult),
                        reads=["B23"], writes=[("qtaug", hh)])
                for it in range(16):
                    P.op("vector", lambda v, i_=it, h_=h: v.reciprocal(
                        rs[:], sums[:, h_ * 16 + i_:h_ * 16 + i_ + 1]),
                        reads=[("sums", h, it)], writes=["rs"])
                    P.op("vector", lambda v, i_=it, h_=h: v.tensor_scalar(
                        ctmp[:, i_:i_ + 1],
                        gates[:, i_ * H + h_:i_ * H + h_ + 1],
                        rs[:], None, ALU.mult),
                        reads=["rs", ("gates", it)], writes=[("ctmp", it)])
                P.op("tensor", lambda t: t.transpose(
                    PS2[0:16, 1536:1664], ctmp[:], identf[:]),
                    reads=[("ctmp", i_) for i_ in range(16)] + ["identf"],
                    writes=["B23"])
                P.op("vector", lambda v: v.tensor_copy(mrows16[:], PS2[0:16, 1536:1664]),
                     reads=["B23"], writes=["mrows16"])
                P.dma("qg", "gpsimd", lambda e, u=hh: e.dma_start(
                    out=crow_hbm[u], in_=mrows16[:]), reads=["mrows16"],
                    writes=["crow_hbm"])
                P.dma("qg", "gpsimd", lambda e, u=hh: e.dma_start(
                    out=crow[u][:],
                    in_=crow_hbm[u].rearrange("a b -> (a b)")[None, :]),
                    reads=["crow_hbm"], writes=[("crow", hh)])
            for I in range(4):
                nj = 4 * I + 4
                for hh in range(2):
                    h = 2 * pair + hh
                    Q, Kt = qt[hh], kt[hh]
                    qk_r = [("qt", hh, II) for II in range(4)] + [("qtaug", hh)]
                    kk_r = [("kt", hh, II) for II in range(4)] + [("kt1", hh)]
                    for jb in range(nj):
                        u2_ = jb % 2
                        pbank = bank(PS2, 0) if u2_ == 0 else bank(PS1, 0)
                        pkey = "B20" if u2_ == 0 else ("B1", 0)
                        P.op("tensor", lambda t, j_=jb, II=I, Q=Q, Kt=Kt,
                                    pb=pbank:
                             t.matmul(pb,
                                      Kt[:, j_ * 128:(j_ + 1) * 128],
                                      Q[:, II * 512:(II + 1) * 512],
                                      start=True, stop=True),
                             reads=kk_r + qk_r,
                             waits=P.peek_war([pkey]),
                             writes=[pkey])
                        dpos = jb - 4 * I
                        if dpos >= 0:
                            P.op("vector", lambda v, d_=dpos, pb=pbank:
                                 v.tensor_tensor(pb, pb, maskT[d_][:],
                                                 ALU.add),
                                 reads=[pkey, ("mT", dpos)],
                                 writes=[pkey])
                        P.op("scalar", lambda a, j_=jb, pb=pbank: a.activation(
                            p_T[j_][:], pb, AF.Exp),
                            reads=[pkey], writes=[("pT", jb)])
                    tp = (0, 0) if hh == 0 else (0, 64)
                    lo = 0 if hh == 0 else 64
                    pvw = P.peek_raw([("pT", j) for j in range(nj)]
                                     + [("v", j) for j in range(nj)]
                                     + [("v1", j) for j in range(nj)]) \
                        + P.peek_war(["B21"])
                    for jb in range(nj):
                        P.op("tensor",
                             lambda t, j_=jb, h_=h, tp=tp, lo=lo,
                                    st=(jb == 0), sp=(jb == nj - 1):
                             t.matmul(bank(PS2, 1)[lo:lo + 64, :],
                                      v_tm[j_][:, h_ * 65:h_ * 65 + 64],
                                      p_T[j_][:], start=st, stop=sp,
                                      tile_position=tp),
                             reads=([("pT", j) for j in range(nj)]
                                    if jb == nj - 1 else ()),
                             writes=(["B21"] if jb == nj - 1 else ()),
                             waits=(pvw if jb == 0 else ()))
                    P.op("tensor", lambda t, II=I, tp=tp, lo=lo, u=hh:
                         t.matmul(PS2[lo:lo + 64, 1024:1536],
                                  onesr[0:1, 0:64],
                                  crow[u][0:1, II * 512:(II + 1) * 512],
                                  start=True, stop=True, tile_position=tp),
                         reads=[("crow", hh), "onesr"],
                         waits=P.peek_war(["B22"]),
                         writes=["B22"])
                P.op("vector", lambda v: v.tensor_copy(cb2[:], bank(PS2, 2)),
                     reads=["B22"], writes=["cb2"])
                P.op("vector", lambda v, II=I, p_=pair: v.tensor_tensor(
                    outsc[p_][:, II * 512:(II + 1) * 512], bank(PS2, 1),
                    cb2[:], ALU.mult),
                    reads=["cb2", ("PSpv", 0), ("PSpv", 1)],
                    writes=[("outsc", pair, I)])
        osk = [("outsc", p_, I) for p_ in range(4) for I in range(4)]
        for tile in range(16):
            rr = s * N + tile * 128
            P.dma("q5", "sync", lambda e, r0=rr: e.dma_start(
                out=x0t[:], in_=x0[r0:r0 + 128, :]), writes=["x0t"])
            for dstrip in range(2):
                k_loop(P, bank(PS1, dstrip),
                       [(outsc[p_][:, tile * 128:(tile + 1) * 128],
                         wob[p_][:, dstrip * 512:(dstrip + 1) * 512])
                        for p_ in range(4)],
                       reads=osk + [("wo", p_) for p_ in range(4)],
                       writes=[("B1", dstrip)])
                P.op("vector", lambda v, ds=dstrip: v.tensor_tensor(
                    outt[:, ds * 512:(ds + 1) * 512], bank(PS1, ds),
                    x0t[:, ds * 512:(ds + 1) * 512], ALU.add),
                    reads=[("B1", dstrip), "x0t"], writes=[("outt", dstrip)])
            P.dma("qg", "gpsimd", lambda e, r0=rr: e.dma_start(
                out=x1p[r0:r0 + 128, :], in_=outt[:]),
                reads=[("outt", 0), ("outt", 1)])
    P.op("gpsimd", lambda g_: g_.engine_nop(), waits=[P.dma_token("qg")])
    return


def emit_ph3(nc, P, ctx, d, pfx="ph3_"):
    x1 = d["x1"]; lngc_d = d["lngc"]; lnbc_d = d["lnbc"]
    identb_d = d["identb"]; identf_d = d["identf"]
    xn1 = d["xn1"]; normfm = d["normfm"]; x2p = d["x2p"]; lm1 = d["lm1"]
    sb = lambda n, s, dt=F32: ctx.enter_context(nc.sbuf_tensor(pfx + n, s, dt))
    psn = lambda n, s, dt=F32: ctx.enter_context(nc.psum_tensor(pfx + n, s, dt))
    xt = [sb(f"xt{i}", [128, D]) for i in range(2)]
    sq_ = [sb(f"sq{u}", [128, D]) for u in range(2)]
    mcol_ = [sb(f"mcol{u}", [128, 1]) for u in range(2)]
    vcol_ = [sb(f"vcol{u}", [128, 1]) for u in range(2)]
    rstd_ = [sb(f"rstd{u}", [128, 1]) for u in range(2)]
    xnt_ = [sb(f"xnt{u}", [128, D]) for u in range(2)]
    u2_ = [sb(f"u2{u}", [128, D]) for u in range(2)]
    ub_ = [sb(f"ub{u}", [128, D], BF16) for u in range(2)]
    x2t_ = [sb(f"x2t{u}", [128, D]) for u in range(2)]
    tmax_ = [sb(f"tmax{u}", [128, 1]) for u in range(2)]
    idb = sb("idb", [128, 128], BF16)
    idf = sb("idf", [128, 128], F32)
    lngc = sb("s_lngc", [128, DC]); lnbc = sb("s_lnbc", [128, DC])
    fm = [sb(f"fm{c}", [128, R], BF16) for c in range(DC)]
    runmax = sb("runmax", [128, 1])
    mrow_s = sb("mrow_s", [1, 128]); mone = sb("mone", [1, 1])
    pstr_ = [psn(f"pstr{u}", [128, 128], BF16) for u in range(2)]
    ps_r = psn("ps_r", [1, 128], F32)
    K = ConstBank(P, nc, ctx, pfx=pfx)
    b_ln = K.get(EPS_LN); k_ln = K.key(EPS_LN)

    P.dma("qs", "sync", lambda e: e.dma_start(out=idb[:], in_=identb_d),
          writes=["idb"])
    P.dma("qs", "sync", lambda e: e.dma_start(out=idf[:], in_=identf_d),
          writes=["idf"])
    P.dma("qs", "sync", lambda e: e.dma_start(out=lngc[:], in_=lngc_d),
          writes=["lngc"])
    P.dma("qs", "sync", lambda e: e.dma_start(out=lnbc[:], in_=lnbc_d),
          writes=["lnbc"])
    P.op("vector", lambda v: v.memset(runmax[:], 0.0), writes=["runmax"])

    def ln_pass(s, src_key, srcb, dst, dst_key):
        sq = sq_[s]; mcol = mcol_[s]; vcol = vcol_[s]; rstd = rstd_[s]
        _k = lambda nm: (nm, s)
        P.op("vector", lambda v: v.reduce_sum(mcol[:], srcb[:], axis=AX.X),
             reads=[src_key], writes=[_k("mcol")])
        P.op("vector", lambda v: v.tensor_scalar(
            mcol[:], mcol[:], 1.0 / D, None, ALU.mult),
            reads=[_k("mcol")], writes=[_k("mcol")])
        P.op("vector", lambda v: v.tensor_scalar(
            dst[:], srcb[:], mcol[:], None, ALU.subtract),
            reads=[src_key, _k("mcol")], writes=[dst_key])
        P.op("scalar", lambda a: a.activation(sq[:], dst[:], AF.Square),
             reads=[dst_key], writes=[_k("sq")])
        P.op("vector", lambda v: v.reduce_sum(vcol[:], sq[:], axis=AX.X),
             reads=[_k("sq")], writes=[_k("vcol")])
        P.op("scalar", lambda a: a.activation(
            rstd[:], vcol[:], AF.Sqrt, bias=b_ln, scale=1.0 / D),
            reads=[_k("vcol"), k_ln], writes=[_k("rstd")])
        P.op("vector", lambda v: v.reciprocal(rstd[:], rstd[:]),
             reads=[_k("rstd")], writes=[_k("rstd")])
        P.op("vector", lambda v: v.tensor_scalar(
            dst[:], dst[:], rstd[:], None, ALU.mult),
            reads=[dst_key, _k("rstd")], writes=[dst_key])

    def tile(r, s):
        b = xt[s]; xnt = xnt_[s]; u2 = u2_[s]; ub = ub_[s]
        x2t = x2t_[s]; tmax = tmax_[s]; pstr = pstr_[s]
        _k = lambda nm: (nm, s)
        P.dma("qs" if s == 0 else "qa", "sync",
              lambda e: e.dma_start(out=b[:], in_=x1[r * 128:(r + 1) * 128, :]),
              writes=[f"xt{s}"])
        ln_pass(s, f"xt{s}", b, xnt, _k("xnt"))
        P.dma("qg", "gpsimd", lambda e: e.dma_start(
            out=xn1[r * 128:(r + 1) * 128, :], in_=xnt[:]), reads=[_k("xnt")])
        P.op("gpsimd", lambda g_: g_.tensor_add(
            x2t[:], b[:], xnt[:]),
            reads=[f"xt{s}", _k("xnt")], writes=[_k("x2t")])
        P.dma("qg", "gpsimd", lambda e: e.dma_start(
            out=x2p[r * 128:(r + 1) * 128, :], in_=x2t[:]), reads=[_k("x2t")])
        P.op("vector", lambda v: v.reduce_max(
            tmax[:], xnt[:], axis=AX.X, apply_absolute_value=True),
            reads=[_k("xnt")], writes=[_k("tmax")])
        P.op("vector", lambda v: v.tensor_tensor(
            runmax[:], runmax[:], tmax[:], ALU.max),
            reads=["runmax", _k("tmax")], writes=["runmax"])
        ln_pass(s, _k("xnt"), xnt, u2, _k("u2"))
        P.op("scalar", lambda a: a.copy(ub[:], u2[:]),
             reads=[_k("u2")], writes=[_k("ub")])
        for c in range(DC):
            P.op("tensor", lambda t, cc=c: t.transpose(
                pstr[:], ub[:, cc * 128:(cc + 1) * 128], idb[:]),
                reads=[_k("ub"), "idb"], writes=[_k("pstr")])
            P.op("vector", lambda v, cc=c: v.tensor_scalar(
                fm[cc][:, r * 128:(r + 1) * 128], pstr[:],
                lngc[:, cc:cc + 1], lnbc[:, cc:cc + 1], ALU.mult, ALU.add),
                reads=[_k("pstr"), "lngc", "lnbc"], writes=[("fm", c, r)])

    for r in range(R // 128):
        tile(r, r % 2)
    P.op("tensor", lambda t: t.matmul(
        ps_r[:], runmax[:, 0:1], idf[:], start=True, stop=True),
        reads=["runmax", "idf"], writes=["ps_r"])
    P.op("vector", lambda v: v.tensor_copy(mrow_s[:], ps_r[:]),
         reads=["ps_r"], writes=["mrow_s"])
    P.op("vector", lambda v: v.reduce_max(mone[:], mrow_s[:], axis=AX.X),
         reads=["mrow_s"], writes=["mone"])
    P.dma("qg", "gpsimd", lambda e: e.dma_start(out=lm1, in_=mone[:]),
          reads=["mone"], writes=[("lm1_hbm",)])
    for c in range(DC):
        P.dma("qg", "gpsimd",
              lambda e, cc=c: e.dma_start(out=normfm[cc], in_=fm[cc][:]),
              reads=[("fm", c, r) for r in range(R // 128)])
    P.op("gpsimd", lambda g_: g_.engine_nop(), waits=[P.dma_token("qg")])
    return

def emit_ph4(nc, P, ctx, d, pfx="ph4_"):
    normfm = d["normfm"]; Whb = d["Whb"]; Wqk_d = d["Wqk"]
    bhv_row = d["bhv"]; bhg_c = d["bhgc"]; bqk_c = d["bqkc"]
    gb_d = d["gb"]; Wob_d = d["Wob"]; bo_row = d["bor"]
    x2p = d["x2p"]; maskT_d = d["maskT"]; ones_d = d["onesrow"]
    x2q = d["x2q"]; g_scr = d["g_scr"]; gd_scr = d["gd_scr"]

    sb = lambda n, s, dt=F32: ctx.enter_context(nc.sbuf_tensor(pfx + n, s, dt))
    psn = lambda n, s, dt=F32: ctx.enter_context(nc.psum_tensor(pfx + n, s, dt))
    nrm = [sb(f"nrm{c}", [128, N], BF16) for c in range(DC)]
    whs = [sb(f"whs{c}", [128, 512], BF16) for c in range(DC)]
    wgate_ = [sb(f"wgate{u}", [128, DC * 128], BF16) for u in range(2)]
    wqkb = [sb(f"wqkb{c}", [128, QK], BF16) for c in range(DC)]
    wob = [sb(f"wob{c}", [128, D], BF16) for c in range(HC)]
    qv = sb("qv", [128, N], BF16)
    kv = sb("kv", [128, N], BF16)
    gbc = sb("s_gb", [128, 4])
    bqkc = sb("s_bqk", [128, 1])
    bhgc = sb("s_bhg", [128, HC])
    bhvb = sb("bhvb", [128, HID])
    bob = sb("bob", [128, D])
    onesr = sb("onesr", [1, 128])
    maskT = [sb(f"mT{v}", [128, 512], BF16) for v in range(4)]
    v_tm = [sb(f"v_tm{j}", [128, HID], BF16) for j in range(16)]
    p_T = [sb(f"p_T{j}", [128, 512], BF16) for j in range(16)]
    rsb_ = [sb(f"rsb{u}", [128, 512]) for u in range(2)]
    gsl_ = [sb(f"gsl{u}", [128, 512], BF16) for u in range(2)]
    gout_ = [sb(f"gout{u}", [128, 512], BF16) for u in range(2)]
    gtile_ = [sb(f"gtile{u}", [128, HC * 128], BF16) for u in range(2)]
    x2t = sb("x2t", [128, D])
    outt = sb("outt", [128, D])
    PS1 = psn("PS1", [128, 2048])
    PS2 = psn("PS2", [128, 2048])
    bank = lambda ps, k: ps[:, k * 512:(k + 1) * 512]

    P.dma("qs", "sync", lambda e: e.dma_start(out=gbc[:], in_=gb_d),
          writes=["gbc"])
    P.dma("qs", "sync", lambda e: e.dma_start(out=bqkc[:], in_=bqk_c),
          writes=["bqkc"])
    P.dma("qs", "sync", lambda e: e.dma_start(out=bhgc[:], in_=bhg_c),
          writes=["bhgc"])
    P.dma("qs", "sync", lambda e: e.dma_start(out=onesr[:], in_=ones_d),
          writes=["onesr"])
    for v_ in range(4):
        P.dma("qs", "sync", lambda e, vv=v_: e.dma_start(
            out=maskT[vv][:], in_=maskT_d[vv]), writes=[("mT", v_)])
    for c in range(DC):
        P.dma("qs", "sync", lambda e, cc=c: e.dma_start(
            out=x2t[:, 0:QK], in_=Wqk_d[cc * 128:(cc + 1) * 128, :]),
            writes=["x2t"])
        P.op("vector", lambda v, cc=c: v.tensor_copy(
            wqkb[cc][:], x2t[:, 0:QK]), reads=["x2t"], writes=[("wqk", c)])
    for c in range(HC):
        P.dma("qs", "sync", lambda e, cc=c: e.dma_start(
            out=wob[cc][:], in_=Wob_d[cc * 128:(cc + 1) * 128, :]),
            writes=[("wob", c)])
    P.dma("qs", "sync", lambda e: e.dma_start(out=bhvb[0:1, :], in_=bhv_row),
          writes=["bhv_r"])
    P.dma("qs", "sync", lambda e: e.dma_start(out=bob[0:1, :], in_=bo_row),
          writes=["bo_r"])
    for k in range(4):
        P.op("tensor", lambda t, kk=k: t.matmul(
            bank(PS2, 2), onesr[:], bhvb[0:1, kk * 512:(kk + 1) * 512],
            start=True, stop=True),
            reads=["bhv_r", "onesr"], waits=P.peek_war(["B22"]),
            writes=["B22"])
        P.op("vector", lambda v, kk=k: v.tensor_copy(
            bhvb[:, kk * 512:(kk + 1) * 512], bank(PS2, 2)),
            reads=["B22"], writes=[("bhvb", k)])
    for k in range(2):
        P.op("tensor", lambda t, kk=k: t.matmul(
            bank(PS2, 2), onesr[:], bob[0:1, kk * 512:(kk + 1) * 512],
            start=True, stop=True),
            reads=["bo_r", "onesr"], waits=P.peek_war(["B22"]),
            writes=["B22"])
        P.op("vector", lambda v, kk=k: v.tensor_copy(
            bob[:, kk * 512:(kk + 1) * 512], bank(PS2, 2)),
            reads=["B22"], writes=[("bob", k)])

    nk = [("nrm", c) for c in range(DC)]
    for s in range(SEQ):
        for c in range(DC):
            P.dma("qs", "sync", lambda e, cc=c, ss=s: e.dma_start(
                out=nrm[cc][:], in_=normfm[cc][:, ss * N:(ss + 1) * N]),
                writes=[("nrm", c)])
        for I in range(4):
            k_loop(P, bank(PS1, 2),
                   [(wqkb[c][:], nrm[c][:, I * 512:(I + 1) * 512])
                    for c in range(DC)],
                   reads=[("wqk", c) for c in range(DC)] + nk,
                   writes=[("B12", 0)])
            P.op("scalar", lambda a, II=I: a.activation(
                qv[:, II * 512:(II + 1) * 512], bank(PS1, 2), AF.Silu,
                bias=bqkc[:]),
                reads=[("B12", 0), "bqkc"], writes=[("qks", I)])
        qkk = [("qks", I) for I in range(4)]
        P.op("vector", lambda v: v.tensor_scalar(
            kv[:], qv[:], gbc[:, 1:2], gbc[:, 3:4], ALU.mult, ALU.add),
            reads=qkk + ["gbc"], writes=["kv"])
        P.op("vector", lambda v: v.tensor_scalar(
            qv[:], qv[:], gbc[:, 0:1], gbc[:, 2:3], ALU.mult, ALU.add),
            reads=qkk + ["gbc", "kv"], writes=["qv"] + qkk)
        for vs in range(4):
            P.dma("qa", "sync", lambda e, v0=vs: e.dma_start(
                out=whs[0][:].tensor.full_ap() if False else whs[0][:],
                in_=Whb[0:128, v0 * 512:(v0 + 1) * 512]),
                writes=[("whs", 0)])
            for c in range(1, DC):
                P.dma("qa", "sync", lambda e, cc=c, v0=vs: e.dma_start(
                    out=whs[cc][:],
                    in_=Whb[cc * 128:(cc + 1) * 128, v0 * 512:(v0 + 1) * 512]),
                    writes=[("whs", c)])
            for jt in range(16):
                u = jt % 2
                u4 = jt % 4
                pb = bank(PS1, 2 + u) if u4 < 2 else bank(PS2, u)
                pk = ("B12", u) if u4 < 2 else ("B20", u)
                k_loop(P, pb,
                       [(nrm[c][:, jt * 128:(jt + 1) * 128], whs[c][:])
                        for c in range(DC)],
                       reads=nk + [("whs", c) for c in range(DC)],
                       writes=[pk])
                P.op("vector", lambda v, v0=vs, u=u, pb=pb, pk=pk:
                     v.tensor_tensor(
                         rsb_[u][:], pb, bhvb[:, v0 * 512:(v0 + 1) * 512],
                         ALU.add),
                     reads=[pk, ("bhvb", vs)], writes=[("rsb", u)])
                P.op("scalar", lambda a, j=jt, v0=vs, u=u: a.activation(
                    v_tm[j][:, v0 * 512:(v0 + 1) * 512], rsb_[u][:], AF.Silu),
                    reads=[("rsb", u)], writes=[("v", jt, vs)])
        for hc in range(HC):
            ug = hc % 2
            wg_t = wgate_[ug]
            P.dma("qa", "sync", lambda e, h0=hc, w_=wg_t: e.dma_start(
                out=w_[:].rearrange("p (c n) -> p c n", c=DC),
                in_=Whb[:, HID + h0 * 128:HID + (h0 + 1) * 128].rearrange(
                    "(c p) n -> p c n", p=128)),
                writes=[("wgate", ug)])
            for I in range(4):
                u = I % 2
                pb = bank(PS1, 2 + u) if I < 2 else bank(PS2, u)
                pk = ("B12", u) if I < 2 else ("B20", u)
                k_loop(P, pb,
                       [(wg_t[:, c * 128:(c + 1) * 128],
                         nrm[c][:, I * 512:(I + 1) * 512])
                        for c in range(DC)],
                       reads=nk + [("wgate", ug)],
                       writes=[pk])
                P.op("scalar", lambda a, h0=hc, u=u, pb=pb, pk=pk: a.activation(
                    gsl_[u][:], pb, AF.Silu, bias=bhgc[:, h0:h0 + 1]),
                    reads=[pk, "bhgc"], writes=[("gsl", u)])
                P.dma("qg", "gpsimd", lambda e, h0=hc, II=I, u=u: e.dma_start(
                    out=g_scr[h0][:, II * 512:(II + 1) * 512], in_=gsl_[u][:]),
                    reads=[("gsl", u)], writes=[("gscr", hc, I)])
        for I in range(4):
            nj = 4 * I + 4
            for jb in range(nj):
                u = jb % 2
                pb = bank(PS2, u)
                P.op("tensor", lambda t, j_=jb, II=I, pb=pb: t.matmul(
                    pb, kv[:, j_ * 128:(j_ + 1) * 128],
                    qv[:, II * 512:(II + 1) * 512], start=True, stop=True),
                    reads=["kv", "qv"], waits=P.peek_war([("B20", u)]),
                    writes=[("B20", u)])
                dpos = jb - 4 * I
                if dpos >= 0:
                    P.op("vector", lambda v, d_=dpos, pb=pb: v.tensor_tensor(
                        pb, pb, maskT[d_][:], ALU.add),
                        reads=[("B20", u), ("mT", dpos)], writes=[("B20", u)])
                P.op("scalar", lambda a, u=u, pb=pb: a.activation(
                    rsb_[u][:], pb, AF.Relu),
                    reads=[("B20", u)], writes=[("rsb", u)])
                P.op("scalar", lambda a, j_=jb, u=u: a.activation(
                    p_T[j_][:], rsb_[u][:], AF.Square),
                    reads=[("rsb", u)], writes=[("pT", jb)])
            vtk = [("v", jb, vs) for jb in range(nj) for vs in range(4)]
            ptk = [("pT", jb) for jb in range(nj)]
            for hc in range(HC):
                u = hc % 2
                pb = bank(PS1, u)
                k_loop(P, pb,
                       [(v_tm[jb][:, hc * 128:(hc + 1) * 128], p_T[jb][:])
                        for jb in range(nj)],
                       reads=vtk + ptk, writes=[("B1", u)])
                P.dma("qa", "sync", lambda e, h0=hc, II=I, u=u: e.dma_start(
                    out=gsl_[u][:], in_=g_scr[h0][:, II * 512:(II + 1) * 512]),
                    reads=[("gscr", hc, I)], writes=[("gsl", u)])
                P.op("vector", lambda v, u=u, pb=pb: v.scalar_tensor_tensor(
                    gout_[u][:], pb, 1.0 / (2048.0 * 2048.0),
                    gsl_[u][:], ALU.mult, ALU.mult),
                    reads=[("B1", u), ("gsl", u)], writes=[("gout", u)])
                P.dma("qg", "gpsimd", lambda e, h0=hc, II=I, u=u: e.dma_start(
                    out=gd_scr[h0][:, II * 512:(II + 1) * 512],
                    in_=gout_[u][:]), reads=[("gout", u)],
                    writes=[("gd", hc, I)])
        for tile in range(16):
            rr = s * N + tile * 128
            ut = tile % 2
            gt_t = gtile_[ut]
            P.dma("q5", "sync", lambda e, t0=tile, g_=gt_t: e.dma_start(
                out=g_[:].rearrange("p (h n) -> p h n", h=HC),
                in_=gd_scr.rearrange("h p n -> p h n")[
                    :, :, t0 * 128:(t0 + 1) * 128]),
                reads=[("gd", hc, tile // 4) for hc in range(HC)],
                writes=[("gt", ut)])
            P.dma("q6", "gpsimd", lambda e, r0=rr: e.dma_start(
                out=x2t[:], in_=x2p[r0:r0 + 128, :]), writes=["x2t"])
            for ds in range(2):
                k_loop(P, bank(PS2, 2 + ds),
                       [(gt_t[:, hc * 128:(hc + 1) * 128],
                         wob[hc][:, ds * 512:(ds + 1) * 512])
                        for hc in range(HC)],
                       reads=[("gt", ut)]
                       + [("wob", hc) for hc in range(HC)],
                       writes=["B22" if ds == 0 else "B23"])
                P.op("vector", lambda v, d_=ds: v.tensor_tensor(
                    outt[:, d_ * 512:(d_ + 1) * 512], bank(PS2, 2 + d_),
                    bob[:, d_ * 512:(d_ + 1) * 512], ALU.add),
                    reads=[("B22" if ds == 0 else "B23"), ("bob", ds)],
                    writes=[("outt", ds)])
                P.op("vector", lambda v, d_=ds: v.tensor_tensor(
                    outt[:, d_ * 512:(d_ + 1) * 512],
                    outt[:, d_ * 512:(d_ + 1) * 512],
                    x2t[:, d_ * 512:(d_ + 1) * 512], ALU.add),
                    reads=[("outt", ds), "x2t"], writes=[("outt", ds)])
            P.dma("qg", "gpsimd", lambda e, r0=rr: e.dma_start(
                out=x2q[r0:r0 + 128, :], in_=outt[:]),
                reads=[("outt", 0), ("outt", 1)])
    P.op("gpsimd", lambda g_: g_.engine_nop(), waits=[P.dma_token("qg")])
    return

def P_vscr_keys(s, j, hg):
    return [("vscr", j, vs) for vs in range(4)]


def _emit_scalar_prep(P, nc, ctx, maxes, clip, onesr_key, onesr, PSb, psb_key,
                      pfx=""):
    sb = lambda n, s_, dt=F32: ctx.enter_context(
        nc.sbuf_tensor(pfx + n, s_, dt))
    mx8 = sb("mx8", [1, 8])
    clp = sb("clp", [1, 1])
    am = sb("am", [1, 1])
    sg = sb("sg", [1, 1])
    sv = sb("sv", [1, 1])
    rv = sb("rv", [1, 1])
    scol = sb("scol_b", [128, 1])
    rcol = sb("rcol_b", [128, 1])
    P.dma("qs", "sync", lambda e: e.dma_start(out=mx8[:], in_=maxes),
          writes=["mx8"])
    P.dma("qs", "sync", lambda e: e.dma_start(out=clp[:], in_=clip),
          writes=["clp"])
    P.op("vector", lambda v: v.reduce_max(am[:], mx8[:], axis=AX.X),
         reads=["mx8"], writes=["am"])
    P.op("scalar", lambda a: a.activation(sg[:], clp[:], AF.Sigmoid),
         reads=["clp"], writes=["sg"])
    P.op("vector", lambda v: v.tensor_tensor(sv[:], sg[:], am[:], ALU.mult),
         reads=["sg", "am"], writes=["sv"])
    P.op("vector", lambda v: v.tensor_scalar(
        sv[:], sv[:], 1.0 / QMAX, None, ALU.mult), reads=["sv"], writes=["sv"])
    P.op("vector", lambda v: v.reciprocal(rv[:], sv[:]),
         reads=["sv"], writes=["rv"])
    for (src, dst, kk) in ((sv, scol, "scol_b"), (rv, rcol, "rcol_b")):
        P.op("tensor", lambda t, s_=src: t.matmul(
            PSb[:, 0:1], onesr[:], s_[:], start=True, stop=True),
            reads=[("sv" if src is sv else "rv"), onesr_key],
            waits=P.peek_war([psb_key]), writes=[psb_key])
        P.op("vector", lambda v, d_=dst: v.tensor_copy(d_[:], PSb[:, 0:1]),
             reads=[psb_key], writes=[kk])
    return scol, rcol


def emit_ph5a(nc, P, ctx, d, pfx="ph5a_"):

    xn1 = d["xn1"]
    maxes = d["maxes"]
    clipx = d["clipx"]
    wgTi = d["wgTi"]
    wuTi = d["wuTi"]
    sgc_d = d["sgc"]
    suc_d = d["suc"]
    identb_d = d["identb"]
    identf_d = d["identf"]
    ones_d = d["onesrow"]
    hmid = d["hmid"]
    lm2 = d["lm2"]
    nfm_scr = d["nfm_scr"]

    sb = lambda n, s, dt=F32: ctx.enter_context(nc.sbuf_tensor(pfx + n, s, dt))
    psn = lambda n, s, dt=F32: ctx.enter_context(nc.psum_tensor(pfx + n, s, dt))
    PS2 = psn("PS2", [128, 2048])
    bank = lambda ps, k: ps[:, k * 512:(k + 1) * 512]
    onesr = sb("onesr", [1, 128])
    P.dma("qs", "sync", lambda e: e.dma_start(out=onesr[:], in_=ones_d),
          writes=["onesr"])
    scol, rcol = _emit_scalar_prep(P, nc, ctx, maxes, clipx, "onesr",
                                   onesr, PS2[:, 1536:2048], "B23", pfx)
    idb = sb("idb", [128, 128], BF16)
    idf = sb("idf", [128, 128], F32)
    P.dma("qs", "sync", lambda e: e.dma_start(out=idb[:], in_=identb_d),
          writes=["idb"])
    P.dma("qs", "sync", lambda e: e.dma_start(out=idf[:], in_=identf_d),
          writes=["idf"])
    sgc = sb("s_sgc", [128, IC1]); suc = sb("s_suc", [128, IC1])
    P.dma("qs", "sync", lambda e: e.dma_start(out=sgc[:], in_=sgc_d),
          writes=["sgc0"])
    P.dma("qs", "sync", lambda e: e.dma_start(out=suc[:], in_=suc_d),
          writes=["suc0"])
    P.op("vector", lambda v: v.tensor_scalar(
        sgc[:], sgc[:], scol[:], None, ALU.mult),
        reads=["sgc0", "scol_b"], writes=["sgc"])
    P.op("vector", lambda v: v.tensor_scalar(
        suc[:], suc[:], scol[:], None, ALU.mult),
        reads=["suc0", "scol_b"], writes=["suc"])

    xt = sb("xt", [128, D])
    vq = sb("vq", [128, D])
    vqb = sb("vqb", [128, D], BF16)
    ntile = sb("ntile", [128, 128], BF16)
    pstr = psn("pstr", [128, 128], BF16)
    for r in range(R // 128):
        P.dma("q5", "sync", lambda e, r0=r * 128: e.dma_start(
            out=xt[:], in_=xn1[r0:r0 + 128, :]), writes=["xt"])
        P.op("vector", lambda v: v.tensor_scalar(
            vq[:], xt[:], rcol[:], None, ALU.mult),
            reads=["xt", "rcol_b"], writes=["vq"])
        P.op("vector", lambda v: v.tensor_scalar(
            vq[:], vq[:], MAGIC, None, ALU.add), reads=["vq"], writes=["vq"])
        P.op("vector", lambda v: v.tensor_scalar(
            vq[:], vq[:], MAGIC, None, ALU.subtract),
            reads=["vq"], writes=["vq"])
        P.op("vector", lambda v: v.tensor_scalar(
            vq[:], vq[:], -QMAX - 1.0, QMAX, ALU.max, ALU.min),
            reads=["vq"], writes=["vq"])
        P.op("vector", lambda v: v.tensor_copy(vqb[:], vq[:]),
             reads=["vq"], writes=["vqb"])
        for c in range(DC):
            P.op("tensor", lambda t, cc=c: t.transpose(
                pstr[:], vqb[:, cc * 128:(cc + 1) * 128], idb[:]),
                reads=["vqb", "idb"], writes=["pstr"])
            P.op("vector", lambda v: v.tensor_copy(ntile[:], pstr[:]),
                 reads=["pstr"], writes=["ntile"])
            P.dma("q6", "gpsimd", lambda e, cc=c, r0=r * 128: e.dma_start(
                out=nfm_scr[cc][:, r0:r0 + 128], in_=ntile[:]),
                reads=["ntile"], writes=[("nfm", c, r)])
    nst = [sb(f"nst{c}", [128, 512], BF16) for c in range(DC)]
    wgs = [sb(f"wgs{c}", [128, I1], BF16) for c in range(DC)]
    wus = [sb(f"wus{c}", [128, I1], BF16) for c in range(DC)]
    sil_ = [sb(f"sil{u}", [128, 512]) for u in range(2)]
    upv_ = [sb(f"upv{u}", [128, 512]) for u in range(2)]
    hm_ = [sb(f"hm{u}", [128, 512]) for u in range(2)]
    runmax = sb("runmax", [128, 1])
    tmax_ = [sb(f"tmax{u}", [128, 1]) for u in range(2)]
    mrow_s = sb("mrow_s", [1, 128]); mone = sb("mone", [1, 1])
    ps_r = psn("ps_r", [1, 128], F32)
    P.op("vector", lambda v: v.memset(runmax[:], 0.0), writes=["runmax"])
    for c in range(DC):
        P.dma("qa", "sync", lambda e, cc=c: e.dma_start(
            out=wgs[cc][:], in_=wgTi[cc * 128:(cc + 1) * 128, :]),
            writes=[("wgs", c)])
        P.dma("qa", "sync", lambda e, cc=c: e.dma_start(
            out=wus[cc][:], in_=wuTi[cc * 128:(cc + 1) * 128, :]),
            writes=[("wus", c)])
    for st in range(8):
        for c in range(DC):
            P.dma("qs", "sync", lambda e, cc=c, s0=st * 512: e.dma_start(
                out=nst[cc][:], in_=nfm_scr[cc][:, s0:s0 + 512]),
                reads=[("nfm", c, rr) for rr in range(st * 4, st * 4 + 4)],
                writes=[("nst", c)])
        for ic in range(IC1):
            u = ic % 2
            sil = sil_[u]; upv = upv_[u]; hm = hm_[u]
            psg = bank(PS2, u)
            psu = bank(PS2, 2 + u)
            k_loop(P, psg,
                   [(wgs[c][:, ic * 128:(ic + 1) * 128], nst[c][:])
                    for c in range(DC)],
                   reads=[("wgs", c) for c in range(DC)]
                   + [("nst", c) for c in range(DC)], writes=[("B2", u)])
            k_loop(P, psu,
                   [(wus[c][:, ic * 128:(ic + 1) * 128], nst[c][:])
                    for c in range(DC)],
                   reads=[("wus", c) for c in range(DC)]
                   + [("nst", c) for c in range(DC)], writes=[("B2", 2 + u)])
            P.op("scalar", lambda a_, i0=ic, sil=sil, psg=psg: a_.activation(
                sil[:], psg, AF.Silu, scale=sgc[:, i0:i0 + 1]),
                reads=[("B2", u), "sgc"], writes=[("sil", u)])
            P.op("vector", lambda v, i0=ic, upv=upv, psu=psu: v.tensor_scalar(
                upv[:], psu, suc[:, i0:i0 + 1], None, ALU.mult),
                reads=[("B2", 2 + u), "suc"], writes=[("upv", u)])
            P.op("vector", lambda v, hm=hm, sil=sil, upv=upv: v.tensor_tensor(
                hm[:], sil[:], upv[:], ALU.mult),
                reads=[("sil", u), ("upv", u)], writes=[("hm", u)])
            P.op("vector", lambda v, hm=hm, u=u: v.reduce_max(
                tmax_[u][:], hm[:], axis=AX.X, apply_absolute_value=True),
                reads=[("hm", u)], writes=[("tmax", u)])
            P.op("vector", lambda v, u=u: v.tensor_tensor(
                runmax[:], runmax[:], tmax_[u][:], ALU.max),
                reads=["runmax", ("tmax", u)], writes=["runmax"])
            P.dma("qg", "gpsimd", lambda e, i0=ic, s0=st * 512, hm=hm:
                  e.dma_start(out=hmid[i0][:, s0:s0 + 512], in_=hm[:]),
                  reads=[("hm", u)])
    P.op("tensor", lambda t: t.matmul(
        ps_r[:], runmax[:, 0:1], idf[:], start=True, stop=True),
        reads=["runmax", "idf"], writes=["ps_r"])
    P.op("vector", lambda v: v.tensor_copy(mrow_s[:], ps_r[:]),
         reads=["ps_r"], writes=["mrow_s"])
    P.op("vector", lambda v: v.reduce_max(mone[:], mrow_s[:], axis=AX.X),
         reads=["mrow_s"], writes=["mone"])
    P.dma("qg", "gpsimd", lambda e: e.dma_start(out=lm2, in_=mone[:]),
          reads=["mone"], writes=[("lm2_hbm",)])
    P.op("gpsimd", lambda g_: g_.engine_nop(), waits=[P.dma_token("qg")])
    return


def emit_ph5b(nc, P, ctx, d, pfx="ph5b_"):

    hmid = d["hmid"]
    maxes = d["maxes"]
    clipd = d["clipd"]
    wdTq_d = d["wdTq"]
    x2q = d["x2q"]
    ones_d = d["onesrow"]
    OUT = d["OUT"]

    sb = lambda n, s, dt=F32: ctx.enter_context(nc.sbuf_tensor(pfx + n, s, dt))
    psn = lambda n, s, dt=F32: ctx.enter_context(nc.psum_tensor(pfx + n, s, dt))
    PS2 = psn("PS2", [128, 2048])
    bank = lambda ps, k: ps[:, k * 512:(k + 1) * 512]
    onesr = sb("onesr", [1, 128])
    P.dma("qs", "sync", lambda e: e.dma_start(out=onesr[:], in_=ones_d),
          writes=["onesr"])
    scol, rcol = _emit_scalar_prep(P, nc, ctx, maxes, clipd, "onesr",
                                   onesr, PS2[:, 1536:2048], "B23", pfx)
    wdq = [sb(f"wdq{i}", [128, D], BF16) for i in range(IC1)]
    for i in range(IC1):
        P.dma("qs", "sync", lambda e, i0=i: e.dma_start(
            out=wdq[i0][:], in_=wdTq_d[i0 * 128:(i0 + 1) * 128, :]),
            writes=[("wdq", i)])
    hl = [sb(f"hl{u}", [128, 512]) for u in range(2)]
    mfm = [sb(f"mfm{i}", [128, 512], BF16) for i in range(IC1)]
    x2t = sb("x2t", [128, D])
    outt = sb("outt", [128, D])
    for st in range(8):
        for ic in range(IC1):
            u = ic % 2
            P.dma("qa" if u == 0 else "qv", "sync",
                  lambda e, i0=ic, s0=st * 512, b=hl[u]:
                  e.dma_start(out=b[:], in_=hmid[i0][:, s0:s0 + 512]),
                  writes=[("hl", u)])
            P.op("vector", lambda v, b=hl[u]: v.tensor_scalar(
                b[:], b[:], rcol[:], None, ALU.mult),
                reads=[("hl", u), "rcol_b"], writes=[("hl", u)])
            P.op("vector", lambda v, b=hl[u]: v.tensor_scalar(
                b[:], b[:], MAGIC, None, ALU.add),
                reads=[("hl", u)], writes=[("hl", u)])
            P.op("vector", lambda v, b=hl[u]: v.tensor_scalar(
                b[:], b[:], MAGIC, None, ALU.subtract),
                reads=[("hl", u)], writes=[("hl", u)])
            P.op("vector", lambda v, b=hl[u]: v.tensor_scalar(
                b[:], b[:], -QMAX - 1.0, QMAX, ALU.max, ALU.min),
                reads=[("hl", u)], writes=[("hl", u)])
            P.op("vector", lambda v, b=hl[u], i0=ic: v.tensor_copy(
                mfm[i0][:], b[:]), reads=[("hl", u)], writes=[("mfm", ic)])
        for tile in range(4):
            rr = st * 512 + tile * 128
            P.dma("q5", "sync", lambda e, r0=rr: e.dma_start(
                out=x2t[:], in_=x2q[r0:r0 + 128, :]), writes=["x2t"])
            for ds in range(2):
                k_loop(P, bank(PS2, ds),
                       [(mfm[ic][:, tile * 128:(tile + 1) * 128],
                         wdq[ic][:, ds * 512:(ds + 1) * 512])
                        for ic in range(IC1)],
                       reads=[("mfm", ic) for ic in range(IC1)]
                       + [("wdq", ic) for ic in range(IC1)],
                       writes=[("B2", ds)])
                P.op("vector", lambda v, d_=ds: v.tensor_scalar(
                    outt[:, d_ * 512:(d_ + 1) * 512], bank(PS2, d_),
                    scol[:], None, ALU.mult),
                    reads=[("B2", ds), "scol_b"], writes=[("outt", ds)])
                P.op("vector", lambda v, d_=ds: v.tensor_tensor(
                    outt[:, d_ * 512:(d_ + 1) * 512],
                    outt[:, d_ * 512:(d_ + 1) * 512],
                    x2t[:, d_ * 512:(d_ + 1) * 512], ALU.add),
                    reads=[("outt", ds), "x2t"], writes=[("outt", ds)])
            P.dma("qg", "gpsimd", lambda e, r0=rr: e.dma_start(
                out=OUT[r0:r0 + 128, :], in_=outt[:]),
                reads=[("outt", 0), ("outt", 1)])
    P.op("gpsimd", lambda g_: g_.engine_nop(), waits=[P.dma_token("qg")])
    return



def build_fused():
    nc = bass.Bass("TRN2", target_bir_lowering=False, debug=False)
    d = {}

    def din(name, shape, dt=F32):
        d[name] = nc.dram_tensor(name, shape, dt, kind="ExternalInput").ap()

    def dint(name, shape, dt=F32):
        d[name] = nc.dram_tensor(name, shape, dt, kind="Internal").ap()

    din("x0", [R, D]); din("wq", [D, 512]); din("wk", [D, 512])
    din("wv", [D, 512]); din("wo", [4, 128, D]); din("wgates", [D, H])
    din("rmsc", [128, DC]); din("maskIJ", [4, 128, 512], BF16)
    din("maskT", [4, 128, 512], BF16); din("identf", [128, 128])
    din("identb", [128, 128], BF16); din("erow", [1, 128])
    din("onesrow", [1, 128])
    din("w1", [D, F0]); din("w2", [F0, D]); din("Wh", [D, 2 * HID])
    din("Wo", [HID, D]); din("wg", [I1, D]); din("wgT", [D, I1])
    din("wu", [I1, D]); din("wuT", [D, I1]); din("wd", [D, I1])
    din("wdT", [I1, D])
    din("b1c", [128, FC0]); din("b2", [1, D])
    din("lngc", [128, DC]); din("lnbc", [128, DC])
    din("Wqk", [D, QK]); din("bhv", [1, HID]); din("bhgc", [128, HC])
    din("bqkc", [128, 1]); din("gb", [128, 4]); din("bor", [1, D])
    din("clipx", [1, 1]); din("clipd", [1, 1])
    dint("xnfm", [DC, 128, R], BF16); dint("tsc", [32, 128])
    dint("x1p", [R, D]); dint("x1", [R, D]); dint("xn1", [R, D])
    dint("normfm", [DC, 128, R], BF16); dint("x2p", [R, D])
    dint("x2q", [R, D]); dint("hmid", [IC1, 128, R])
    dint("lm1", [1, 1]); dint("lm2", [1, 1])
    dint("w1b", [D, F0], BF16); dint("w2b", [F0, D], BF16)
    dint("Whb", [D, 2 * HID], BF16); dint("Wob", [HID, D], BF16)
    dint("wgTi", [D, I1], BF16); dint("wuTi", [D, I1], BF16)
    dint("wdTq", [I1, D], BF16)
    dint("sgc", [128, IC1]); dint("suc", [128, IC1])
    dint("mrow_scr", [16, 128]); dint("crow_scr", [2, 16, 128])
    dint("v_scr", [16, 128, HID], BF16)
    dint("g_scr", [HC, 128, N], BF16); dint("gd_scr", [HC, 128, N], BF16)
    d["OUT"] = nc.dram_tensor("OUT", [R, D], F32, kind="ExternalOutput").ap()
    dint("nfm_scr", [DC, 128, R], BF16)
    cc1_in = nc.dram_tensor("cc1_in", [128, 128], F32, kind="Internal")
    cc1_out = nc.dram_tensor("cc1_out", [128, 128], F32, kind="Internal")
    cc2_in = nc.dram_tensor("cc2_in", [128, 128], F32, kind="Internal")
    cc2_out = nc.dram_tensor("cc2_out", [128, 128], F32, kind="Internal")

    with ExitStack() as octx:
        P = Prog(nc, octx)

        def phase(emit, dd, pfx):
            with ExitStack() as ctx:
                emit(nc, P, ctx, dd, pfx)
            P.join()

        phase(emit_phW, d, "W_")
        phase(emit_ph0, d, "A_")
        phase(emit_ph1, d, "B_")
        phase(emit_ph2, dict(d, x1p=d["x1p"]), "C_")
        phase(emit_ph3, d, "D_")

        def collective(src_ap, cin, cout, tag):
            with ExitStack() as ctx:
                pad = ctx.enter_context(
                    nc.sbuf_tensor(tag + "pad", [128, 128], F32))
                P.op("vector", lambda v: v.memset(pad[:], 0.0),
                     writes=[tag + "pad"])
                P.dma("qg", "gpsimd", lambda e: e.dma_start(
                    out=pad[0:1, 0:1], in_=src_ap),
                    reads=[(tag + "lm",)], writes=[tag + "pad0"])
                P.dma("qg", "gpsimd", lambda e: e.dma_start(
                    out=cin[:, :], in_=pad[:]),
                    reads=[tag + "pad", tag + "pad0"], writes=[tag + "in"])
                P.op("gpsimd", lambda g_: g_.collective_compute(
                    "AllReduce", ALU.max,
                    replica_groups=[list(range(NCORES))],
                    ins=[cin.ap().opt()], outs=[cout.ap().opt()]),
                    reads=[tag + "in"], writes=[tag + "done"])

        P.last_write[("cc1lm",)] = P.last_write.get(("lm1_hbm",))
        collective(d["lm1"], cc1_in, cc1_out, "cc1")
        phase(emit_ph4, d, "E_")
        d5a = dict(d, maxes=cc1_out.ap()[0:1, 0:8])
        phase(emit_ph5a, d5a, "F_")
        P.last_write[("cc2lm",)] = P.last_write.get(("lm2_hbm",))
        collective(d["lm2"], cc2_in, cc2_out, "cc2")
        P.join()
        d5b = dict(d, maxes=cc2_out.ap()[0:1, 0:8], x2q=d["x2q"])
        d5b["OUT"] = d["OUT"]
        phase(emit_ph5b, d5b, "G_")
        block = octx.enter_context(nc.Block())
        P.emit(block)
    return nc


_FUSED_NC = None


def kernel(**inp):
    global _FUSED_NC
    x = np.ascontiguousarray(np.asarray(inp["L0_attn_tokens"], np.float32)
                             .reshape(NCORES, SEQ * N, D))
    wq = np.ascontiguousarray(np.transpose(
        np.asarray(inp["L0_attn_wq"], np.float32), (1, 0, 2)).reshape(D, 512))
    wk = np.ascontiguousarray(np.transpose(
        np.asarray(inp["L0_attn_wk"], np.float32), (1, 0, 2)).reshape(D, 512))
    wv = np.ascontiguousarray(np.transpose(
        np.asarray(inp["L0_attn_wv"], np.float32), (1, 0, 2)).reshape(D, 512))
    wo = np.ascontiguousarray(np.asarray(inp["L0_attn_wo"], np.float32)
                              .reshape(4, 128, D))
    wgates = np.ascontiguousarray(np.asarray(inp["L0_attn_w_gates"], np.float32))
    rmsc = _cols(inp["L0_attn_rms_w"], DC)
    mask_T, mask_IJ = _host_masks()
    mask_T = mask_T.astype(ml_dtypes.bfloat16)
    mask_IJ = mask_IJ.astype(ml_dtypes.bfloat16)
    erow = np.zeros((1, 128), np.float32); erow[0, 64] = 1.0
    onesrow = np.ones((1, 128), np.float32)
    identf = np.eye(128, dtype=np.float32)
    identb = np.eye(128).astype(ml_dtypes.bfloat16)
    w1 = np.ascontiguousarray(np.asarray(inp["L0_ffn_w1"], np.float32))
    b1c = _cols(inp["L0_ffn_b1"], FC0)
    w2 = np.ascontiguousarray(np.asarray(inp["L0_ffn_w2"], np.float32))
    b2 = np.asarray(inp["L0_ffn_b2"], np.float32).reshape(1, D)
    Wh = np.ascontiguousarray(np.asarray(inp["L1_attn_Wh"], np.float32))
    Wo1 = np.ascontiguousarray(np.asarray(inp["L1_attn_Wo"], np.float32))
    wg = np.ascontiguousarray(np.asarray(inp["L1_ffn_w_gate"], np.float32))
    wu = np.ascontiguousarray(np.asarray(inp["L1_ffn_w_up"], np.float32))
    wd = np.ascontiguousarray(np.asarray(inp["L1_ffn_w_down"], np.float32))
    wgT = np.ascontiguousarray(wg.T)
    wuT = np.ascontiguousarray(wu.T)
    wdT = np.ascontiguousarray(wd.T)
    lngc = _cols(inp["L1_attn_ln_g"], DC)
    lnbc = _cols(inp["L1_attn_ln_b"], DC)
    bh = np.asarray(inp["L1_attn_bh"], np.float32)
    bhv = bh[:HID].reshape(1, HID)
    bhgc = _cols(bh[HID:], HC)
    bqkc = np.ascontiguousarray(np.asarray(inp["L1_attn_bqk"], np.float32)
                                .reshape(QK, 1))
    gamma = np.asarray(inp["L1_attn_gamma"], np.float32)
    beta = np.asarray(inp["L1_attn_beta"], np.float32)
    gb = np.ascontiguousarray(np.stack(
        [gamma[0], gamma[1], beta[0], beta[1]], axis=1))
    Wqk = np.ascontiguousarray(np.asarray(inp["L1_attn_Wqk"], np.float32))
    bo = np.asarray(inp["L1_attn_bo"], np.float32).reshape(1, D)
    clipx = np.asarray(inp["L1_ffn_clip_x"], np.float32).reshape(1, 1)
    clipd = np.asarray(inp["L1_ffn_clip_down"], np.float32).reshape(1, 1)

    if _FUSED_NC is None:
        _FUSED_NC = build_fused()
    nc = _FUSED_NC
    common = {"wq": wq, "wk": wk, "wv": wv, "wo": wo, "wgates": wgates,
              "rmsc": rmsc, "maskIJ": mask_IJ, "maskT": mask_T,
              "identf": identf, "identb": identb, "erow": erow,
              "onesrow": onesrow, "w1": w1, "w2": w2, "Wh": Wh, "Wo": Wo1,
              "wg": wg, "wgT": wgT, "wu": wu, "wuT": wuT, "wd": wd,
              "wdT": wdT, "b1c": b1c, "b2": b2, "lngc": lngc, "lnbc": lnbc,
              "Wqk": Wqk, "bhv": bhv, "bhgc": bhgc, "bqkc": bqkc, "gb": gb,
              "bor": bo, "clipx": clipx, "clipd": clipd}
    cores = list(range(NCORES))
    in_maps = [dict(common, x0=x[c]) for c in cores]
    res = run_bass_kernel_spmd(nc, in_maps, core_ids=cores)
    out = np.stack([res.results[c]["OUT"] for c in cores]).reshape(16, N, D)
    return out


# version 22
# speedup vs baseline: 1.1158x; 1.1158x over previous
import math
from contextlib import ExitStack

import numpy as np
import ml_dtypes

import concourse.bass as bass
import concourse.mybir as mybir
from concourse.bass_utils import run_bass_kernel_spmd

F32 = mybir.dt.float32
BF16 = mybir.dt.bfloat16
AF = mybir.ActivationFunctionType
ALU = mybir.AluOpType
AX = mybir.AxisListType
MAGIC = float(np.float32(1.5 * 2**23))
NEG = -1e9
EPS_LN = 1e-5
EPS_RMS = float(np.finfo(np.float32).eps)
QMAX = 127.0

NCORES = 8
SEQ = 2
N = 2048
R = SEQ * N
D = 1024
DC = D // 128
H = 8
E = 64
F0 = 4096
FC0 = F0 // 128
HID = 2048
HC = HID // 128
QK = 128
I1 = 2816
IC1 = I1 // 128


class Prog:

    ENGINES = ("sync", "tensor", "vector", "scalar", "gpsimd")
    QUEUES = ("qs", "qg", "qa", "qv", "q5", "q6")

    def __init__(self, nc, ctx):
        self.nc = nc
        self.ops = []
        self.join_pos = set()
        self.sem = {}
        self.count = {e: 0 for e in self.ENGINES}
        self.dma_count = {q: 0 for q in self.QUEUES}
        for e in self.ENGINES:
            self.sem[e] = ctx.enter_context(nc.semaphore("sm_" + e))
        for q in self.QUEUES:
            self.sem[q] = ctx.enter_context(nc.semaphore("sm_" + q))
        self.last_write = {}
        self.readers = {}

    def _record(self, eng, fn, reads, writes, waits, tsem_name, tamt):
        w = list(waits)
        for b in reads:
            t = self.last_write.get(b)
            if t:
                w.append(t)
        for b in writes:
            t = self.last_write.get(b)
            if t:
                w.append(t)
            w.extend(self.readers.get(b, ()))
        if tsem_name in self.dma_count:
            self.dma_count[tsem_name] += tamt
            tok = (tsem_name, self.dma_count[tsem_name])
        else:
            self.count[eng] += tamt
            tok = (eng, self.count[eng])
        self.ops.append((eng, fn, tuple(w), self.sem[tsem_name], tamt))
        for b in reads:
            self.readers.setdefault(b, []).append(tok)
        for b in writes:
            self.last_write[b] = tok
            self.readers[b] = []
        return tok

    def op(self, eng, fn, reads=(), writes=(), waits=()):
        return self._record(eng, fn, reads, writes, waits, eng, 1)

    def peek_raw(self, bufs):
        out = []
        for b in bufs:
            t = self.last_write.get(b)
            if t:
                out.append(t)
        return out

    def peek_war(self, bufs):
        out = []
        for b in bufs:
            t = self.last_write.get(b)
            if t:
                out.append(t)
            out.extend(self.readers.get(b, ()))
        return out

    def dma(self, queue, eng, fn, reads=(), writes=(), waits=()):
        return self._record(eng, fn, reads, writes, waits, queue, 16)

    def barrier_token(self, eng):
        return (eng, self.count[eng])

    def join(self):
        self.join_pos.add(len(self.ops))
        toks = [(e, self.count[e]) for e in self.ENGINES] + \
               [(q, self.dma_count[q]) for q in self.QUEUES]
        for e in self.ENGINES:
            self.op(e, lambda h: h.nop(), waits=list(toks))

    def dma_token(self, queue):
        return (queue, self.dma_count[queue])

    def emit(self, block):
        qname = {}
        for q in self.QUEUES:
            qname[id(self.sem[q])] = q
        dma_running = {q: 0 for q in self.QUEUES}
        resolved = []
        for eng, fn, waits, tsem, tamt in self.ops:
            w = []
            for (s, c) in waits:
                if s in dma_running:
                    c = dma_running[s]
                if c > 0:
                    w.append((s, c))
            is_q = qname.get(id(tsem))
            resolved.append([eng, fn, w, tsem, tamt, is_q])
            if is_q:
                dma_running[is_q] += tamt
        boundaries = {q: [0] for q in self.QUEUES}
        running = {q: 0 for q in self.QUEUES}
        for pos, rec in enumerate(resolved):
            if pos in self.join_pos:
                for q in self.QUEUES:
                    if running[q] > boundaries[q][-1]:
                        boundaries[q].append(running[q])
            eng, fn, w, tsem, tamt, is_q = rec
            if is_q:
                if w and running[is_q] > boundaries[is_q][-1]:
                    boundaries[is_q].append(running[is_q])
                running[is_q] += tamt
            elif w:
                for q in self.QUEUES:
                    if running[q] > boundaries[q][-1]:
                        pass
        for q in self.QUEUES:
            if running[q] > boundaries[q][-1]:
                boundaries[q].append(running[q])

        def round_up(q, c):
            import bisect
            b = boundaries[q]
            i = bisect.bisect_left(b, c)
            return b[i] if i < len(b) else b[-1]

        per_eng = {e: [] for e in self.ENGINES}
        for eng, fn, w, tsem, tamt, is_q in resolved:
            w = [(s, round_up(s, c) if s in self.QUEUES else c) for (s, c) in w]
            per_eng[eng].append((fn, w, tsem, tamt, bool(is_q)))
        for eng in self.ENGINES:
            steps = per_eng[eng]
            if not steps:
                continue
            sems = self.sem

            def body(handle, steps=steps):
                seen = {}
                for fn, w, tsem, tamt, isdma in steps:
                    for (s, c) in w:
                        if isdma or seen.get(s, 0) < c:
                            handle.wait_ge(sems[s], c)
                            seen[s] = max(seen.get(s, 0), c)
                    fn(handle).then_inc(tsem, tamt)

            getattr(block, eng)(body)


class ConstBank:

    def __init__(self, P, nc, ctx, n=16, pfx=""):
        self.P = P
        self.tile = ctx.enter_context(
            nc.sbuf_tensor(pfx + "konst", [128, n], F32))
        self.vals = {}
        self.pfx = pfx

    def get(self, val):
        val = float(val)
        if val not in self.vals:
            i = len(self.vals)
            self.vals[val] = i
            self.P.op("vector",
                      lambda v, i=i, val=val: v.memset(
                          self.tile[:, i:i + 1], val),
                      writes=[(self.pfx + "konst", i)])
        return self.tile[:, self.vals[val]:self.vals[val] + 1]

    def key(self, val):
        return (self.pfx + "konst", self.vals[float(val)])


def k_loop(P, psum_ap, pairs, reads, writes, waits=()):
    n = len(pairs)
    first_waits = list(waits) + P.peek_raw(reads) + P.peek_war(writes)
    for k, (l, r) in enumerate(pairs):
        P.op("tensor",
             lambda t, o=psum_ap, l=l, r=r, st=(k == 0), sp=(k == n - 1):
             t.matmul(o, l, r, start=st, stop=sp),
             reads=(reads if k == n - 1 else ()),
             writes=(writes if k == n - 1 else ()),
             waits=(first_waits if k == 0 else ()))



def _host_masks():
    j = np.arange(128)[:, None]
    i = np.arange(512)[None, :]
    mask_T = np.stack([np.where(d * 128 + j <= i, 0.0, NEG)
                       for d in range(4)]).astype(np.float32)
    ii = np.arange(128)[:, None]
    jj = np.arange(512)[None, :]
    mask_IJ = np.stack([np.where(jj <= v * 128 + ii, 0.0, NEG)
                        for v in range(4)]).astype(np.float32)
    return mask_T, mask_IJ


def _host_selpair():
    sel = np.zeros((8, 8 * 128), np.float32)
    for p in range(4):
        sel[2 * p, p * 128:p * 128 + 64] = 1.0
        sel[2 * p + 1, p * 128 + 64:p * 128 + 128] = 1.0
    return sel


def _cols(vec, nch):
    return np.ascontiguousarray(np.asarray(vec, np.float32)
                                .reshape(nch, 128).T)



def emit_phW(nc, P, ctx, d, pfx="phW_", part=0):



    sb = lambda n, s, dt=F32: ctx.enter_context(nc.sbuf_tensor(pfx + n, s, dt))
    psn = lambda n, s, dt=F32: ctx.enter_context(nc.psum_tensor(pfx + n, s, dt))
    fb = [sb(f"fb{i}", [128, 4096]) for i in range(2)]
    bb = [sb(f"bb{i}", [128, 4096], BF16) for i in range(2)]

    def cast_stream(src, dst, nrows, ncols):
        for c in range(nrows // 128):
            s = c % 2
            P.dma("qs" if s == 0 else "qa", "sync",
                  lambda e, o=fb[s], i=src[c * 128:(c + 1) * 128, :],
                         n_=ncols: e.dma_start(out=o[:, 0:n_], in_=i),
                  writes=[f"fb{s}"])
            P.op("vector",
                 lambda v, o=bb[s], i=fb[s], n_=ncols:
                 v.tensor_copy(o[:, 0:n_], i[:, 0:n_]),
                 reads=[f"fb{s}"], writes=[f"bb{s}"])
            P.dma("qg", "gpsimd",
                  lambda e, o=dst[c * 128:(c + 1) * 128, :], i=bb[s],
                         n_=ncols: e.dma_start(out=o, in_=i[:, 0:n_]),
                  reads=[f"bb{s}"])

    if part in (0, 1):
        cast_stream(d["w1"], d["w1b"], D, F0)
        cast_stream(d["w2"], d["w2b"], F0, D)
        cast_stream(d["Wh"], d["Whb"], D, 2 * HID)
        cast_stream(d["Wo"], d["Wob"], HID, D)
    if part == 1:
        return

    ident = sb("ident", [128, 128])
    ones1 = sb("ones1", [1, 128])
    P.dma("qs", "sync", lambda e: e.dma_start(out=ident[:], in_=d["identf"]),
          writes=["ident"])
    P.op("vector", lambda v: v.memset(ones1[:], 1.0), writes=["ones1"])

    nat = sb("nat", [128, 4096])
    scol = sb("scol", [128, 1])
    ps_r = psn("ps_r", [1, 128])
    srow = {w: sb(f"srow_{w}", [1, 4096]) for w in ("wg", "wu", "wd")}
    sgcol = sb("sgcol", [128, IC1]); sucol = sb("sucol", [128, IC1])
    coltile = {"wg": sgcol, "wu": sucol}
    for wname, nchunks, ncols in (("wg", IC1, D), ("wu", IC1, D),
                                  ("wd", DC, I1)):
        for c in range(nchunks):
            P.dma("qv", "sync",
                  lambda e, i=d[wname][c * 128:(c + 1) * 128, :], n_=ncols:
                  e.dma_start(out=nat[:, 0:n_], in_=i), writes=["nat"])
            P.op("vector",
                 lambda v, n_=ncols: v.reduce_max(
                     scol[:], nat[:, 0:n_], axis=AX.X,
                     apply_absolute_value=True),
                 reads=["nat"], writes=["scol"])
            P.op("vector",
                 lambda v: v.tensor_scalar(scol[:], scol[:], 1.0 / QMAX,
                                           None, ALU.mult),
                 reads=["scol"], writes=["scol"])
            if wname in coltile:
                P.op("vector",
                     lambda v, o=coltile[wname], cc=c: v.tensor_copy(
                         o[:, cc:cc + 1], scol[:]),
                     reads=["scol"], writes=[("wcol", wname, c)])
            P.op("tensor",
                 lambda t: t.transpose(ps_r[:], scol[:, 0:1], ident[:]),
                 reads=["scol", "ident"], writes=["ps_r"])
            P.op("vector",
                 lambda v, o=srow[wname], cc=c:
                 v.tensor_copy(o[:, cc * 128:(cc + 1) * 128], ps_r[:]),
                 reads=["ps_r"], writes=[f"srow_{wname}_{c}"])
    P.dma("qg", "gpsimd",
          lambda e: e.dma_start(out=d["sgc"], in_=sgcol[:]),
          reads=[("wcol", "wg", c) for c in range(IC1)])
    P.dma("qg", "gpsimd",
          lambda e: e.dma_start(out=d["suc"], in_=sucol[:]),
          reads=[("wcol", "wu", c) for c in range(IC1)])

    rrow = sb("rrow", [1, 4096])
    ps_b = psn("ps_b", [128, 512])
    bc = sb("bc", [128, 4096])
    sbc = sb("sbc", [128, 4096])
    qf = sb("qf", [128, 4096])
    qb = sb("qb", [128, 4096], BF16)

    for wname, tname, oname, scaled in (("wg", "wgT", "wgTi", False),
                                        ("wu", "wuT", "wuTi", False),
                                        ("wd", "wdT", "wdTq", True)):
        ncols = I1 if wname != "wd" else D
        nchunks = DC if wname != "wd" else IC1
        nsch = IC1 if wname != "wd" else DC
        P.op("vector",
             lambda v, s=srow[wname], n_=ncols:
             v.reciprocal(rrow[:, 0:n_], s[:, 0:n_]),
             reads=[f"srow_{wname}_{c}" for c in range(nsch)],
             writes=["rrow"])
        for st in range((ncols + 511) // 512):
            lo, hi = st * 512, min((st + 1) * 512, ncols)
            P.op("tensor",
                 lambda t, lo=lo, hi=hi: t.matmul(
                     ps_b[:, 0:hi - lo], ones1[:], rrow[0:1, lo:hi],
                     start=True, stop=True),
                 reads=["rrow", "ones1"], writes=["ps_b"])
            P.op("vector",
                 lambda v, lo=lo, hi=hi:
                 v.tensor_copy(bc[:, lo:hi], ps_b[:, 0:hi - lo]),
                 reads=["ps_b"], writes=[("bc", st)])
        if scaled:
            for st in range((ncols + 511) // 512):
                lo, hi = st * 512, min((st + 1) * 512, ncols)
                P.op("tensor",
                     lambda t, lo=lo, hi=hi, s=srow[wname]: t.matmul(
                         ps_b[:, 0:hi - lo], ones1[:], s[0:1, lo:hi],
                         start=True, stop=True),
                     reads=[f"srow_{wname}_{c}" for c in range(nsch)]
                     + ["ones1"], writes=["ps_b"])
                P.op("vector",
                     lambda v, lo=lo, hi=hi:
                     v.tensor_copy(sbc[:, lo:hi], ps_b[:, 0:hi - lo]),
                     reads=["ps_b"], writes=[("sbc", st)])
        bkeys = [("bc", st) for st in range((ncols + 511) // 512)]
        skeys = [("sbc", st) for st in range((ncols + 511) // 512)] \
            if scaled else []
        for c in range(nchunks):
            P.dma("q5", "sync",
                  lambda e, i=d[tname][c * 128:(c + 1) * 128, :], n_=ncols:
                  e.dma_start(out=qf[:, 0:n_], in_=i), writes=["qf"])
            P.op("vector",
                 lambda v, n_=ncols:
                 v.tensor_mul(qf[:, 0:n_], qf[:, 0:n_], bc[:, 0:n_]),
                 reads=["qf"] + bkeys, writes=["qf"])
            P.op("vector",
                 lambda v, n_=ncols: v.tensor_scalar(
                     qf[:, 0:n_], qf[:, 0:n_], MAGIC, None, ALU.add),
                 reads=["qf"], writes=["qf"])
            P.op("vector",
                 lambda v, n_=ncols: v.tensor_scalar(
                     qf[:, 0:n_], qf[:, 0:n_], MAGIC, None, ALU.subtract),
                 reads=["qf"], writes=["qf"])
            P.op("vector",
                 lambda v, n_=ncols: v.tensor_scalar(
                     qf[:, 0:n_], qf[:, 0:n_], -QMAX - 1.0, QMAX,
                     ALU.max, ALU.min),
                 reads=["qf"], writes=["qf"])
            if scaled:
                P.op("vector",
                     lambda v, n_=ncols:
                     v.tensor_mul(qf[:, 0:n_], qf[:, 0:n_], sbc[:, 0:n_]),
                     reads=["qf"] + skeys, writes=["qf"])
            P.op("scalar",
                 lambda a_, n_=ncols: a_.copy(qb[:, 0:n_], qf[:, 0:n_]),
                 reads=["qf"], writes=["qb"])
            P.dma("qg", "gpsimd",
                  lambda e, o=d[oname][c * 128:(c + 1) * 128, :], n_=ncols:
                  e.dma_start(out=o, in_=qb[:, 0:n_]), reads=["qb"])

    P.op("gpsimd", lambda g: g.engine_nop(),
         waits=[P.dma_token("qg"), P.dma_token("qs")])
    return


def emit_ph0(nc, P, ctx, d, pfx="ph0_"):
    x0 = d["x0"]; identb = d["identb"]; identf = d["identf"]
    xnfm = d["xnfm"]; tsc = d["tsc"]
    sb = lambda n, s, dt=F32: ctx.enter_context(nc.sbuf_tensor(pfx + n, s, dt))
    psn = lambda n, s, dt=F32: ctx.enter_context(nc.psum_tensor(pfx + n, s, dt))
    xt = [sb(f"xt{i}", [128, D]) for i in range(2)]
    sq_ = [sb(f"sq{u}", [128, D]) for u in range(2)]
    mcol_ = [sb(f"mcol{u}", [128, 1]) for u in range(2)]
    vcol_ = [sb(f"vcol{u}", [128, 1]) for u in range(2)]
    rstd_ = [sb(f"rstd{u}", [128, 1]) for u in range(2)]
    r2_ = [sb(f"r2{u}", [128, 1]) for u in range(2)]
    xn_ = [sb(f"xn{u}", [128, D]) for u in range(2)]
    xnb_ = [sb(f"xnb{u}", [128, D], BF16) for u in range(2)]
    idb = sb("idb", [128, 128], BF16)
    idf = sb("idf", [128, 128], F32)
    tcols = sb("tcols", [128, 32])
    fm = [sb(f"fm{c}", [128, R], BF16) for c in range(DC)]
    pstr_ = [psn(f"pstr{u}", [128, 128], BF16) for u in range(2)]
    ps_t = psn("ps_t", [32, 128], F32)
    K = ConstBank(P, nc, ctx, pfx=pfx)
    b_ln = K.get(EPS_LN); k_ln = K.key(EPS_LN)
    b_rms = K.get(EPS_RMS); k_rms = K.key(EPS_RMS)

    P.dma("qs", "sync", lambda e: e.dma_start(out=idb[:], in_=identb),
          writes=["idb"])
    P.dma("qs", "sync", lambda e: e.dma_start(out=idf[:], in_=identf),
          writes=["idf"])

    def tile(r, s):
        sq = sq_[s]; mcol = mcol_[s]; vcol = vcol_[s]; rstd = rstd_[s]
        r2 = r2_[s]; xn = xn_[s]; xnb = xnb_[s]; b = xt[s]; pstr = pstr_[s]
        _k = lambda nm: (nm, s)
        P.dma("qs" if s == 0 else "qa", "sync",
              lambda e: e.dma_start(out=b[:], in_=x0[r * 128:(r + 1) * 128, :]),
              writes=[f"xt{s}"])
        P.op("vector", lambda v: v.reduce_sum(mcol[:], b[:], axis=AX.X),
             reads=[f"xt{s}"], writes=[_k("mcol")])
        P.op("vector", lambda v: v.tensor_scalar(
            mcol[:], mcol[:], 1.0 / D, None, ALU.mult),
            reads=[_k("mcol")], writes=[_k("mcol")])
        P.op("vector", lambda v: v.tensor_scalar(
            xn[:], b[:], mcol[:], None, ALU.subtract),
            reads=[f"xt{s}", _k("mcol")], writes=[_k("xn")])
        P.op("scalar", lambda a: a.activation(sq[:], xn[:], AF.Square),
             reads=[_k("xn")], writes=[_k("sq")])
        P.op("vector", lambda v: v.reduce_sum(vcol[:], sq[:], axis=AX.X),
             reads=[_k("sq")], writes=[_k("vcol")])
        P.op("scalar", lambda a: a.activation(
            rstd[:], vcol[:], AF.Sqrt, bias=b_ln, scale=1.0 / D),
            reads=[_k("vcol"), k_ln], writes=[_k("rstd")])
        P.op("vector", lambda v: v.reciprocal(rstd[:], rstd[:]),
             reads=[_k("rstd")], writes=[_k("rstd")])
        P.op("vector", lambda v: v.tensor_scalar(
            xn[:], xn[:], rstd[:], None, ALU.mult),
            reads=[_k("xn"), _k("rstd")], writes=[_k("xn")])
        P.op("scalar", lambda a: a.activation(sq[:], xn[:], AF.Square),
             reads=[_k("xn")], writes=[_k("sq")])
        P.op("vector", lambda v: v.reduce_sum(r2[:], sq[:], axis=AX.X),
             reads=[_k("sq")], writes=[_k("r2")])
        P.op("scalar", lambda a: a.activation(
            r2[:], r2[:], AF.Sqrt, bias=b_rms, scale=1.0 / D),
            reads=[_k("r2"), k_rms], writes=[_k("r2")])
        P.op("vector", lambda v: v.reciprocal(r2[:], r2[:]),
             reads=[_k("r2")], writes=[_k("r2")])
        P.op("vector", lambda v: v.tensor_copy(tcols[:, r:r + 1], r2[:]),
             reads=[_k("r2")], writes=[("tc", r)])
        P.op("scalar", lambda a: a.copy(xnb[:], xn[:]),
             reads=[_k("xn")], writes=[_k("xnb")])
        for c in range(DC):
            P.op("tensor",
                 lambda t, cc=c: t.transpose(
                     pstr[:], xnb[:, cc * 128:(cc + 1) * 128], idb[:]),
                 reads=[_k("xnb"), "idb"], writes=[_k("pstr")])
            P.op("scalar",
                 lambda a, cc=c: a.copy(
                     fm[cc][:, r * 128:(r + 1) * 128], pstr[:]),
                 reads=[_k("pstr")], writes=[("fm", c, r)])

    for r in range(R // 128):
        tile(r, r % 2)
    P.op("tensor", lambda t: t.transpose(ps_t[:], tcols[:], idf[:]),
         reads=[("tc", r) for r in range(32)] + ["idf"], writes=["ps_t"])
    trow = sb("trow", [32, 128], F32)
    P.op("vector", lambda v: v.tensor_copy(trow[:], ps_t[:]),
         reads=["ps_t"], writes=["trow"])
    P.dma("qg", "gpsimd", lambda e: e.dma_start(out=tsc, in_=trow[:]),
          reads=["trow"])
    for c in range(DC):
        P.dma("qg", "gpsimd",
              lambda e, cc=c: e.dma_start(out=xnfm[cc], in_=fm[cc][:]),
              reads=[("fm", c, r) for r in range(32)])
    P.op("gpsimd", lambda g: g.engine_nop(), waits=[P.dma_token("qg")])
    return

def emit_ph2(nc, P, ctx, d, pfx="ph2_"):
    TS = 512
    NS = R // TS

    """L0 FFN: h=xn@w1+b1 -> q256 -> exact PWL gelu -> q256 -> @w2+b2 -> q256;
    x1 = x1p + f0."""
    xnfm = d["xnfm"]
    x1p = d["x1p"]
    w1b = d["w1b"]
    w2b = d["w2b"]
    b1c = d["b1c"]
    b2 = d["b2"]
    x1 = d["x1"]
    sb = lambda n, s, dt=F32: ctx.enter_context(nc.sbuf_tensor(pfx + n, s, dt))
    psn = lambda n, s, dt=F32: ctx.enter_context(nc.psum_tensor(pfx + n, s, dt))
    w1s = [sb(f"w1s{c}", [128, F0], BF16) for c in range(DC)]
    w2s = [sb(f"w2s{c}", [128, D], BF16) for c in range(FC0)]
    b1t = sb("b1t", [128, FC0])
    ones1 = sb("ones1", [1, 128])
    b2bc = sb("b2bc", [128, D])
    xns = [sb(f"xns{c}", [128, TS], BF16) for c in range(DC)]
    g = [sb(f"g{c}", [128, TS], BF16) for c in range(FC0)]
    XQ4_ = [sb(f"XQ4{u}", [128, TS]) for u in range(2)]
    I4_ = [sb(f"I4{u}", [128, TS]) for u in range(2)]
    FR_ = [sb(f"FR{u}", [128, TS]) for u in range(2)]
    YLO_ = [sb(f"YLO{u}", [128, TS]) for u in range(2)]
    YHI_ = [sb(f"YHI{u}", [128, TS]) for u in range(2)]
    YY_ = [sb(f"YY{u}", [128, TS]) for u in range(2)]
    x1t = sb("x1t", [128, D]); outt = sb("outt", [128, D])
    ps_h_ = [psn(f"ps_h{u}", [128, TS]) for u in range(2)]
    ps_f = psn("ps_f", [128, D])
    ps_bc = psn("ps_bc", [128, 512])
    K = ConstBank(P, nc, ctx, pfx=pfx)
    b_q = K.get(0.25); k_q = K.key(0.25)

    for c in range(DC):
        P.dma("qs", "sync",
              lambda e, cc=c: e.dma_start(
                  out=w1s[cc][:], in_=w1b[cc * 128:(cc + 1) * 128, :]),
              writes=[("w1", c)])
    for c in range(FC0):
        P.dma("qs", "sync",
              lambda e, cc=c: e.dma_start(
                  out=w2s[cc][:], in_=w2b[cc * 128:(cc + 1) * 128, :]),
              writes=[("w2", c)])
    P.dma("qs", "sync", lambda e: e.dma_start(out=b1t[:], in_=b1c),
          writes=["b1t"])
    P.dma("qs", "sync", lambda e: e.dma_start(out=b2bc[0:1, :], in_=b2),
          writes=["b2row"])
    P.op("vector", lambda v: v.memset(ones1[:], 1.0), writes=["ones1"])
    for dstrip in range(2):
        P.op("tensor", lambda t, ds=dstrip: t.matmul(
            ps_bc[:], ones1[:], b2bc[0:1, ds * 512:(ds + 1) * 512],
            start=True, stop=True),
            reads=["b2row", "ones1"], writes=["ps_bc"])
        P.op("vector", lambda v, ds=dstrip: v.tensor_copy(
            b2bc[:, ds * 512:(ds + 1) * 512], ps_bc[:]),
            reads=["ps_bc"], writes=[("b2bc", dstrip)])

    w1keys = [("w1", c) for c in range(DC)]
    w2keys = [("w2", c) for c in range(FC0)]
    for s in range(NS):
        for c in range(DC):
            P.dma("qs", "sync",
                  lambda e, cc=c, ss=s: e.dma_start(
                      out=xns[cc][:],
                      in_=xnfm[cc][:, ss * TS:(ss + 1) * TS]),
                  writes=[("xns", c)])
        for fc in range(FC0):
            u = fc % 2
            XQ4 = XQ4_[u]; I4 = I4_[u]; FR = FR_[u]
            YLO = YLO_[u]; YHI = YHI_[u]; YY = YY_[u]
            ps_h = ps_h_[u]
            kx = lambda nm, u=u: (nm, u)
            k_loop(P, ps_h[:],
                   [(w1s[c][:, fc * 128:(fc + 1) * 128], xns[c][:])
                    for c in range(DC)],
                   reads=w1keys + [("xns", c) for c in range(DC)] + ["b1t"],
                   writes=[kx("ps_h")])
            P.op("vector", lambda v, f=fc, XQ4=XQ4, ps_h=ps_h: v.tensor_scalar(
                XQ4[:], ps_h[:], b1t[:, f:f + 1], 256.0, ALU.add, ALU.mult),
                reads=[kx("ps_h"), "b1t"], writes=[kx("XQ4")])
            P.op("vector", lambda v, XQ4=XQ4: v.tensor_scalar(
                XQ4[:], XQ4[:], MAGIC, None, ALU.add),
                reads=[kx("XQ4")], writes=[kx("XQ4")])
            P.op("vector", lambda v, XQ4=XQ4: v.tensor_scalar(
                XQ4[:], XQ4[:], MAGIC, 2.0 ** -6, ALU.subtract, ALU.mult),
                reads=[kx("XQ4")], writes=[kx("XQ4")])
            P.op("vector", lambda v, XQ4=XQ4, I4=I4: v.tensor_scalar(
                I4[:], XQ4[:], MAGIC - 63.0 / 128.0, None, ALU.add),
                reads=[kx("XQ4")], writes=[kx("I4")])
            P.op("vector", lambda v, I4=I4: v.tensor_scalar(
                I4[:], I4[:], MAGIC, None, ALU.subtract),
                reads=[kx("I4")], writes=[kx("I4")])
            P.op("gpsimd", lambda g_, XQ4=XQ4, I4=I4, FR=FR: g_.tensor_sub(
                FR[:], XQ4[:], I4[:]),
                reads=[kx("XQ4"), kx("I4")], writes=[kx("FR")])
            P.op("scalar", lambda a_, I4=I4, YLO=YLO: a_.activation(
                YLO[:], I4[:], AF.Gelu, scale=0.25),
                reads=[kx("I4")], writes=[kx("YLO")])
            P.op("scalar", lambda a_, I4=I4, YHI=YHI: a_.activation(
                YHI[:], I4[:], AF.Gelu, bias=b_q, scale=0.25),
                reads=[kx("I4"), k_q], writes=[kx("YHI")])
            P.op("gpsimd", lambda g_, YHI=YHI, YLO=YLO: g_.tensor_sub(
                YHI[:], YHI[:], YLO[:]),
                reads=[kx("YHI"), kx("YLO")], writes=[kx("YHI")])
            P.op("vector", lambda v, YY=YY, FR=FR, YHI=YHI: v.tensor_mul(
                YY[:], FR[:], YHI[:]),
                reads=[kx("FR"), kx("YHI")], writes=[kx("YY")])
            P.op("gpsimd", lambda g_, YY=YY, YLO=YLO: g_.tensor_add(
                YY[:], YY[:], YLO[:]),
                reads=[kx("YY"), kx("YLO")], writes=[kx("YY")])
            P.op("vector", lambda v, YY=YY: v.tensor_scalar(
                YY[:], YY[:], 256.0, MAGIC, ALU.mult, ALU.add),
                reads=[kx("YY")], writes=[kx("YY")])
            P.op("scalar", lambda a_, f=fc, YY=YY: a_.activation(
                g[f][:], YY[:], AF.Copy, bias=-MAGIC * 2.0 ** -8,
                scale=2.0 ** -8),
                reads=[kx("YY")], writes=[("g", fc)])
        for tile in range(TS // 128):
            rr = s * TS + tile * 128
            P.dma("q5", "sync",
                  lambda e, r0=rr: e.dma_start(out=x1t[:],
                                               in_=x1p[r0:r0 + 128, :]),
                  writes=["x1t"])
            for dstrip in range(2):
                k_loop(P, ps_f[:, dstrip * 512:(dstrip + 1) * 512],
                       [(g[fc][:, tile * 128:(tile + 1) * 128],
                         w2s[fc][:, dstrip * 512:(dstrip + 1) * 512])
                        for fc in range(FC0)],
                       reads=[("g", fc) for fc in range(FC0)] + w2keys,
                       writes=[("ps_f", dstrip)])
                P.op("vector", lambda v, ds=dstrip: v.tensor_tensor(
                    outt[:, ds * 512:(ds + 1) * 512],
                    ps_f[:, ds * 512:(ds + 1) * 512],
                    b2bc[:, ds * 512:(ds + 1) * 512], ALU.add),
                    reads=[("ps_f", dstrip), ("b2bc", dstrip)],
                    writes=[("outt", dstrip)])
                P.op("vector", lambda v, ds=dstrip: v.tensor_scalar(
                    outt[:, ds * 512:(ds + 1) * 512],
                    outt[:, ds * 512:(ds + 1) * 512],
                    256.0, MAGIC, ALU.mult, ALU.add),
                    reads=[("outt", dstrip)], writes=[("outt", dstrip)])
                P.op("vector", lambda v, ds=dstrip: v.tensor_scalar(
                    outt[:, ds * 512:(ds + 1) * 512],
                    outt[:, ds * 512:(ds + 1) * 512],
                    MAGIC, 2.0 ** -8, ALU.subtract, ALU.mult),
                    reads=[("outt", dstrip)], writes=[("outt", dstrip)])
                P.op("vector", lambda v, ds=dstrip: v.tensor_tensor(
                    outt[:, ds * 512:(ds + 1) * 512],
                    outt[:, ds * 512:(ds + 1) * 512],
                    x1t[:, ds * 512:(ds + 1) * 512], ALU.add),
                    reads=[("outt", dstrip), "x1t"],
                    writes=[("outt", dstrip)])
            P.dma("qg", "gpsimd",
                  lambda e, r0=rr: e.dma_start(out=x1[r0:r0 + 128, :],
                                               in_=outt[:]),
                  reads=[("outt", 0), ("outt", 1)])
    P.op("gpsimd", lambda g_: g_.engine_nop(), waits=[P.dma_token("qg")])
    return


def emit_ph1(nc, P, ctx, d, pfx="ph1_"):

    xnfm = d["xnfm"]
    tsc = d["tsc"]
    x0 = d["x0"]
    wq_d = d["wq"]
    wk_d = d["wk"]
    wv_d = d["wv"]
    wo_d = d["wo"]
    wgates_d = d["wgates"]
    rmsc_d = d["rmsc"]
    maskIJ_d = d["maskIJ"]
    maskT_d = d["maskT"]
    identf_d = d["identf"]
    erow_d = d["erow"]
    ones_d = d["onesrow"]
    x1p = d["x1p"]
    mrow_hbm = d["mrow_scr"]
    crow_hbm = d["crow_scr"]

    sb = lambda n, s, dt=F32: ctx.enter_context(nc.sbuf_tensor(pfx + n, s, dt))
    psn = lambda n, s, dt=F32: ctx.enter_context(nc.psum_tensor(pfx + n, s, dt))
    wqb = [sb(f"wqb{c}", [128, 512], BF16) for c in range(DC)]
    wkb = [sb(f"wkb{c}", [128, 512], BF16) for c in range(DC)]
    wvb = [sb(f"wvb{c}", [128, 512], BF16) for c in range(DC)]
    wob = [sb(f"wob{p}", [128, D], BF16) for p in range(4)]
    wgb = [sb(f"wgb{c}", [128, H], BF16) for c in range(DC)]
    wld2 = sb("wld2", [128, D])
    rmsc = sb("s_rmsc", [128, DC])
    maskIJ = [sb(f"mIJ{v}", [128, 512], BF16) for v in range(4)]
    maskT = [sb(f"mT{v}", [128, 512], BF16) for v in range(4)]
    identf = sb("s_identf", [128, 128])
    erow = sb("s_erow", [1, 128])
    onesr = sb("onesr", [1, 128])
    xn_sb = sb("xn_sb", [128, N], BF16)
    t_fm = [sb(f"t_fm{c}", [128, N], BF16) for c in range(DC)]
    tscrow = sb("tscrow", [1, N])
    tscb = sb("tscb", [128, N])
    qt = [sb(f"qt{u}", [65, N], BF16) for u in range(2)]
    kt = [sb(f"kt{u}", [65, N], BF16) for u in range(2)]
    v_tm = [sb(f"v_tm{j}", [128, H * 65], BF16) for j in range(16)]
    gates = sb("gates", [128, 16 * H])
    sums = sb("sums", [128, 16 * H])
    expdump_ = [sb(f"expdump{u}", [128, N], BF16) for u in range(2)]
    p_T = [sb(f"p_T{j}", [128, 512], BF16) for j in range(16)]
    outsc = [sb(f"outsc{p}", [128, N], BF16) for p in range(4)]
    mtmp_ = [sb(f"mtmp{u}", [128, 4]) for u in range(2)]
    mcol_ = [sb(f"mcol{u}", [128, 1]) for u in range(2)]
    mb16_ = [sb(f"mb16{u}", [128, 1], BF16) for u in range(2)]
    mcols = sb("mcols", [128, 16])
    negm_ = [sb(f"negm{u}", [128, 1]) for u in range(2)]
    ctmp = sb("ctmp", [128, 16])
    rs = sb("rs", [128, 1])
    mrow = [sb("mrow0", [1, N])] * 2
    crow = [sb(f"crow{u}", [1, N]) for u in range(2)]
    mrows16 = sb("mrows16", [16, 128])
    cb2 = sb("cb2", [128, 512])
    x0t = sb("x0t", [128, D])
    outt = sb("outt", [128, D])
    PS1 = psn("PS1", [128, 2048])
    PS2 = psn("PS2", [128, 2048])

    bank = lambda ps, k: ps[:, k * 512:(k + 1) * 512]

    for c in range(DC):
        for (wd_, dst, key, scale) in ((wq_d, wqb, "wq", 0.125),
                                       (wk_d, wkb, "wk", None),
                                       (wv_d, wvb, "wv", None)):
            P.dma("qs", "sync", lambda e, cc=c, w_=wd_: e.dma_start(
                out=wld2[:, 0:512], in_=w_[cc * 128:(cc + 1) * 128, :]),
                writes=["wld2"])
            if scale is None:
                P.op("vector", lambda v, cc=c, d_=dst: v.tensor_copy(
                    d_[cc][:], wld2[:, 0:512]), reads=["wld2"], writes=[(key, c)])
            else:
                P.op("vector", lambda v, cc=c, d_=dst, sc=scale:
                     v.tensor_scalar(d_[cc][:], wld2[:, 0:512], sc, None, ALU.mult),
                     reads=["wld2"], writes=[(key, c)])
        P.dma("qs", "sync", lambda e, cc=c: e.dma_start(
            out=wld2[:, 0:H], in_=wgates_d[cc * 128:(cc + 1) * 128, :]),
            writes=["wld2"])
        P.op("vector", lambda v, cc=c: v.tensor_copy(wgb[cc][:], wld2[:, 0:H]),
             reads=["wld2"], writes=[("wg", c)])
    for p in range(4):
        P.dma("qs", "sync", lambda e, pp=p: e.dma_start(
            out=wld2[:], in_=wo_d[pp]), writes=["wld2"])
        P.op("vector", lambda v, pp=p: v.tensor_copy(wob[pp][:], wld2[:]),
             reads=["wld2"], writes=[("wo", p)])
    P.dma("qs", "sync", lambda e: e.dma_start(out=rmsc[:], in_=rmsc_d),
          writes=["rmsc"])
    for v_ in range(4):
        P.dma("qs", "sync", lambda e, vv=v_: e.dma_start(
            out=maskIJ[vv][:], in_=maskIJ_d[vv]), writes=[("mIJ", v_)])
        P.dma("qs", "sync", lambda e, vv=v_: e.dma_start(
            out=maskT[vv][:], in_=maskT_d[vv]), writes=[("mT", v_)])
    P.dma("qs", "sync", lambda e: e.dma_start(out=identf[:], in_=identf_d),
          writes=["identf"])
    P.dma("qs", "sync", lambda e: e.dma_start(out=erow[:], in_=erow_d),
          writes=["erow"])
    P.dma("qs", "sync", lambda e: e.dma_start(out=onesr[:], in_=ones_d),
          writes=["onesr"])

    wqk = [("wq", c) for c in range(DC)]
    wkk = [("wk", c) for c in range(DC)]
    wvk = [("wv", c) for c in range(DC)]
    wgk = [("wg", c) for c in range(DC)]
    tfk = [("t", c) for c in range(DC)]

    for s in range(SEQ):
        P.dma("qs", "sync", lambda e, ss=s: e.dma_start(
            out=tscrow[:],
            in_=tsc.rearrange("a b -> (a b)")[None, ss * N:(ss + 1) * N]),
            writes=["tscrow"])
        for k in range(4):
            P.op("tensor", lambda t, kk=k: t.matmul(
                bank(PS2, 3), onesr[:], tscrow[0:1, kk * 512:(kk + 1) * 512],
                start=True, stop=True),
                reads=["tscrow", "onesr"],
                waits=P.peek_war(["B23"]), writes=["B23"])
            P.op("vector", lambda v, kk=k: v.tensor_copy(
                tscb[:, kk * 512:(kk + 1) * 512], bank(PS2, 3)),
                reads=["B23"], writes=[("tscb", k)])
        tscbk = [("tscb", k) for k in range(4)]
        for c in range(DC):
            P.dma("qs", "sync", lambda e, cc=c, ss=s: e.dma_start(
                out=xn_sb[:], in_=xnfm[cc][:, ss * N:(ss + 1) * N]),
                writes=["xn_sb"])
            P.op("vector", lambda v, cc=c: v.scalar_tensor_tensor(
                t_fm[cc][:], xn_sb[:], rmsc[:, cc:cc + 1], tscb[:],
                ALU.mult, ALU.mult),
                reads=["xn_sb", "rmsc"] + tscbk, writes=[("t", c)])
        for jt in range(16):
            k_loop(P, bank(PS2, 3),
                   [(t_fm[c][:, jt * 128:(jt + 1) * 128], wvb[c][:])
                    for c in range(DC)],
                   reads=tfk + wvk, writes=["B23"])
            P.op("vector", lambda v, j=jt: v.tensor_copy(
                v_tm[j][:].rearrange("p (h n) -> p h n", n=65)[:, :, 0:64],
                bank(PS2, 3).rearrange("p (h n) -> p h n", n=64)),
                reads=["B23"], writes=[("v", jt)])
            P.op("vector", lambda v, j=jt: v.memset(
                v_tm[j][:].rearrange("p (h n) -> p h n", n=65)[:, :, 64:65],
                1.0), writes=[("v1", jt)])
            k_loop(P, PS2[:, 1536:1544],
                   [(t_fm[c][:, jt * 128:(jt + 1) * 128], wgb[c][:])
                    for c in range(DC)],
                   reads=tfk + wgk, writes=["B23"])
            P.op("scalar", lambda a, j=jt: a.activation(
                gates[:, j * H:(j + 1) * H], PS2[:, 1536:1544], AF.Sigmoid),
                reads=["B23"], writes=[("gates", jt)])
        P.op("vector", lambda v: v.memset(sums[:], 0.0),
             writes=[("sums", h_, i_) for h_ in range(H) for i_ in range(16)])

        for pair in range(4):
            for hh in range(2):
                h = 2 * pair + hh
                Q, Kt = qt[hh], kt[hh]
                qk_r = [("qt", hh, I) for I in range(4)] + [("qtaug", hh)]
                kk_r = [("kt", hh, I) for I in range(4)] + [("kt1", hh)]
                for I in range(4):
                    k_loop(P, bank(PS2, 2)[0:64, :],
                           [(wqb[c][:, h * 64:(h + 1) * 64],
                             t_fm[c][:, I * 512:(I + 1) * 512])
                            for c in range(DC)],
                           reads=wqk + tfk, writes=["B22"])
                    P.op("vector", lambda v, II=I, Q=Q: v.tensor_copy(
                        Q[0:64, II * 512:(II + 1) * 512],
                        bank(PS2, 2)[0:64, :]),
                        reads=["B22"], writes=[("qt", hh, I)])
                    k_loop(P, bank(PS2, 2)[0:64, :],
                           [(wkb[c][:, h * 64:(h + 1) * 64],
                             t_fm[c][:, I * 512:(I + 1) * 512])
                            for c in range(DC)],
                           reads=wkk + tfk, writes=["B22"])
                    P.op("vector", lambda v, II=I, Kt=Kt: v.tensor_copy(
                        Kt[0:64, II * 512:(II + 1) * 512],
                        bank(PS2, 2)[0:64, :]),
                        reads=["B22"], writes=[("kt", hh, I)])
                P.op("vector", lambda v, Kt=Kt: v.memset(Kt[64:65, :], 1.0),
                     writes=[("kt1", hh)])
                for it in range(16):
                    W4 = it // 4 + 1
                    up = it % 2
                    PSS = PS1 if up == 0 else PS2
                    skey = (lambda j, up=up: ("B1", j) if up == 0
                            else ["B20", "B21", "B22", "B23"][j])
                    mtmp = mtmp_[up]; mcol = mcol_[up]
                    mb16 = mb16_[up]; negm = negm_[up]
                    expdump = expdump_[up]
                    for jb4 in range(W4):
                        P.op("tensor", lambda t, i_=it, j_=jb4, Q=Q, Kt=Kt,
                                    PSS=PSS:
                             t.matmul(bank(PSS, j_),
                                      Q[0:64, i_ * 128:(i_ + 1) * 128],
                                      Kt[0:64, j_ * 512:(j_ + 1) * 512],
                                      start=True, stop=True),
                             reads=qk_r[:4] + kk_r[:4],
                             waits=P.peek_war([skey(jb4)]),
                             writes=[skey(jb4)])
                    P.op("vector", lambda v, i_=it, j_=W4 - 1, PSS=PSS:
                         v.tensor_tensor(
                             bank(PSS, j_), bank(PSS, j_), maskIJ[i_ % 4][:],
                             ALU.add),
                         reads=[skey(W4 - 1), ("mIJ", it % 4)],
                         writes=[skey(W4 - 1)])
                    for jb4 in range(W4):
                        P.op("vector", lambda v, j_=jb4, PSS=PSS, mtmp=mtmp:
                             v.reduce_max(
                                 mtmp[:, j_:j_ + 1], bank(PSS, j_), axis=AX.X),
                             reads=[skey(jb4)], writes=[("mtmp", up, jb4)])
                    P.op("vector", lambda v, W=W4, mtmp=mtmp, mcol=mcol:
                         v.reduce_max(mcol[:], mtmp[:, 0:W], axis=AX.X),
                         reads=[("mtmp", up, j) for j in range(W4)],
                         writes=[("mcol", up)])
                    P.op("vector", lambda v, mb16=mb16, mcol=mcol:
                         v.tensor_copy(mb16[:], mcol[:]),
                         reads=[("mcol", up)], writes=[("mb16", up)])
                    P.op("vector", lambda v, i_=it, mb16=mb16: v.tensor_copy(
                        mcols[:, i_:i_ + 1], mb16[:]),
                        reads=[("mb16", up)], writes=[("mcols", it)])
                    P.op("vector", lambda v, i_=it, negm=negm: v.tensor_scalar(
                        negm[:], mcols[:, i_:i_ + 1], -1.0, None, ALU.mult),
                        reads=[("mcols", it)], writes=[("negm", up)])
                    P.op("scalar", lambda a, i_=it, W=W4, h_=h, PSS=PSS,
                                negm=negm, expdump=expdump: a.activation(
                        expdump[:, 0:W * 512], PSS[:, 0:W * 512], AF.Exp,
                        bias=negm[:], scale=1.0,
                        accum_out=sums[:, h_ * 16 + i_:h_ * 16 + i_ + 1]),
                        reads=[skey(j) for j in range(W4)] + [("negm", up)],
                        writes=[("expdump", up), ("sums", h, it)])
                P.op("tensor", lambda t: t.transpose(
                    PS2[0:16, 1536:1664], mcols[:], identf[:]),
                    reads=[("mcols", i_) for i_ in range(16)] + ["identf"],
                    writes=["B23"])
                P.op("vector", lambda v: v.tensor_copy(mrows16[:], PS2[0:16, 1536:1664]),
                     reads=["B23"], writes=["mrows16"])
                P.dma("q6", "sync", lambda e, u=hh: e.dma_start(
                    out=mrow[u][:], in_=mrows16[:]),
                    reads=["mrows16"], writes=[("mrowsb", hh)])
                for I in range(4):
                    P.op("tensor", lambda t, II=I, u=hh: t.matmul(
                        PS2[0:65, 1536:2048], erow[0:1, 0:65],
                        mrow[u][0:1, II * 512:(II + 1) * 512],
                        start=True, stop=True),
                        reads=[("mrowsb", hh), "erow"],
                        waits=P.peek_war(["B23"]), writes=["B23"])
                    P.op("vector", lambda v, II=I, Q=Q: v.tensor_scalar(
                        Q[64:65, II * 512:(II + 1) * 512],
                        PS2[64:65, 1536:2048], -1.0, None, ALU.mult),
                        reads=["B23"], writes=[("qtaug", hh)])
                for it in range(16):
                    P.op("vector", lambda v, i_=it, h_=h: v.reciprocal(
                        rs[:], sums[:, h_ * 16 + i_:h_ * 16 + i_ + 1]),
                        reads=[("sums", h, it)], writes=["rs"])
                    P.op("vector", lambda v, i_=it, h_=h: v.tensor_scalar(
                        ctmp[:, i_:i_ + 1],
                        gates[:, i_ * H + h_:i_ * H + h_ + 1],
                        rs[:], None, ALU.mult),
                        reads=["rs", ("gates", it)], writes=[("ctmp", it)])
                P.op("tensor", lambda t: t.transpose(
                    PS2[0:16, 1536:1664], ctmp[:], identf[:]),
                    reads=[("ctmp", i_) for i_ in range(16)] + ["identf"],
                    writes=["B23"])
                P.op("vector", lambda v: v.tensor_copy(mrows16[:], PS2[0:16, 1536:1664]),
                     reads=["B23"], writes=["mrows16"])
                P.dma("q6", "sync", lambda e, u=hh: e.dma_start(
                    out=crow[u][:], in_=mrows16[:]),
                    reads=["mrows16"], writes=[("crow", hh)])
            for I in range(4):
                nj = 4 * I + 4
                for hh in range(2):
                    h = 2 * pair + hh
                    Q, Kt = qt[hh], kt[hh]
                    qk_r = [("qt", hh, II) for II in range(4)] + [("qtaug", hh)]
                    kk_r = [("kt", hh, II) for II in range(4)] + [("kt1", hh)]
                    for jb in range(nj):
                        u2_ = jb % 2
                        pbank = bank(PS2, 0) if u2_ == 0 else bank(PS1, 0)
                        pkey = "B20" if u2_ == 0 else ("B1", 0)
                        P.op("tensor", lambda t, j_=jb, II=I, Q=Q, Kt=Kt,
                                    pb=pbank:
                             t.matmul(pb,
                                      Kt[:, j_ * 128:(j_ + 1) * 128],
                                      Q[:, II * 512:(II + 1) * 512],
                                      start=True, stop=True),
                             reads=kk_r + qk_r,
                             waits=P.peek_war([pkey]),
                             writes=[pkey])
                        dpos = jb - 4 * I
                        if dpos >= 0:
                            P.op("vector", lambda v, d_=dpos, pb=pbank:
                                 v.tensor_tensor(pb, pb, maskT[d_][:],
                                                 ALU.add),
                                 reads=[pkey, ("mT", dpos)],
                                 writes=[pkey])
                        P.op("scalar", lambda a, j_=jb, pb=pbank: a.activation(
                            p_T[j_][:], pb, AF.Exp),
                            reads=[pkey], writes=[("pT", jb)])
                    tp = (0, 0) if hh == 0 else (0, 64)
                    lo = 0 if hh == 0 else 64
                    pvw = P.peek_raw([("pT", j) for j in range(nj)]
                                     + [("v", j) for j in range(nj)]
                                     + [("v1", j) for j in range(nj)]) \
                        + P.peek_war(["B21"])
                    for jb in range(nj):
                        P.op("tensor",
                             lambda t, j_=jb, h_=h, tp=tp, lo=lo,
                                    st=(jb == 0), sp=(jb == nj - 1):
                             t.matmul(bank(PS2, 1)[lo:lo + 64, :],
                                      v_tm[j_][:, h_ * 65:h_ * 65 + 64],
                                      p_T[j_][:], start=st, stop=sp,
                                      tile_position=tp),
                             reads=([("pT", j) for j in range(nj)]
                                    if jb == nj - 1 else ()),
                             writes=(["B21"] if jb == nj - 1 else ()),
                             waits=(pvw if jb == 0 else ()))
                    P.op("tensor", lambda t, II=I, tp=tp, lo=lo, u=hh:
                         t.matmul(PS2[lo:lo + 64, 1024:1536],
                                  onesr[0:1, 0:64],
                                  crow[u][0:1, II * 512:(II + 1) * 512],
                                  start=True, stop=True, tile_position=tp),
                         reads=[("crow", hh), "onesr"],
                         waits=P.peek_war(["B22"]),
                         writes=["B22"])
                P.op("vector", lambda v: v.tensor_copy(cb2[:], bank(PS2, 2)),
                     reads=["B22"], writes=["cb2"])
                P.op("vector", lambda v, II=I, p_=pair: v.tensor_tensor(
                    outsc[p_][:, II * 512:(II + 1) * 512], bank(PS2, 1),
                    cb2[:], ALU.mult),
                    reads=["cb2", ("PSpv", 0), ("PSpv", 1)],
                    writes=[("outsc", pair, I)])
        osk = [("outsc", p_, I) for p_ in range(4) for I in range(4)]
        for tile in range(16):
            rr = s * N + tile * 128
            P.dma("q5", "sync", lambda e, r0=rr: e.dma_start(
                out=x0t[:], in_=x0[r0:r0 + 128, :]), writes=["x0t"])
            for dstrip in range(2):
                k_loop(P, bank(PS1, dstrip),
                       [(outsc[p_][:, tile * 128:(tile + 1) * 128],
                         wob[p_][:, dstrip * 512:(dstrip + 1) * 512])
                        for p_ in range(4)],
                       reads=osk + [("wo", p_) for p_ in range(4)],
                       writes=[("B1", dstrip)])
                P.op("vector", lambda v, ds=dstrip: v.tensor_tensor(
                    outt[:, ds * 512:(ds + 1) * 512], bank(PS1, ds),
                    x0t[:, ds * 512:(ds + 1) * 512], ALU.add),
                    reads=[("B1", dstrip), "x0t"], writes=[("outt", dstrip)])
            P.dma("qg", "gpsimd", lambda e, r0=rr: e.dma_start(
                out=x1p[r0:r0 + 128, :], in_=outt[:]),
                reads=[("outt", 0), ("outt", 1)])
    P.op("gpsimd", lambda g_: g_.engine_nop(), waits=[P.dma_token("qg")])
    return


def emit_ph3(nc, P, ctx, d, pfx="ph3_"):
    x1 = d["x1"]; lngc_d = d["lngc"]; lnbc_d = d["lnbc"]
    identb_d = d["identb"]; identf_d = d["identf"]
    xn1 = d["xn1"]; normfm = d["normfm"]; x2p = d["x2p"]; lm1 = d["lm1"]
    sb = lambda n, s, dt=F32: ctx.enter_context(nc.sbuf_tensor(pfx + n, s, dt))
    psn = lambda n, s, dt=F32: ctx.enter_context(nc.psum_tensor(pfx + n, s, dt))
    xt = [sb(f"xt{i}", [128, D]) for i in range(2)]
    sq_ = [sb(f"sq{u}", [128, D]) for u in range(2)]
    mcol_ = [sb(f"mcol{u}", [128, 1]) for u in range(2)]
    vcol_ = [sb(f"vcol{u}", [128, 1]) for u in range(2)]
    rstd_ = [sb(f"rstd{u}", [128, 1]) for u in range(2)]
    xnt_ = [sb(f"xnt{u}", [128, D]) for u in range(2)]
    u2_ = [sb(f"u2{u}", [128, D]) for u in range(2)]
    ub_ = [sb(f"ub{u}", [128, D], BF16) for u in range(2)]
    x2t_ = [sb(f"x2t{u}", [128, D]) for u in range(2)]
    tmax_ = [sb(f"tmax{u}", [128, 1]) for u in range(2)]
    idb = sb("idb", [128, 128], BF16)
    idf = sb("idf", [128, 128], F32)
    lngc = sb("s_lngc", [128, DC]); lnbc = sb("s_lnbc", [128, DC])
    fm = [sb(f"fm{c}", [128, R], BF16) for c in range(DC)]
    runmax = sb("runmax", [128, 1])
    mrow_s = sb("mrow_s", [1, 128]); mone = sb("mone", [1, 1])
    pstr_ = [psn(f"pstr{u}", [128, 128], BF16) for u in range(2)]
    ps_r = psn("ps_r", [1, 128], F32)
    K = ConstBank(P, nc, ctx, pfx=pfx)
    b_ln = K.get(EPS_LN); k_ln = K.key(EPS_LN)

    P.dma("qs", "sync", lambda e: e.dma_start(out=idb[:], in_=identb_d),
          writes=["idb"])
    P.dma("qs", "sync", lambda e: e.dma_start(out=idf[:], in_=identf_d),
          writes=["idf"])
    P.dma("qs", "sync", lambda e: e.dma_start(out=lngc[:], in_=lngc_d),
          writes=["lngc"])
    P.dma("qs", "sync", lambda e: e.dma_start(out=lnbc[:], in_=lnbc_d),
          writes=["lnbc"])
    P.op("vector", lambda v: v.memset(runmax[:], 0.0), writes=["runmax"])

    def ln_pass(s, src_key, srcb, dst, dst_key):
        sq = sq_[s]; mcol = mcol_[s]; vcol = vcol_[s]; rstd = rstd_[s]
        _k = lambda nm: (nm, s)
        P.op("vector", lambda v: v.reduce_sum(mcol[:], srcb[:], axis=AX.X),
             reads=[src_key], writes=[_k("mcol")])
        P.op("vector", lambda v: v.tensor_scalar(
            mcol[:], mcol[:], 1.0 / D, None, ALU.mult),
            reads=[_k("mcol")], writes=[_k("mcol")])
        P.op("vector", lambda v: v.tensor_scalar(
            dst[:], srcb[:], mcol[:], None, ALU.subtract),
            reads=[src_key, _k("mcol")], writes=[dst_key])
        P.op("scalar", lambda a: a.activation(sq[:], dst[:], AF.Square),
             reads=[dst_key], writes=[_k("sq")])
        P.op("vector", lambda v: v.reduce_sum(vcol[:], sq[:], axis=AX.X),
             reads=[_k("sq")], writes=[_k("vcol")])
        P.op("scalar", lambda a: a.activation(
            rstd[:], vcol[:], AF.Sqrt, bias=b_ln, scale=1.0 / D),
            reads=[_k("vcol"), k_ln], writes=[_k("rstd")])
        P.op("vector", lambda v: v.reciprocal(rstd[:], rstd[:]),
             reads=[_k("rstd")], writes=[_k("rstd")])
        P.op("vector", lambda v: v.tensor_scalar(
            dst[:], dst[:], rstd[:], None, ALU.mult),
            reads=[dst_key, _k("rstd")], writes=[dst_key])

    def tile(r, s):
        b = xt[s]; xnt = xnt_[s]; u2 = u2_[s]; ub = ub_[s]
        x2t = x2t_[s]; tmax = tmax_[s]; pstr = pstr_[s]
        _k = lambda nm: (nm, s)
        P.dma("qs" if s == 0 else "qa", "sync",
              lambda e: e.dma_start(out=b[:], in_=x1[r * 128:(r + 1) * 128, :]),
              writes=[f"xt{s}"])
        ln_pass(s, f"xt{s}", b, xnt, _k("xnt"))
        P.dma("qg", "gpsimd", lambda e: e.dma_start(
            out=xn1[r * 128:(r + 1) * 128, :], in_=xnt[:]), reads=[_k("xnt")])
        P.op("gpsimd", lambda g_: g_.tensor_add(
            x2t[:], b[:], xnt[:]),
            reads=[f"xt{s}", _k("xnt")], writes=[_k("x2t")])
        P.dma("qg", "gpsimd", lambda e: e.dma_start(
            out=x2p[r * 128:(r + 1) * 128, :], in_=x2t[:]), reads=[_k("x2t")])
        P.op("vector", lambda v: v.reduce_max(
            tmax[:], xnt[:], axis=AX.X, apply_absolute_value=True),
            reads=[_k("xnt")], writes=[_k("tmax")])
        P.op("vector", lambda v: v.tensor_tensor(
            runmax[:], runmax[:], tmax[:], ALU.max),
            reads=["runmax", _k("tmax")], writes=["runmax"])
        ln_pass(s, _k("xnt"), xnt, u2, _k("u2"))
        P.op("scalar", lambda a: a.copy(ub[:], u2[:]),
             reads=[_k("u2")], writes=[_k("ub")])
        for c in range(DC):
            P.op("tensor", lambda t, cc=c: t.transpose(
                pstr[:], ub[:, cc * 128:(cc + 1) * 128], idb[:]),
                reads=[_k("ub"), "idb"], writes=[_k("pstr")])
            P.op("vector", lambda v, cc=c: v.tensor_scalar(
                fm[cc][:, r * 128:(r + 1) * 128], pstr[:],
                lngc[:, cc:cc + 1], lnbc[:, cc:cc + 1], ALU.mult, ALU.add),
                reads=[_k("pstr"), "lngc", "lnbc"], writes=[("fm", c, r)])

    for r in range(R // 128):
        tile(r, r % 2)
    P.op("tensor", lambda t: t.matmul(
        ps_r[:], runmax[:, 0:1], idf[:], start=True, stop=True),
        reads=["runmax", "idf"], writes=["ps_r"])
    P.op("vector", lambda v: v.tensor_copy(mrow_s[:], ps_r[:]),
         reads=["ps_r"], writes=["mrow_s"])
    P.op("vector", lambda v: v.reduce_max(mone[:], mrow_s[:], axis=AX.X),
         reads=["mrow_s"], writes=["mone"])
    P.dma("qg", "gpsimd", lambda e: e.dma_start(out=lm1, in_=mone[:]),
          reads=["mone"], writes=[("lm1_hbm",)])
    for c in range(DC):
        P.dma("qg", "gpsimd",
              lambda e, cc=c: e.dma_start(out=normfm[cc], in_=fm[cc][:]),
              reads=[("fm", c, r) for r in range(R // 128)])
    P.op("gpsimd", lambda g_: g_.engine_nop(), waits=[P.dma_token("qg")])
    return

def emit_ph4(nc, P, ctx, d, pfx="ph4_"):
    normfm = d["normfm"]; Whb = d["Whb"]; Wqk_d = d["Wqk"]
    bhv_row = d["bhv"]; bhg_c = d["bhgc"]; bqk_c = d["bqkc"]
    gb_d = d["gb"]; Wob_d = d["Wob"]; bo_row = d["bor"]
    x2p = d["x2p"]; maskT_d = d["maskT"]; ones_d = d["onesrow"]
    x2q = d["x2q"]; g_scr = d["g_scr"]; gd_scr = d["gd_scr"]

    sb = lambda n, s, dt=F32: ctx.enter_context(nc.sbuf_tensor(pfx + n, s, dt))
    psn = lambda n, s, dt=F32: ctx.enter_context(nc.psum_tensor(pfx + n, s, dt))
    nrm = [sb(f"nrm{c}", [128, N], BF16) for c in range(DC)]
    whs = [sb(f"whs{c}", [128, 512], BF16) for c in range(DC)]
    wgate_ = [sb(f"wgate{u}", [128, DC * 128], BF16) for u in range(2)]
    wqkb = [sb(f"wqkb{c}", [128, QK], BF16) for c in range(DC)]
    wob = [sb(f"wob{c}", [128, D], BF16) for c in range(HC)]
    qv = sb("qv", [128, N], BF16)
    kv = sb("kv", [128, N], BF16)
    gbc = sb("s_gb", [128, 4])
    bqkc = sb("s_bqk", [128, 1])
    bhgc = sb("s_bhg", [128, HC])
    bhvb = sb("bhvb", [128, HID])
    bob = sb("bob", [128, D])
    onesr = sb("onesr", [1, 128])
    maskT = [sb(f"mT{v}", [128, 512], BF16) for v in range(4)]
    v_tm = [sb(f"v_tm{j}", [128, HID], BF16) for j in range(16)]
    p_T = [sb(f"p_T{j}", [128, 512], BF16) for j in range(16)]
    rsb_ = [sb(f"rsb{u}", [128, 512]) for u in range(2)]
    gsl_ = [sb(f"gsl{u}", [128, 512], BF16) for u in range(2)]
    gout_ = [sb(f"gout{u}", [128, 512], BF16) for u in range(2)]
    gtile_ = [sb(f"gtile{u}", [128, HC * 128], BF16) for u in range(2)]
    x2t = sb("x2t", [128, D])
    outt = sb("outt", [128, D])
    PS1 = psn("PS1", [128, 2048])
    PS2 = psn("PS2", [128, 2048])
    bank = lambda ps, k: ps[:, k * 512:(k + 1) * 512]

    P.dma("qs", "sync", lambda e: e.dma_start(out=gbc[:], in_=gb_d),
          writes=["gbc"])
    P.dma("qs", "sync", lambda e: e.dma_start(out=bqkc[:], in_=bqk_c),
          writes=["bqkc"])
    P.dma("qs", "sync", lambda e: e.dma_start(out=bhgc[:], in_=bhg_c),
          writes=["bhgc"])
    P.dma("qs", "sync", lambda e: e.dma_start(out=onesr[:], in_=ones_d),
          writes=["onesr"])
    for v_ in range(4):
        P.dma("qs", "sync", lambda e, vv=v_: e.dma_start(
            out=maskT[vv][:], in_=maskT_d[vv]), writes=[("mT", v_)])
    for c in range(DC):
        P.dma("qs", "sync", lambda e, cc=c: e.dma_start(
            out=x2t[:, 0:QK], in_=Wqk_d[cc * 128:(cc + 1) * 128, :]),
            writes=["x2t"])
        P.op("vector", lambda v, cc=c: v.tensor_copy(
            wqkb[cc][:], x2t[:, 0:QK]), reads=["x2t"], writes=[("wqk", c)])
    for c in range(HC):
        P.dma("qs", "sync", lambda e, cc=c: e.dma_start(
            out=wob[cc][:], in_=Wob_d[cc * 128:(cc + 1) * 128, :]),
            writes=[("wob", c)])
    P.dma("qs", "sync", lambda e: e.dma_start(out=bhvb[0:1, :], in_=bhv_row),
          writes=["bhv_r"])
    P.dma("qs", "sync", lambda e: e.dma_start(out=bob[0:1, :], in_=bo_row),
          writes=["bo_r"])
    for k in range(4):
        P.op("tensor", lambda t, kk=k: t.matmul(
            bank(PS2, 2), onesr[:], bhvb[0:1, kk * 512:(kk + 1) * 512],
            start=True, stop=True),
            reads=["bhv_r", "onesr"], waits=P.peek_war(["B22"]),
            writes=["B22"])
        P.op("vector", lambda v, kk=k: v.tensor_copy(
            bhvb[:, kk * 512:(kk + 1) * 512], bank(PS2, 2)),
            reads=["B22"], writes=[("bhvb", k)])
    for k in range(2):
        P.op("tensor", lambda t, kk=k: t.matmul(
            bank(PS2, 2), onesr[:], bob[0:1, kk * 512:(kk + 1) * 512],
            start=True, stop=True),
            reads=["bo_r", "onesr"], waits=P.peek_war(["B22"]),
            writes=["B22"])
        P.op("vector", lambda v, kk=k: v.tensor_copy(
            bob[:, kk * 512:(kk + 1) * 512], bank(PS2, 2)),
            reads=["B22"], writes=[("bob", k)])

    nk = [("nrm", c) for c in range(DC)]
    for s in range(SEQ):
        for c in range(DC):
            P.dma("qs", "sync", lambda e, cc=c, ss=s: e.dma_start(
                out=nrm[cc][:], in_=normfm[cc][:, ss * N:(ss + 1) * N]),
                writes=[("nrm", c)])
        for I in range(4):
            k_loop(P, bank(PS1, 2),
                   [(wqkb[c][:], nrm[c][:, I * 512:(I + 1) * 512])
                    for c in range(DC)],
                   reads=[("wqk", c) for c in range(DC)] + nk,
                   writes=[("B12", 0)])
            P.op("scalar", lambda a, II=I: a.activation(
                qv[:, II * 512:(II + 1) * 512], bank(PS1, 2), AF.Silu,
                bias=bqkc[:]),
                reads=[("B12", 0), "bqkc"], writes=[("qks", I)])
        qkk = [("qks", I) for I in range(4)]
        P.op("vector", lambda v: v.tensor_scalar(
            kv[:], qv[:], gbc[:, 1:2], gbc[:, 3:4], ALU.mult, ALU.add),
            reads=qkk + ["gbc"], writes=["kv"])
        P.op("vector", lambda v: v.tensor_scalar(
            qv[:], qv[:], gbc[:, 0:1], gbc[:, 2:3], ALU.mult, ALU.add),
            reads=qkk + ["gbc", "kv"], writes=["qv"] + qkk)
        for vs in range(4):
            P.dma("qa", "sync", lambda e, v0=vs: e.dma_start(
                out=whs[0][:].tensor.full_ap() if False else whs[0][:],
                in_=Whb[0:128, v0 * 512:(v0 + 1) * 512]),
                writes=[("whs", 0)])
            for c in range(1, DC):
                P.dma("qa", "sync", lambda e, cc=c, v0=vs: e.dma_start(
                    out=whs[cc][:],
                    in_=Whb[cc * 128:(cc + 1) * 128, v0 * 512:(v0 + 1) * 512]),
                    writes=[("whs", c)])
            for jt in range(16):
                u = jt % 2
                u4 = jt % 4
                pb = bank(PS1, 2 + u) if u4 < 2 else bank(PS2, u)
                pk = ("B12", u) if u4 < 2 else ("B20", u)
                k_loop(P, pb,
                       [(nrm[c][:, jt * 128:(jt + 1) * 128], whs[c][:])
                        for c in range(DC)],
                       reads=nk + [("whs", c) for c in range(DC)],
                       writes=[pk])
                P.op("vector", lambda v, v0=vs, u=u, pb=pb, pk=pk:
                     v.tensor_tensor(
                         rsb_[u][:], pb, bhvb[:, v0 * 512:(v0 + 1) * 512],
                         ALU.add),
                     reads=[pk, ("bhvb", vs)], writes=[("rsb", u)])
                P.op("scalar", lambda a, j=jt, v0=vs, u=u: a.activation(
                    v_tm[j][:, v0 * 512:(v0 + 1) * 512], rsb_[u][:], AF.Silu),
                    reads=[("rsb", u)], writes=[("v", jt, vs)])
        for hc in range(HC):
            ug = hc % 2
            wg_t = wgate_[ug]
            P.dma("qa", "sync", lambda e, h0=hc, w_=wg_t: e.dma_start(
                out=w_[:].rearrange("p (c n) -> p c n", c=DC),
                in_=Whb[:, HID + h0 * 128:HID + (h0 + 1) * 128].rearrange(
                    "(c p) n -> p c n", p=128)),
                writes=[("wgate", ug)])
            for I in range(4):
                u = I % 2
                pb = bank(PS1, 2 + u) if I < 2 else bank(PS2, u)
                pk = ("B12", u) if I < 2 else ("B20", u)
                k_loop(P, pb,
                       [(wg_t[:, c * 128:(c + 1) * 128],
                         nrm[c][:, I * 512:(I + 1) * 512])
                        for c in range(DC)],
                       reads=nk + [("wgate", ug)],
                       writes=[pk])
                P.op("scalar", lambda a, h0=hc, u=u, pb=pb, pk=pk: a.activation(
                    gsl_[u][:], pb, AF.Silu, bias=bhgc[:, h0:h0 + 1]),
                    reads=[pk, "bhgc"], writes=[("gsl", u)])
                P.dma("qg", "gpsimd", lambda e, h0=hc, II=I, u=u: e.dma_start(
                    out=g_scr[h0][:, II * 512:(II + 1) * 512], in_=gsl_[u][:]),
                    reads=[("gsl", u)], writes=[("gscr", hc, I)])
        for I in range(4):
            nj = 4 * I + 4
            for jb in range(nj):
                u = jb % 2
                pb = bank(PS2, u)
                P.op("tensor", lambda t, j_=jb, II=I, pb=pb: t.matmul(
                    pb, kv[:, j_ * 128:(j_ + 1) * 128],
                    qv[:, II * 512:(II + 1) * 512], start=True, stop=True),
                    reads=["kv", "qv"], waits=P.peek_war([("B20", u)]),
                    writes=[("B20", u)])
                dpos = jb - 4 * I
                if dpos >= 0:
                    P.op("vector", lambda v, d_=dpos, pb=pb: v.tensor_tensor(
                        pb, pb, maskT[d_][:], ALU.add),
                        reads=[("B20", u), ("mT", dpos)], writes=[("B20", u)])
                P.op("scalar", lambda a, u=u, pb=pb: a.activation(
                    rsb_[u][:], pb, AF.Relu),
                    reads=[("B20", u)], writes=[("rsb", u)])
                P.op("scalar", lambda a, j_=jb, u=u: a.activation(
                    p_T[j_][:], rsb_[u][:], AF.Square),
                    reads=[("rsb", u)], writes=[("pT", jb)])
            vtk = [("v", jb, vs) for jb in range(nj) for vs in range(4)]
            ptk = [("pT", jb) for jb in range(nj)]
            for hc in range(HC):
                u = hc % 2
                pb = bank(PS1, u)
                k_loop(P, pb,
                       [(v_tm[jb][:, hc * 128:(hc + 1) * 128], p_T[jb][:])
                        for jb in range(nj)],
                       reads=vtk + ptk, writes=[("B1", u)])
                P.dma("qa", "sync", lambda e, h0=hc, II=I, u=u: e.dma_start(
                    out=gsl_[u][:], in_=g_scr[h0][:, II * 512:(II + 1) * 512]),
                    reads=[("gscr", hc, I)], writes=[("gsl", u)])
                P.op("vector", lambda v, u=u, pb=pb: v.scalar_tensor_tensor(
                    gout_[u][:], pb, 1.0 / (2048.0 * 2048.0),
                    gsl_[u][:], ALU.mult, ALU.mult),
                    reads=[("B1", u), ("gsl", u)], writes=[("gout", u)])
                P.dma("qg", "gpsimd", lambda e, h0=hc, II=I, u=u: e.dma_start(
                    out=gd_scr[h0][:, II * 512:(II + 1) * 512],
                    in_=gout_[u][:]), reads=[("gout", u)],
                    writes=[("gd", hc, I)])
        for tile in range(16):
            rr = s * N + tile * 128
            ut = tile % 2
            gt_t = gtile_[ut]
            P.dma("q5", "sync", lambda e, t0=tile, g_=gt_t: e.dma_start(
                out=g_[:].rearrange("p (h n) -> p h n", h=HC),
                in_=gd_scr.rearrange("h p n -> p h n")[
                    :, :, t0 * 128:(t0 + 1) * 128]),
                reads=[("gd", hc, tile // 4) for hc in range(HC)],
                writes=[("gt", ut)])
            P.dma("q6", "gpsimd", lambda e, r0=rr: e.dma_start(
                out=x2t[:], in_=x2p[r0:r0 + 128, :]), writes=["x2t"])
            for ds in range(2):
                k_loop(P, bank(PS2, 2 + ds),
                       [(gt_t[:, hc * 128:(hc + 1) * 128],
                         wob[hc][:, ds * 512:(ds + 1) * 512])
                        for hc in range(HC)],
                       reads=[("gt", ut)]
                       + [("wob", hc) for hc in range(HC)],
                       writes=["B22" if ds == 0 else "B23"])
                P.op("vector", lambda v, d_=ds: v.tensor_tensor(
                    outt[:, d_ * 512:(d_ + 1) * 512], bank(PS2, 2 + d_),
                    bob[:, d_ * 512:(d_ + 1) * 512], ALU.add),
                    reads=[("B22" if ds == 0 else "B23"), ("bob", ds)],
                    writes=[("outt", ds)])
                P.op("vector", lambda v, d_=ds: v.tensor_tensor(
                    outt[:, d_ * 512:(d_ + 1) * 512],
                    outt[:, d_ * 512:(d_ + 1) * 512],
                    x2t[:, d_ * 512:(d_ + 1) * 512], ALU.add),
                    reads=[("outt", ds), "x2t"], writes=[("outt", ds)])
            P.dma("qg", "gpsimd", lambda e, r0=rr: e.dma_start(
                out=x2q[r0:r0 + 128, :], in_=outt[:]),
                reads=[("outt", 0), ("outt", 1)])
    P.op("gpsimd", lambda g_: g_.engine_nop(), waits=[P.dma_token("qg")])
    return

def P_vscr_keys(s, j, hg):
    return [("vscr", j, vs) for vs in range(4)]


def _emit_scalar_prep(P, nc, ctx, maxes, clip, onesr_key, onesr, PSb, psb_key,
                      pfx=""):
    sb = lambda n, s_, dt=F32: ctx.enter_context(
        nc.sbuf_tensor(pfx + n, s_, dt))
    mx8 = sb("mx8", [1, 8])
    clp = sb("clp", [1, 1])
    am = sb("am", [1, 1])
    sg = sb("sg", [1, 1])
    sv = sb("sv", [1, 1])
    rv = sb("rv", [1, 1])
    scol = sb("scol_b", [128, 1])
    rcol = sb("rcol_b", [128, 1])
    P.dma("qs", "sync", lambda e: e.dma_start(out=mx8[:], in_=maxes),
          writes=["mx8"])
    P.dma("qs", "sync", lambda e: e.dma_start(out=clp[:], in_=clip),
          writes=["clp"])
    P.op("vector", lambda v: v.reduce_max(am[:], mx8[:], axis=AX.X),
         reads=["mx8"], writes=["am"])
    P.op("scalar", lambda a: a.activation(sg[:], clp[:], AF.Sigmoid),
         reads=["clp"], writes=["sg"])
    P.op("vector", lambda v: v.tensor_tensor(sv[:], sg[:], am[:], ALU.mult),
         reads=["sg", "am"], writes=["sv"])
    P.op("vector", lambda v: v.tensor_scalar(
        sv[:], sv[:], 1.0 / QMAX, None, ALU.mult), reads=["sv"], writes=["sv"])
    P.op("vector", lambda v: v.reciprocal(rv[:], sv[:]),
         reads=["sv"], writes=["rv"])
    for (src, dst, kk) in ((sv, scol, "scol_b"), (rv, rcol, "rcol_b")):
        P.op("tensor", lambda t, s_=src: t.matmul(
            PSb[:, 0:1], onesr[:], s_[:], start=True, stop=True),
            reads=[("sv" if src is sv else "rv"), onesr_key],
            waits=P.peek_war([psb_key]), writes=[psb_key])
        P.op("vector", lambda v, d_=dst: v.tensor_copy(d_[:], PSb[:, 0:1]),
             reads=[psb_key], writes=[kk])
    return scol, rcol


def emit_ph5a(nc, P, ctx, d, pfx="ph5a_"):

    xn1 = d["xn1"]
    maxes = d["maxes"]
    clipx = d["clipx"]
    wgTi = d["wgTi"]
    wuTi = d["wuTi"]
    sgc_d = d["sgc"]
    suc_d = d["suc"]
    identb_d = d["identb"]
    identf_d = d["identf"]
    ones_d = d["onesrow"]
    hmid = d["hmid"]
    lm2 = d["lm2"]
    nfm_scr = d["nfm_scr"]

    sb = lambda n, s, dt=F32: ctx.enter_context(nc.sbuf_tensor(pfx + n, s, dt))
    psn = lambda n, s, dt=F32: ctx.enter_context(nc.psum_tensor(pfx + n, s, dt))
    PS2 = psn("PS2", [128, 2048])
    bank = lambda ps, k: ps[:, k * 512:(k + 1) * 512]
    onesr = sb("onesr", [1, 128])
    P.dma("qs", "sync", lambda e: e.dma_start(out=onesr[:], in_=ones_d),
          writes=["onesr"])
    scol, rcol = _emit_scalar_prep(P, nc, ctx, maxes, clipx, "onesr",
                                   onesr, PS2[:, 1536:2048], "B23", pfx)
    idb = sb("idb", [128, 128], BF16)
    idf = sb("idf", [128, 128], F32)
    P.dma("qs", "sync", lambda e: e.dma_start(out=idb[:], in_=identb_d),
          writes=["idb"])
    P.dma("qs", "sync", lambda e: e.dma_start(out=idf[:], in_=identf_d),
          writes=["idf"])
    sgc = sb("s_sgc", [128, IC1]); suc = sb("s_suc", [128, IC1])
    P.dma("qs", "sync", lambda e: e.dma_start(out=sgc[:], in_=sgc_d),
          writes=["sgc0"])
    P.dma("qs", "sync", lambda e: e.dma_start(out=suc[:], in_=suc_d),
          writes=["suc0"])
    P.op("vector", lambda v: v.tensor_scalar(
        sgc[:], sgc[:], scol[:], None, ALU.mult),
        reads=["sgc0", "scol_b"], writes=["sgc"])
    P.op("vector", lambda v: v.tensor_scalar(
        suc[:], suc[:], scol[:], None, ALU.mult),
        reads=["suc0", "scol_b"], writes=["suc"])

    xt = sb("xt", [128, D])
    vq = sb("vq", [128, D])
    vqb = sb("vqb", [128, D], BF16)
    ntile = sb("ntile", [128, 128], BF16)
    pstr = psn("pstr", [128, 128], BF16)
    for r in range(R // 128):
        P.dma("q5", "sync", lambda e, r0=r * 128: e.dma_start(
            out=xt[:], in_=xn1[r0:r0 + 128, :]), writes=["xt"])
        P.op("vector", lambda v: v.tensor_scalar(
            vq[:], xt[:], rcol[:], None, ALU.mult),
            reads=["xt", "rcol_b"], writes=["vq"])
        P.op("vector", lambda v: v.tensor_scalar(
            vq[:], vq[:], MAGIC, None, ALU.add), reads=["vq"], writes=["vq"])
        P.op("vector", lambda v: v.tensor_scalar(
            vq[:], vq[:], MAGIC, None, ALU.subtract),
            reads=["vq"], writes=["vq"])
        P.op("vector", lambda v: v.tensor_scalar(
            vq[:], vq[:], -QMAX - 1.0, QMAX, ALU.max, ALU.min),
            reads=["vq"], writes=["vq"])
        P.op("vector", lambda v: v.tensor_copy(vqb[:], vq[:]),
             reads=["vq"], writes=["vqb"])
        for c in range(DC):
            P.op("tensor", lambda t, cc=c: t.transpose(
                pstr[:], vqb[:, cc * 128:(cc + 1) * 128], idb[:]),
                reads=["vqb", "idb"], writes=["pstr"])
            P.op("vector", lambda v: v.tensor_copy(ntile[:], pstr[:]),
                 reads=["pstr"], writes=["ntile"])
            P.dma("q6", "gpsimd", lambda e, cc=c, r0=r * 128: e.dma_start(
                out=nfm_scr[cc][:, r0:r0 + 128], in_=ntile[:]),
                reads=["ntile"], writes=[("nfm", c, r)])
    nst = [sb(f"nst{c}", [128, 512], BF16) for c in range(DC)]
    wgs = [sb(f"wgs{c}", [128, I1], BF16) for c in range(DC)]
    wus = [sb(f"wus{c}", [128, I1], BF16) for c in range(DC)]
    sil_ = [sb(f"sil{u}", [128, 512]) for u in range(2)]
    upv_ = [sb(f"upv{u}", [128, 512]) for u in range(2)]
    hm_ = [sb(f"hm{u}", [128, 512]) for u in range(2)]
    runmax = sb("runmax", [128, 1])
    tmax_ = [sb(f"tmax{u}", [128, 1]) for u in range(2)]
    mrow_s = sb("mrow_s", [1, 128]); mone = sb("mone", [1, 1])
    ps_r = psn("ps_r", [1, 128], F32)
    P.op("vector", lambda v: v.memset(runmax[:], 0.0), writes=["runmax"])
    for c in range(DC):
        P.dma("qa", "sync", lambda e, cc=c: e.dma_start(
            out=wgs[cc][:], in_=wgTi[cc * 128:(cc + 1) * 128, :]),
            writes=[("wgs", c)])
        P.dma("qa", "sync", lambda e, cc=c: e.dma_start(
            out=wus[cc][:], in_=wuTi[cc * 128:(cc + 1) * 128, :]),
            writes=[("wus", c)])
    for st in range(8):
        for c in range(DC):
            P.dma("qs", "sync", lambda e, cc=c, s0=st * 512: e.dma_start(
                out=nst[cc][:], in_=nfm_scr[cc][:, s0:s0 + 512]),
                reads=[("nfm", c, rr) for rr in range(st * 4, st * 4 + 4)],
                writes=[("nst", c)])
        for ic in range(IC1):
            u = ic % 2
            sil = sil_[u]; upv = upv_[u]; hm = hm_[u]
            psg = bank(PS2, u)
            psu = bank(PS2, 2 + u)
            k_loop(P, psg,
                   [(wgs[c][:, ic * 128:(ic + 1) * 128], nst[c][:])
                    for c in range(DC)],
                   reads=[("wgs", c) for c in range(DC)]
                   + [("nst", c) for c in range(DC)], writes=[("B2", u)])
            k_loop(P, psu,
                   [(wus[c][:, ic * 128:(ic + 1) * 128], nst[c][:])
                    for c in range(DC)],
                   reads=[("wus", c) for c in range(DC)]
                   + [("nst", c) for c in range(DC)], writes=[("B2", 2 + u)])
            P.op("scalar", lambda a_, i0=ic, sil=sil, psg=psg: a_.activation(
                sil[:], psg, AF.Silu, scale=sgc[:, i0:i0 + 1]),
                reads=[("B2", u), "sgc"], writes=[("sil", u)])
            P.op("vector", lambda v, i0=ic, upv=upv, psu=psu: v.tensor_scalar(
                upv[:], psu, suc[:, i0:i0 + 1], None, ALU.mult),
                reads=[("B2", 2 + u), "suc"], writes=[("upv", u)])
            P.op("vector", lambda v, hm=hm, sil=sil, upv=upv: v.tensor_tensor(
                hm[:], sil[:], upv[:], ALU.mult),
                reads=[("sil", u), ("upv", u)], writes=[("hm", u)])
            P.op("vector", lambda v, hm=hm, u=u: v.reduce_max(
                tmax_[u][:], hm[:], axis=AX.X, apply_absolute_value=True),
                reads=[("hm", u)], writes=[("tmax", u)])
            P.op("vector", lambda v, u=u: v.tensor_tensor(
                runmax[:], runmax[:], tmax_[u][:], ALU.max),
                reads=["runmax", ("tmax", u)], writes=["runmax"])
            P.dma("qg", "gpsimd", lambda e, i0=ic, s0=st * 512, hm=hm:
                  e.dma_start(out=hmid[i0][:, s0:s0 + 512], in_=hm[:]),
                  reads=[("hm", u)])
    P.op("tensor", lambda t: t.matmul(
        ps_r[:], runmax[:, 0:1], idf[:], start=True, stop=True),
        reads=["runmax", "idf"], writes=["ps_r"])
    P.op("vector", lambda v: v.tensor_copy(mrow_s[:], ps_r[:]),
         reads=["ps_r"], writes=["mrow_s"])
    P.op("vector", lambda v: v.reduce_max(mone[:], mrow_s[:], axis=AX.X),
         reads=["mrow_s"], writes=["mone"])
    P.dma("qg", "gpsimd", lambda e: e.dma_start(out=lm2, in_=mone[:]),
          reads=["mone"], writes=[("lm2_hbm",)])
    P.op("gpsimd", lambda g_: g_.engine_nop(), waits=[P.dma_token("qg")])
    return


def emit_ph5b(nc, P, ctx, d, pfx="ph5b_"):

    hmid = d["hmid"]
    maxes = d["maxes"]
    clipd = d["clipd"]
    wdTq_d = d["wdTq"]
    x2q = d["x2q"]
    ones_d = d["onesrow"]
    OUT = d["OUT"]

    sb = lambda n, s, dt=F32: ctx.enter_context(nc.sbuf_tensor(pfx + n, s, dt))
    psn = lambda n, s, dt=F32: ctx.enter_context(nc.psum_tensor(pfx + n, s, dt))
    PS2 = psn("PS2", [128, 2048])
    bank = lambda ps, k: ps[:, k * 512:(k + 1) * 512]
    onesr = sb("onesr", [1, 128])
    P.dma("qs", "sync", lambda e: e.dma_start(out=onesr[:], in_=ones_d),
          writes=["onesr"])
    scol, rcol = _emit_scalar_prep(P, nc, ctx, maxes, clipd, "onesr",
                                   onesr, PS2[:, 1536:2048], "B23", pfx)
    wdq = [sb(f"wdq{i}", [128, D], BF16) for i in range(IC1)]
    for i in range(IC1):
        P.dma("qs", "sync", lambda e, i0=i: e.dma_start(
            out=wdq[i0][:], in_=wdTq_d[i0 * 128:(i0 + 1) * 128, :]),
            writes=[("wdq", i)])
    hl = [sb(f"hl{u}", [128, 512]) for u in range(2)]
    mfm = [sb(f"mfm{i}", [128, 512], BF16) for i in range(IC1)]
    x2t = sb("x2t", [128, D])
    outt = sb("outt", [128, D])
    for st in range(8):
        for ic in range(IC1):
            u = ic % 2
            P.dma("qa" if u == 0 else "qv", "sync",
                  lambda e, i0=ic, s0=st * 512, b=hl[u]:
                  e.dma_start(out=b[:], in_=hmid[i0][:, s0:s0 + 512]),
                  writes=[("hl", u)])
            P.op("vector", lambda v, b=hl[u]: v.tensor_scalar(
                b[:], b[:], rcol[:], None, ALU.mult),
                reads=[("hl", u), "rcol_b"], writes=[("hl", u)])
            P.op("vector", lambda v, b=hl[u]: v.tensor_scalar(
                b[:], b[:], MAGIC, None, ALU.add),
                reads=[("hl", u)], writes=[("hl", u)])
            P.op("vector", lambda v, b=hl[u]: v.tensor_scalar(
                b[:], b[:], MAGIC, None, ALU.subtract),
                reads=[("hl", u)], writes=[("hl", u)])
            P.op("vector", lambda v, b=hl[u]: v.tensor_scalar(
                b[:], b[:], -QMAX - 1.0, QMAX, ALU.max, ALU.min),
                reads=[("hl", u)], writes=[("hl", u)])
            P.op("vector", lambda v, b=hl[u], i0=ic: v.tensor_copy(
                mfm[i0][:], b[:]), reads=[("hl", u)], writes=[("mfm", ic)])
        for tile in range(4):
            rr = st * 512 + tile * 128
            P.dma("q5", "sync", lambda e, r0=rr: e.dma_start(
                out=x2t[:], in_=x2q[r0:r0 + 128, :]), writes=["x2t"])
            for ds in range(2):
                k_loop(P, bank(PS2, ds),
                       [(mfm[ic][:, tile * 128:(tile + 1) * 128],
                         wdq[ic][:, ds * 512:(ds + 1) * 512])
                        for ic in range(IC1)],
                       reads=[("mfm", ic) for ic in range(IC1)]
                       + [("wdq", ic) for ic in range(IC1)],
                       writes=[("B2", ds)])
                P.op("vector", lambda v, d_=ds: v.tensor_scalar(
                    outt[:, d_ * 512:(d_ + 1) * 512], bank(PS2, d_),
                    scol[:], None, ALU.mult),
                    reads=[("B2", ds), "scol_b"], writes=[("outt", ds)])
                P.op("vector", lambda v, d_=ds: v.tensor_tensor(
                    outt[:, d_ * 512:(d_ + 1) * 512],
                    outt[:, d_ * 512:(d_ + 1) * 512],
                    x2t[:, d_ * 512:(d_ + 1) * 512], ALU.add),
                    reads=[("outt", ds), "x2t"], writes=[("outt", ds)])
            P.dma("qg", "gpsimd", lambda e, r0=rr: e.dma_start(
                out=OUT[r0:r0 + 128, :], in_=outt[:]),
                reads=[("outt", 0), ("outt", 1)])
    P.op("gpsimd", lambda g_: g_.engine_nop(), waits=[P.dma_token("qg")])
    return



def build_fused():
    nc = bass.Bass("TRN2", target_bir_lowering=False, debug=False)
    d = {}

    def din(name, shape, dt=F32):
        d[name] = nc.dram_tensor(name, shape, dt, kind="ExternalInput").ap()

    def dint(name, shape, dt=F32):
        d[name] = nc.dram_tensor(name, shape, dt, kind="Internal").ap()

    din("x0", [R, D]); din("wq", [D, 512]); din("wk", [D, 512])
    din("wv", [D, 512]); din("wo", [4, 128, D]); din("wgates", [D, H])
    din("rmsc", [128, DC]); din("maskIJ", [4, 128, 512], BF16)
    din("maskT", [4, 128, 512], BF16); din("identf", [128, 128])
    din("identb", [128, 128], BF16); din("erow", [1, 128])
    din("onesrow", [1, 128])
    din("w1", [D, F0]); din("w2", [F0, D]); din("Wh", [D, 2 * HID])
    din("Wo", [HID, D]); din("wg", [I1, D]); din("wgT", [D, I1])
    din("wu", [I1, D]); din("wuT", [D, I1]); din("wd", [D, I1])
    din("wdT", [I1, D])
    din("b1c", [128, FC0]); din("b2", [1, D])
    din("lngc", [128, DC]); din("lnbc", [128, DC])
    din("Wqk", [D, QK]); din("bhv", [1, HID]); din("bhgc", [128, HC])
    din("bqkc", [128, 1]); din("gb", [128, 4]); din("bor", [1, D])
    din("clipx", [1, 1]); din("clipd", [1, 1])
    dint("xnfm", [DC, 128, R], BF16); dint("tsc", [32, 128])
    dint("x1p", [R, D]); dint("x1", [R, D]); dint("xn1", [R, D])
    dint("normfm", [DC, 128, R], BF16); dint("x2p", [R, D])
    dint("x2q", [R, D]); dint("hmid", [IC1, 128, R])
    dint("lm1", [1, 1]); dint("lm2", [1, 1])
    dint("w1b", [D, F0], BF16); dint("w2b", [F0, D], BF16)
    dint("Whb", [D, 2 * HID], BF16); dint("Wob", [HID, D], BF16)
    dint("wgTi", [D, I1], BF16); dint("wuTi", [D, I1], BF16)
    dint("wdTq", [I1, D], BF16)
    dint("sgc", [128, IC1]); dint("suc", [128, IC1])
    dint("mrow_scr", [16, 128]); dint("crow_scr", [2, 16, 128])
    dint("v_scr", [16, 128, HID], BF16)
    dint("g_scr", [HC, 128, N], BF16); dint("gd_scr", [HC, 128, N], BF16)
    d["OUT"] = nc.dram_tensor("OUT", [R, D], F32, kind="ExternalOutput").ap()
    dint("nfm_scr", [DC, 128, R], BF16)
    cc1_in = nc.dram_tensor("cc1_in", [128, 128], F32, kind="Internal")
    cc1_out = nc.dram_tensor("cc1_out", [128, 128], F32, kind="Internal")
    cc2_in = nc.dram_tensor("cc2_in", [128, 128], F32, kind="Internal")
    cc2_out = nc.dram_tensor("cc2_out", [128, 128], F32, kind="Internal")

    with ExitStack() as octx:
        P = Prog(nc, octx)

        def phase(emit, dd, pfx):
            with ExitStack() as ctx:
                emit(nc, P, ctx, dd, pfx)
            P.join()

        phase(emit_phW, d, "W_")
        phase(emit_ph0, d, "A_")
        phase(emit_ph1, d, "B_")
        phase(emit_ph2, dict(d, x1p=d["x1p"]), "C_")
        phase(emit_ph3, d, "D_")

        def collective(src_ap, cin, cout, tag):
            with ExitStack() as ctx:
                pad = ctx.enter_context(
                    nc.sbuf_tensor(tag + "pad", [128, 128], F32))
                P.op("vector", lambda v: v.memset(pad[:], 0.0),
                     writes=[tag + "pad"])
                P.dma("qg", "gpsimd", lambda e: e.dma_start(
                    out=pad[0:1, 0:1], in_=src_ap),
                    reads=[(tag + "lm",)], writes=[tag + "pad0"])
                P.dma("qg", "gpsimd", lambda e: e.dma_start(
                    out=cin[:, :], in_=pad[:]),
                    reads=[tag + "pad", tag + "pad0"], writes=[tag + "in"])
                P.op("gpsimd", lambda g_: g_.collective_compute(
                    "AllReduce", ALU.max,
                    replica_groups=[list(range(NCORES))],
                    ins=[cin.ap().opt()], outs=[cout.ap().opt()]),
                    reads=[tag + "in"], writes=[tag + "done"])

        P.last_write[("cc1lm",)] = P.last_write.get(("lm1_hbm",))
        collective(d["lm1"], cc1_in, cc1_out, "cc1")
        phase(emit_ph4, d, "E_")
        d5a = dict(d, maxes=cc1_out.ap()[0:1, 0:8])
        phase(emit_ph5a, d5a, "F_")
        P.last_write[("cc2lm",)] = P.last_write.get(("lm2_hbm",))
        collective(d["lm2"], cc2_in, cc2_out, "cc2")
        P.join()
        d5b = dict(d, maxes=cc2_out.ap()[0:1, 0:8], x2q=d["x2q"])
        d5b["OUT"] = d["OUT"]
        phase(emit_ph5b, d5b, "G_")
        block = octx.enter_context(nc.Block())
        P.emit(block)
    return nc


_FUSED_NC = None


def kernel(**inp):
    global _FUSED_NC
    x = np.ascontiguousarray(np.asarray(inp["L0_attn_tokens"], np.float32)
                             .reshape(NCORES, SEQ * N, D))
    wq = np.ascontiguousarray(np.transpose(
        np.asarray(inp["L0_attn_wq"], np.float32), (1, 0, 2)).reshape(D, 512))
    wk = np.ascontiguousarray(np.transpose(
        np.asarray(inp["L0_attn_wk"], np.float32), (1, 0, 2)).reshape(D, 512))
    wv = np.ascontiguousarray(np.transpose(
        np.asarray(inp["L0_attn_wv"], np.float32), (1, 0, 2)).reshape(D, 512))
    wo = np.ascontiguousarray(np.asarray(inp["L0_attn_wo"], np.float32)
                              .reshape(4, 128, D))
    wgates = np.ascontiguousarray(np.asarray(inp["L0_attn_w_gates"], np.float32))
    rmsc = _cols(inp["L0_attn_rms_w"], DC)
    mask_T, mask_IJ = _host_masks()
    mask_T = mask_T.astype(ml_dtypes.bfloat16)
    mask_IJ = mask_IJ.astype(ml_dtypes.bfloat16)
    erow = np.zeros((1, 128), np.float32); erow[0, 64] = 1.0
    onesrow = np.ones((1, 128), np.float32)
    identf = np.eye(128, dtype=np.float32)
    identb = np.eye(128).astype(ml_dtypes.bfloat16)
    w1 = np.ascontiguousarray(np.asarray(inp["L0_ffn_w1"], np.float32))
    b1c = _cols(inp["L0_ffn_b1"], FC0)
    w2 = np.ascontiguousarray(np.asarray(inp["L0_ffn_w2"], np.float32))
    b2 = np.asarray(inp["L0_ffn_b2"], np.float32).reshape(1, D)
    Wh = np.ascontiguousarray(np.asarray(inp["L1_attn_Wh"], np.float32))
    Wo1 = np.ascontiguousarray(np.asarray(inp["L1_attn_Wo"], np.float32))
    wg = np.ascontiguousarray(np.asarray(inp["L1_ffn_w_gate"], np.float32))
    wu = np.ascontiguousarray(np.asarray(inp["L1_ffn_w_up"], np.float32))
    wd = np.ascontiguousarray(np.asarray(inp["L1_ffn_w_down"], np.float32))
    wgT = np.ascontiguousarray(wg.T)
    wuT = np.ascontiguousarray(wu.T)
    wdT = np.ascontiguousarray(wd.T)
    lngc = _cols(inp["L1_attn_ln_g"], DC)
    lnbc = _cols(inp["L1_attn_ln_b"], DC)
    bh = np.asarray(inp["L1_attn_bh"], np.float32)
    bhv = bh[:HID].reshape(1, HID)
    bhgc = _cols(bh[HID:], HC)
    bqkc = np.ascontiguousarray(np.asarray(inp["L1_attn_bqk"], np.float32)
                                .reshape(QK, 1))
    gamma = np.asarray(inp["L1_attn_gamma"], np.float32)
    beta = np.asarray(inp["L1_attn_beta"], np.float32)
    gb = np.ascontiguousarray(np.stack(
        [gamma[0], gamma[1], beta[0], beta[1]], axis=1))
    Wqk = np.ascontiguousarray(np.asarray(inp["L1_attn_Wqk"], np.float32))
    bo = np.asarray(inp["L1_attn_bo"], np.float32).reshape(1, D)
    clipx = np.asarray(inp["L1_ffn_clip_x"], np.float32).reshape(1, 1)
    clipd = np.asarray(inp["L1_ffn_clip_down"], np.float32).reshape(1, 1)

    if _FUSED_NC is None:
        _FUSED_NC = build_fused()
    nc = _FUSED_NC
    common = {"wq": wq, "wk": wk, "wv": wv, "wo": wo, "wgates": wgates,
              "rmsc": rmsc, "maskIJ": mask_IJ, "maskT": mask_T,
              "identf": identf, "identb": identb, "erow": erow,
              "onesrow": onesrow, "w1": w1, "w2": w2, "Wh": Wh, "Wo": Wo1,
              "wg": wg, "wgT": wgT, "wu": wu, "wuT": wuT, "wd": wd,
              "wdT": wdT, "b1c": b1c, "b2": b2, "lngc": lngc, "lnbc": lnbc,
              "Wqk": Wqk, "bhv": bhv, "bhgc": bhgc, "bqkc": bqkc, "gb": gb,
              "bor": bo, "clipx": clipx, "clipd": clipd}
    cores = list(range(NCORES))
    in_maps = [dict(common, x0=x[c]) for c in cores]
    res = run_bass_kernel_spmd(nc, in_maps, core_ids=cores)
    out = np.stack([res.results[c]["OUT"] for c in cores]).reshape(16, N, D)
    return out


# version 23 (final)
# speedup vs baseline: 1.1214x; 1.0051x over previous
import math
from contextlib import ExitStack

import numpy as np
import ml_dtypes

import concourse.bass as bass
import concourse.mybir as mybir
from concourse.bass_utils import run_bass_kernel_spmd

F32 = mybir.dt.float32
BF16 = mybir.dt.bfloat16
AF = mybir.ActivationFunctionType
ALU = mybir.AluOpType
AX = mybir.AxisListType
MAGIC = float(np.float32(1.5 * 2**23))
NEG = -1e9
EPS_LN = 1e-5
EPS_RMS = float(np.finfo(np.float32).eps)
QMAX = 127.0

NCORES = 8
SEQ = 2
N = 2048
R = SEQ * N
D = 1024
DC = D // 128
H = 8
E = 64
F0 = 4096
FC0 = F0 // 128
HID = 2048
HC = HID // 128
QK = 128
I1 = 2816
IC1 = I1 // 128


class Prog:

    ENGINES = ("sync", "tensor", "vector", "scalar", "gpsimd")
    QUEUES = ("qs", "qg", "qa", "qv", "q5", "q6")

    def __init__(self, nc, ctx):
        self.nc = nc
        self.ops = []
        self.join_pos = set()
        self.sem = {}
        self.count = {e: 0 for e in self.ENGINES}
        self.dma_count = {q: 0 for q in self.QUEUES}
        for e in self.ENGINES:
            self.sem[e] = ctx.enter_context(nc.semaphore("sm_" + e))
        for q in self.QUEUES:
            self.sem[q] = ctx.enter_context(nc.semaphore("sm_" + q))
        self.last_write = {}
        self.readers = {}

    def _record(self, eng, fn, reads, writes, waits, tsem_name, tamt):
        w = list(waits)
        for b in reads:
            t = self.last_write.get(b)
            if t:
                w.append(t)
        for b in writes:
            t = self.last_write.get(b)
            if t:
                w.append(t)
            w.extend(self.readers.get(b, ()))
        if tamt == 0:
            tok = (eng, self.count[eng])
            self.ops.append((eng, fn, tuple(w), None, 0))
            return tok
        if tsem_name in self.dma_count:
            self.dma_count[tsem_name] += tamt
            tok = (tsem_name, self.dma_count[tsem_name])
        else:
            self.count[eng] += tamt
            tok = (eng, self.count[eng])
        self.ops.append((eng, fn, tuple(w), self.sem[tsem_name], tamt))
        for b in reads:
            self.readers.setdefault(b, []).append(tok)
        for b in writes:
            self.last_write[b] = tok
            self.readers[b] = []
        return tok

    def op(self, eng, fn, reads=(), writes=(), waits=()):
        return self._record(eng, fn, reads, writes, waits, eng, 1)

    def op_notick(self, eng, fn, waits=()):
        return self._record(eng, fn, (), (), waits, eng, 0)

    def peek_raw(self, bufs):
        out = []
        for b in bufs:
            t = self.last_write.get(b)
            if t:
                out.append(t)
        return out

    def peek_war(self, bufs):
        out = []
        for b in bufs:
            t = self.last_write.get(b)
            if t:
                out.append(t)
            out.extend(self.readers.get(b, ()))
        return out

    def dma(self, queue, eng, fn, reads=(), writes=(), waits=()):
        return self._record(eng, fn, reads, writes, waits, queue, 16)

    def barrier_token(self, eng):
        return (eng, self.count[eng])

    def join(self):
        self.join_pos.add(len(self.ops))
        toks = [(e, self.count[e]) for e in self.ENGINES] + \
               [(q, self.dma_count[q]) for q in self.QUEUES]
        for e in self.ENGINES:
            self.op(e, lambda h: h.nop(), waits=list(toks))

    def dma_token(self, queue):
        return (queue, self.dma_count[queue])

    def emit(self, block):
        qname = {}
        for q in self.QUEUES:
            qname[id(self.sem[q])] = q
        dma_running = {q: 0 for q in self.QUEUES}
        resolved = []
        for eng, fn, waits, tsem, tamt in self.ops:
            w = []
            for (s, c) in waits:
                if s in dma_running:
                    c = dma_running[s]
                if c > 0:
                    w.append((s, c))
            is_q = qname.get(id(tsem)) if tsem is not None else None
            resolved.append([eng, fn, w, tsem, tamt, is_q])
            if is_q:
                dma_running[is_q] += tamt
        boundaries = {q: [0] for q in self.QUEUES}
        running = {q: 0 for q in self.QUEUES}
        for pos, rec in enumerate(resolved):
            if pos in self.join_pos:
                for q in self.QUEUES:
                    if running[q] > boundaries[q][-1]:
                        boundaries[q].append(running[q])
            eng, fn, w, tsem, tamt, is_q = rec
            if is_q:
                if w and running[is_q] > boundaries[is_q][-1]:
                    boundaries[is_q].append(running[is_q])
                running[is_q] += tamt
            elif w:
                for q in self.QUEUES:
                    if running[q] > boundaries[q][-1]:
                        pass
        for q in self.QUEUES:
            if running[q] > boundaries[q][-1]:
                boundaries[q].append(running[q])

        def round_up(q, c):
            import bisect
            b = boundaries[q]
            i = bisect.bisect_left(b, c)
            return b[i] if i < len(b) else b[-1]

        per_eng = {e: [] for e in self.ENGINES}
        for eng, fn, w, tsem, tamt, is_q in resolved:
            w = [(s, round_up(s, c) if s in self.QUEUES else c) for (s, c) in w]
            per_eng[eng].append((fn, w, tsem, tamt, bool(is_q)))
        for eng in self.ENGINES:
            steps = per_eng[eng]
            if not steps:
                continue
            sems = self.sem

            def body(handle, steps=steps):
                seen = {}
                for fn, w, tsem, tamt, isdma in steps:
                    for (s, c) in w:
                        if isdma or seen.get(s, 0) < c:
                            handle.wait_ge(sems[s], c)
                            seen[s] = max(seen.get(s, 0), c)
                    instr = fn(handle)
                    if tsem is not None:
                        instr.then_inc(tsem, tamt)

            getattr(block, eng)(body)


class ConstBank:

    def __init__(self, P, nc, ctx, n=16, pfx=""):
        self.P = P
        self.tile = ctx.enter_context(
            nc.sbuf_tensor(pfx + "konst", [128, n], F32))
        self.vals = {}
        self.pfx = pfx

    def get(self, val):
        val = float(val)
        if val not in self.vals:
            i = len(self.vals)
            self.vals[val] = i
            self.P.op("vector",
                      lambda v, i=i, val=val: v.memset(
                          self.tile[:, i:i + 1], val),
                      writes=[(self.pfx + "konst", i)])
        return self.tile[:, self.vals[val]:self.vals[val] + 1]

    def key(self, val):
        return (self.pfx + "konst", self.vals[float(val)])


def k_loop(P, psum_ap, pairs, reads, writes, waits=()):
    n = len(pairs)
    first_waits = list(waits) + P.peek_raw(reads) + P.peek_war(writes)
    for k, (l, r) in enumerate(pairs):
        fn = (lambda t, o=psum_ap, l=l, r=r, st=(k == 0), sp=(k == n - 1):
              t.matmul(o, l, r, start=st, stop=sp))
        if k == n - 1:
            P.op("tensor", fn, reads=reads, writes=writes,
                 waits=(first_waits if k == 0 else ()))
        else:
            P.op_notick("tensor", fn, waits=(first_waits if k == 0 else ()))



def _host_masks():
    j = np.arange(128)[:, None]
    i = np.arange(512)[None, :]
    mask_T = np.stack([np.where(d * 128 + j <= i, 0.0, NEG)
                       for d in range(4)]).astype(np.float32)
    ii = np.arange(128)[:, None]
    jj = np.arange(512)[None, :]
    mask_IJ = np.stack([np.where(jj <= v * 128 + ii, 0.0, NEG)
                        for v in range(4)]).astype(np.float32)
    return mask_T, mask_IJ


def _host_selpair():
    sel = np.zeros((8, 8 * 128), np.float32)
    for p in range(4):
        sel[2 * p, p * 128:p * 128 + 64] = 1.0
        sel[2 * p + 1, p * 128 + 64:p * 128 + 128] = 1.0
    return sel


def _cols(vec, nch):
    return np.ascontiguousarray(np.asarray(vec, np.float32)
                                .reshape(nch, 128).T)



def emit_phW(nc, P, ctx, d, pfx="phW_", part=0):



    sb = lambda n, s, dt=F32: ctx.enter_context(nc.sbuf_tensor(pfx + n, s, dt))
    psn = lambda n, s, dt=F32: ctx.enter_context(nc.psum_tensor(pfx + n, s, dt))
    fb = [sb(f"fb{i}", [128, 4096]) for i in range(2)]
    bb = [sb(f"bb{i}", [128, 4096], BF16) for i in range(2)]

    def cast_stream(src, dst, nrows, ncols):
        for c in range(nrows // 128):
            s = c % 2
            P.dma("qs" if s == 0 else "qa", "sync",
                  lambda e, o=fb[s], i=src[c * 128:(c + 1) * 128, :],
                         n_=ncols: e.dma_start(out=o[:, 0:n_], in_=i),
                  writes=[f"fb{s}"])
            P.op("vector",
                 lambda v, o=bb[s], i=fb[s], n_=ncols:
                 v.tensor_copy(o[:, 0:n_], i[:, 0:n_]),
                 reads=[f"fb{s}"], writes=[f"bb{s}"])
            P.dma("qg", "gpsimd",
                  lambda e, o=dst[c * 128:(c + 1) * 128, :], i=bb[s],
                         n_=ncols: e.dma_start(out=o, in_=i[:, 0:n_]),
                  reads=[f"bb{s}"])

    if part in (0, 1):
        cast_stream(d["w1"], d["w1b"], D, F0)
        cast_stream(d["w2"], d["w2b"], F0, D)
        cast_stream(d["Wh"], d["Whb"], D, 2 * HID)
        cast_stream(d["Wo"], d["Wob"], HID, D)
    if part == 1:
        return

    ident = sb("ident", [128, 128])
    ones1 = sb("ones1", [1, 128])
    P.dma("qs", "sync", lambda e: e.dma_start(out=ident[:], in_=d["identf"]),
          writes=["ident"])
    P.op("vector", lambda v: v.memset(ones1[:], 1.0), writes=["ones1"])

    nat = sb("nat", [128, 4096])
    scol = sb("scol", [128, 1])
    ps_r = psn("ps_r", [1, 128])
    srow = {w: sb(f"srow_{w}", [1, 4096]) for w in ("wg", "wu", "wd")}
    sgcol = sb("sgcol", [128, IC1]); sucol = sb("sucol", [128, IC1])
    coltile = {"wg": sgcol, "wu": sucol}
    for wname, nchunks, ncols in (("wg", IC1, D), ("wu", IC1, D),
                                  ("wd", DC, I1)):
        for c in range(nchunks):
            P.dma("qv", "sync",
                  lambda e, i=d[wname][c * 128:(c + 1) * 128, :], n_=ncols:
                  e.dma_start(out=nat[:, 0:n_], in_=i), writes=["nat"])
            P.op("vector",
                 lambda v, n_=ncols: v.reduce_max(
                     scol[:], nat[:, 0:n_], axis=AX.X,
                     apply_absolute_value=True),
                 reads=["nat"], writes=["scol"])
            P.op("vector",
                 lambda v: v.tensor_scalar(scol[:], scol[:], 1.0 / QMAX,
                                           None, ALU.mult),
                 reads=["scol"], writes=["scol"])
            if wname in coltile:
                P.op("vector",
                     lambda v, o=coltile[wname], cc=c: v.tensor_copy(
                         o[:, cc:cc + 1], scol[:]),
                     reads=["scol"], writes=[("wcol", wname, c)])
            P.op("tensor",
                 lambda t: t.transpose(ps_r[:], scol[:, 0:1], ident[:]),
                 reads=["scol", "ident"], writes=["ps_r"])
            P.op("vector",
                 lambda v, o=srow[wname], cc=c:
                 v.tensor_copy(o[:, cc * 128:(cc + 1) * 128], ps_r[:]),
                 reads=["ps_r"], writes=[f"srow_{wname}_{c}"])
    P.dma("qg", "gpsimd",
          lambda e: e.dma_start(out=d["sgc"], in_=sgcol[:]),
          reads=[("wcol", "wg", c) for c in range(IC1)])
    P.dma("qg", "gpsimd",
          lambda e: e.dma_start(out=d["suc"], in_=sucol[:]),
          reads=[("wcol", "wu", c) for c in range(IC1)])

    rrow = sb("rrow", [1, 4096])
    ps_b = psn("ps_b", [128, 512])
    bc = sb("bc", [128, 4096])
    sbc = sb("sbc", [128, 4096])
    qf = sb("qf", [128, 4096])
    qb = sb("qb", [128, 4096], BF16)

    for wname, tname, oname, scaled in (("wg", "wgT", "wgTi", False),
                                        ("wu", "wuT", "wuTi", False),
                                        ("wd", "wdT", "wdTq", True)):
        ncols = I1 if wname != "wd" else D
        nchunks = DC if wname != "wd" else IC1
        nsch = IC1 if wname != "wd" else DC
        P.op("vector",
             lambda v, s=srow[wname], n_=ncols:
             v.reciprocal(rrow[:, 0:n_], s[:, 0:n_]),
             reads=[f"srow_{wname}_{c}" for c in range(nsch)],
             writes=["rrow"])
        for st in range((ncols + 511) // 512):
            lo, hi = st * 512, min((st + 1) * 512, ncols)
            P.op("tensor",
                 lambda t, lo=lo, hi=hi: t.matmul(
                     ps_b[:, 0:hi - lo], ones1[:], rrow[0:1, lo:hi],
                     start=True, stop=True),
                 reads=["rrow", "ones1"], writes=["ps_b"])
            P.op("vector",
                 lambda v, lo=lo, hi=hi:
                 v.tensor_copy(bc[:, lo:hi], ps_b[:, 0:hi - lo]),
                 reads=["ps_b"], writes=[("bc", st)])
        if scaled:
            for st in range((ncols + 511) // 512):
                lo, hi = st * 512, min((st + 1) * 512, ncols)
                P.op("tensor",
                     lambda t, lo=lo, hi=hi, s=srow[wname]: t.matmul(
                         ps_b[:, 0:hi - lo], ones1[:], s[0:1, lo:hi],
                         start=True, stop=True),
                     reads=[f"srow_{wname}_{c}" for c in range(nsch)]
                     + ["ones1"], writes=["ps_b"])
                P.op("vector",
                     lambda v, lo=lo, hi=hi:
                     v.tensor_copy(sbc[:, lo:hi], ps_b[:, 0:hi - lo]),
                     reads=["ps_b"], writes=[("sbc", st)])
        bkeys = [("bc", st) for st in range((ncols + 511) // 512)]
        skeys = [("sbc", st) for st in range((ncols + 511) // 512)] \
            if scaled else []
        for c in range(nchunks):
            P.dma("q5", "sync",
                  lambda e, i=d[tname][c * 128:(c + 1) * 128, :], n_=ncols:
                  e.dma_start(out=qf[:, 0:n_], in_=i), writes=["qf"])
            P.op("vector",
                 lambda v, n_=ncols:
                 v.tensor_mul(qf[:, 0:n_], qf[:, 0:n_], bc[:, 0:n_]),
                 reads=["qf"] + bkeys, writes=["qf"])
            P.op("vector",
                 lambda v, n_=ncols: v.tensor_scalar(
                     qf[:, 0:n_], qf[:, 0:n_], MAGIC, None, ALU.add),
                 reads=["qf"], writes=["qf"])
            P.op("vector",
                 lambda v, n_=ncols: v.tensor_scalar(
                     qf[:, 0:n_], qf[:, 0:n_], MAGIC, None, ALU.subtract),
                 reads=["qf"], writes=["qf"])
            P.op("vector",
                 lambda v, n_=ncols: v.tensor_scalar(
                     qf[:, 0:n_], qf[:, 0:n_], -QMAX - 1.0, QMAX,
                     ALU.max, ALU.min),
                 reads=["qf"], writes=["qf"])
            if scaled:
                P.op("vector",
                     lambda v, n_=ncols:
                     v.tensor_mul(qf[:, 0:n_], qf[:, 0:n_], sbc[:, 0:n_]),
                     reads=["qf"] + skeys, writes=["qf"])
            P.op("scalar",
                 lambda a_, n_=ncols: a_.copy(qb[:, 0:n_], qf[:, 0:n_]),
                 reads=["qf"], writes=["qb"])
            P.dma("qg", "gpsimd",
                  lambda e, o=d[oname][c * 128:(c + 1) * 128, :], n_=ncols:
                  e.dma_start(out=o, in_=qb[:, 0:n_]), reads=["qb"])

    P.op("gpsimd", lambda g: g.engine_nop(),
         waits=[P.dma_token("qg"), P.dma_token("qs")])
    return


def emit_ph0(nc, P, ctx, d, pfx="ph0_"):
    x0 = d["x0"]; identb = d["identb"]; identf = d["identf"]
    xnfm = d["xnfm"]; tsc = d["tsc"]
    sb = lambda n, s, dt=F32: ctx.enter_context(nc.sbuf_tensor(pfx + n, s, dt))
    psn = lambda n, s, dt=F32: ctx.enter_context(nc.psum_tensor(pfx + n, s, dt))
    xt = [sb(f"xt{i}", [128, D]) for i in range(2)]
    sq_ = [sb(f"sq{u}", [128, D]) for u in range(2)]
    mcol_ = [sb(f"mcol{u}", [128, 1]) for u in range(2)]
    vcol_ = [sb(f"vcol{u}", [128, 1]) for u in range(2)]
    rstd_ = [sb(f"rstd{u}", [128, 1]) for u in range(2)]
    r2_ = [sb(f"r2{u}", [128, 1]) for u in range(2)]
    xn_ = [sb(f"xn{u}", [128, D]) for u in range(2)]
    xnb_ = [sb(f"xnb{u}", [128, D], BF16) for u in range(2)]
    idb = sb("idb", [128, 128], BF16)
    idf = sb("idf", [128, 128], F32)
    tcols = sb("tcols", [128, 32])
    fm = [sb(f"fm{c}", [128, R], BF16) for c in range(DC)]
    pstr_ = [psn(f"pstr{u}", [128, 128], BF16) for u in range(2)]
    ps_t = psn("ps_t", [32, 128], F32)
    K = ConstBank(P, nc, ctx, pfx=pfx)
    b_ln = K.get(EPS_LN); k_ln = K.key(EPS_LN)
    b_rms = K.get(EPS_RMS); k_rms = K.key(EPS_RMS)

    P.dma("qs", "sync", lambda e: e.dma_start(out=idb[:], in_=identb),
          writes=["idb"])
    P.dma("qs", "sync", lambda e: e.dma_start(out=idf[:], in_=identf),
          writes=["idf"])

    def tile(r, s):
        sq = sq_[s]; mcol = mcol_[s]; vcol = vcol_[s]; rstd = rstd_[s]
        r2 = r2_[s]; xn = xn_[s]; xnb = xnb_[s]; b = xt[s]; pstr = pstr_[s]
        _k = lambda nm: (nm, s)
        P.dma("qs" if s == 0 else "qa", "sync",
              lambda e: e.dma_start(out=b[:], in_=x0[r * 128:(r + 1) * 128, :]),
              writes=[f"xt{s}"])
        P.op("vector", lambda v: v.reduce_sum(mcol[:], b[:], axis=AX.X),
             reads=[f"xt{s}"], writes=[_k("mcol")])
        P.op("vector", lambda v: v.tensor_scalar(
            mcol[:], mcol[:], 1.0 / D, None, ALU.mult),
            reads=[_k("mcol")], writes=[_k("mcol")])
        P.op("vector", lambda v: v.tensor_scalar(
            xn[:], b[:], mcol[:], None, ALU.subtract),
            reads=[f"xt{s}", _k("mcol")], writes=[_k("xn")])
        P.op("scalar", lambda a: a.activation(sq[:], xn[:], AF.Square),
             reads=[_k("xn")], writes=[_k("sq")])
        P.op("vector", lambda v: v.reduce_sum(vcol[:], sq[:], axis=AX.X),
             reads=[_k("sq")], writes=[_k("vcol")])
        P.op("scalar", lambda a: a.activation(
            rstd[:], vcol[:], AF.Sqrt, bias=b_ln, scale=1.0 / D),
            reads=[_k("vcol"), k_ln], writes=[_k("rstd")])
        P.op("vector", lambda v: v.reciprocal(rstd[:], rstd[:]),
             reads=[_k("rstd")], writes=[_k("rstd")])
        P.op("vector", lambda v: v.tensor_scalar(
            xn[:], xn[:], rstd[:], None, ALU.mult),
            reads=[_k("xn"), _k("rstd")], writes=[_k("xn")])
        P.op("scalar", lambda a: a.activation(sq[:], xn[:], AF.Square),
             reads=[_k("xn")], writes=[_k("sq")])
        P.op("vector", lambda v: v.reduce_sum(r2[:], sq[:], axis=AX.X),
             reads=[_k("sq")], writes=[_k("r2")])
        P.op("scalar", lambda a: a.activation(
            r2[:], r2[:], AF.Sqrt, bias=b_rms, scale=1.0 / D),
            reads=[_k("r2"), k_rms], writes=[_k("r2")])
        P.op("vector", lambda v: v.reciprocal(r2[:], r2[:]),
             reads=[_k("r2")], writes=[_k("r2")])
        P.op("vector", lambda v: v.tensor_copy(tcols[:, r:r + 1], r2[:]),
             reads=[_k("r2")], writes=[("tc", r)])
        P.op("scalar", lambda a: a.copy(xnb[:], xn[:]),
             reads=[_k("xn")], writes=[_k("xnb")])
        for c in range(DC):
            P.op("tensor",
                 lambda t, cc=c: t.transpose(
                     pstr[:], xnb[:, cc * 128:(cc + 1) * 128], idb[:]),
                 reads=[_k("xnb"), "idb"], writes=[_k("pstr")])
            P.op("scalar",
                 lambda a, cc=c: a.copy(
                     fm[cc][:, r * 128:(r + 1) * 128], pstr[:]),
                 reads=[_k("pstr")], writes=[("fm", c, r)])

    for r in range(R // 128):
        tile(r, r % 2)
    P.op("tensor", lambda t: t.transpose(ps_t[:], tcols[:], idf[:]),
         reads=[("tc", r) for r in range(32)] + ["idf"], writes=["ps_t"])
    trow = sb("trow", [32, 128], F32)
    P.op("vector", lambda v: v.tensor_copy(trow[:], ps_t[:]),
         reads=["ps_t"], writes=["trow"])
    P.dma("qg", "gpsimd", lambda e: e.dma_start(out=tsc, in_=trow[:]),
          reads=["trow"])
    for c in range(DC):
        P.dma("qg", "gpsimd",
              lambda e, cc=c: e.dma_start(out=xnfm[cc], in_=fm[cc][:]),
              reads=[("fm", c, r) for r in range(32)])
    P.op("gpsimd", lambda g: g.engine_nop(), waits=[P.dma_token("qg")])
    return

def emit_ph2(nc, P, ctx, d, pfx="ph2_"):
    TS = 512
    NS = R // TS

    """L0 FFN: h=xn@w1+b1 -> q256 -> exact PWL gelu -> q256 -> @w2+b2 -> q256;
    x1 = x1p + f0."""
    xnfm = d["xnfm"]
    x1p = d["x1p"]
    w1b = d["w1b"]
    w2b = d["w2b"]
    b1c = d["b1c"]
    b2 = d["b2"]
    x1 = d["x1"]
    sb = lambda n, s, dt=F32: ctx.enter_context(nc.sbuf_tensor(pfx + n, s, dt))
    psn = lambda n, s, dt=F32: ctx.enter_context(nc.psum_tensor(pfx + n, s, dt))
    w1s = [sb(f"w1s{c}", [128, F0], BF16) for c in range(DC)]
    w2s = [sb(f"w2s{c}", [128, D], BF16) for c in range(FC0)]
    b1t = sb("b1t", [128, FC0])
    ones1 = sb("ones1", [1, 128])
    b2bc = sb("b2bc", [128, D])
    xns = [sb(f"xns{c}", [128, TS], BF16) for c in range(DC)]
    g = [sb(f"g{c}", [128, TS], BF16) for c in range(FC0)]
    XQ4_ = [sb(f"XQ4{u}", [128, TS]) for u in range(2)]
    I4_ = [sb(f"I4{u}", [128, TS]) for u in range(2)]
    FR_ = [sb(f"FR{u}", [128, TS]) for u in range(2)]
    YLO_ = [sb(f"YLO{u}", [128, TS]) for u in range(2)]
    YHI_ = [sb(f"YHI{u}", [128, TS]) for u in range(2)]
    YY_ = [sb(f"YY{u}", [128, TS]) for u in range(2)]
    x1t = sb("x1t", [128, D]); outt = sb("outt", [128, D])
    ps_h_ = [psn(f"ps_h{u}", [128, TS]) for u in range(2)]
    ps_f = psn("ps_f", [128, D])
    ps_bc = psn("ps_bc", [128, 512])
    K = ConstBank(P, nc, ctx, pfx=pfx)
    b_q = K.get(0.25); k_q = K.key(0.25)

    for c in range(DC):
        P.dma("qs", "sync",
              lambda e, cc=c: e.dma_start(
                  out=w1s[cc][:], in_=w1b[cc * 128:(cc + 1) * 128, :]),
              writes=[("w1", c)])
    for c in range(FC0):
        P.dma("qs", "sync",
              lambda e, cc=c: e.dma_start(
                  out=w2s[cc][:], in_=w2b[cc * 128:(cc + 1) * 128, :]),
              writes=[("w2", c)])
    P.dma("qs", "sync", lambda e: e.dma_start(out=b1t[:], in_=b1c),
          writes=["b1t"])
    P.dma("qs", "sync", lambda e: e.dma_start(out=b2bc[0:1, :], in_=b2),
          writes=["b2row"])
    P.op("vector", lambda v: v.memset(ones1[:], 1.0), writes=["ones1"])
    for dstrip in range(2):
        P.op("tensor", lambda t, ds=dstrip: t.matmul(
            ps_bc[:], ones1[:], b2bc[0:1, ds * 512:(ds + 1) * 512],
            start=True, stop=True),
            reads=["b2row", "ones1"], writes=["ps_bc"])
        P.op("vector", lambda v, ds=dstrip: v.tensor_copy(
            b2bc[:, ds * 512:(ds + 1) * 512], ps_bc[:]),
            reads=["ps_bc"], writes=[("b2bc", dstrip)])

    w1keys = [("w1", c) for c in range(DC)]
    w2keys = [("w2", c) for c in range(FC0)]
    for s in range(NS):
        for c in range(DC):
            P.dma("qs", "sync",
                  lambda e, cc=c, ss=s: e.dma_start(
                      out=xns[cc][:],
                      in_=xnfm[cc][:, ss * TS:(ss + 1) * TS]),
                  writes=[("xns", c)])
        for fc in range(FC0):
            u = fc % 2
            XQ4 = XQ4_[u]; I4 = I4_[u]; FR = FR_[u]
            YLO = YLO_[u]; YHI = YHI_[u]; YY = YY_[u]
            ps_h = ps_h_[u]
            kx = lambda nm, u=u: (nm, u)
            k_loop(P, ps_h[:],
                   [(w1s[c][:, fc * 128:(fc + 1) * 128], xns[c][:])
                    for c in range(DC)],
                   reads=w1keys + [("xns", c) for c in range(DC)] + ["b1t"],
                   writes=[kx("ps_h")])
            P.op("vector", lambda v, f=fc, XQ4=XQ4, ps_h=ps_h: v.tensor_scalar(
                XQ4[:], ps_h[:], b1t[:, f:f + 1], 256.0, ALU.add, ALU.mult),
                reads=[kx("ps_h"), "b1t"], writes=[kx("XQ4")])
            P.op("vector", lambda v, XQ4=XQ4: v.tensor_scalar(
                XQ4[:], XQ4[:], MAGIC, None, ALU.add),
                reads=[kx("XQ4")], writes=[kx("XQ4")])
            P.op("vector", lambda v, XQ4=XQ4: v.tensor_scalar(
                XQ4[:], XQ4[:], MAGIC, 2.0 ** -6, ALU.subtract, ALU.mult),
                reads=[kx("XQ4")], writes=[kx("XQ4")])
            P.op("vector", lambda v, XQ4=XQ4, I4=I4: v.tensor_scalar(
                I4[:], XQ4[:], MAGIC - 63.0 / 128.0, None, ALU.add),
                reads=[kx("XQ4")], writes=[kx("I4")])
            P.op("vector", lambda v, I4=I4: v.tensor_scalar(
                I4[:], I4[:], MAGIC, None, ALU.subtract),
                reads=[kx("I4")], writes=[kx("I4")])
            P.op("gpsimd", lambda g_, XQ4=XQ4, I4=I4, FR=FR: g_.tensor_sub(
                FR[:], XQ4[:], I4[:]),
                reads=[kx("XQ4"), kx("I4")], writes=[kx("FR")])
            P.op("scalar", lambda a_, I4=I4, YLO=YLO: a_.activation(
                YLO[:], I4[:], AF.Gelu, scale=0.25),
                reads=[kx("I4")], writes=[kx("YLO")])
            P.op("scalar", lambda a_, I4=I4, YHI=YHI: a_.activation(
                YHI[:], I4[:], AF.Gelu, bias=b_q, scale=0.25),
                reads=[kx("I4"), k_q], writes=[kx("YHI")])
            P.op("gpsimd", lambda g_, YHI=YHI, YLO=YLO: g_.tensor_sub(
                YHI[:], YHI[:], YLO[:]),
                reads=[kx("YHI"), kx("YLO")], writes=[kx("YHI")])
            P.op("vector", lambda v, YY=YY, FR=FR, YHI=YHI: v.tensor_mul(
                YY[:], FR[:], YHI[:]),
                reads=[kx("FR"), kx("YHI")], writes=[kx("YY")])
            P.op("gpsimd", lambda g_, YY=YY, YLO=YLO: g_.tensor_add(
                YY[:], YY[:], YLO[:]),
                reads=[kx("YY"), kx("YLO")], writes=[kx("YY")])
            P.op("vector", lambda v, YY=YY: v.tensor_scalar(
                YY[:], YY[:], 256.0, MAGIC, ALU.mult, ALU.add),
                reads=[kx("YY")], writes=[kx("YY")])
            P.op("scalar", lambda a_, f=fc, YY=YY: a_.activation(
                g[f][:], YY[:], AF.Copy, bias=-MAGIC * 2.0 ** -8,
                scale=2.0 ** -8),
                reads=[kx("YY")], writes=[("g", fc)])
        for tile in range(TS // 128):
            rr = s * TS + tile * 128
            P.dma("q5", "sync",
                  lambda e, r0=rr: e.dma_start(out=x1t[:],
                                               in_=x1p[r0:r0 + 128, :]),
                  writes=["x1t"])
            for dstrip in range(2):
                k_loop(P, ps_f[:, dstrip * 512:(dstrip + 1) * 512],
                       [(g[fc][:, tile * 128:(tile + 1) * 128],
                         w2s[fc][:, dstrip * 512:(dstrip + 1) * 512])
                        for fc in range(FC0)],
                       reads=[("g", fc) for fc in range(FC0)] + w2keys,
                       writes=[("ps_f", dstrip)])
                P.op("vector", lambda v, ds=dstrip: v.tensor_tensor(
                    outt[:, ds * 512:(ds + 1) * 512],
                    ps_f[:, ds * 512:(ds + 1) * 512],
                    b2bc[:, ds * 512:(ds + 1) * 512], ALU.add),
                    reads=[("ps_f", dstrip), ("b2bc", dstrip)],
                    writes=[("outt", dstrip)])
                P.op("vector", lambda v, ds=dstrip: v.tensor_scalar(
                    outt[:, ds * 512:(ds + 1) * 512],
                    outt[:, ds * 512:(ds + 1) * 512],
                    256.0, MAGIC, ALU.mult, ALU.add),
                    reads=[("outt", dstrip)], writes=[("outt", dstrip)])
                P.op("vector", lambda v, ds=dstrip: v.tensor_scalar(
                    outt[:, ds * 512:(ds + 1) * 512],
                    outt[:, ds * 512:(ds + 1) * 512],
                    MAGIC, 2.0 ** -8, ALU.subtract, ALU.mult),
                    reads=[("outt", dstrip)], writes=[("outt", dstrip)])
                P.op("vector", lambda v, ds=dstrip: v.tensor_tensor(
                    outt[:, ds * 512:(ds + 1) * 512],
                    outt[:, ds * 512:(ds + 1) * 512],
                    x1t[:, ds * 512:(ds + 1) * 512], ALU.add),
                    reads=[("outt", dstrip), "x1t"],
                    writes=[("outt", dstrip)])
            P.dma("qg", "gpsimd",
                  lambda e, r0=rr: e.dma_start(out=x1[r0:r0 + 128, :],
                                               in_=outt[:]),
                  reads=[("outt", 0), ("outt", 1)])
    P.op("gpsimd", lambda g_: g_.engine_nop(), waits=[P.dma_token("qg")])
    return


def emit_ph1(nc, P, ctx, d, pfx="ph1_"):

    xnfm = d["xnfm"]
    tsc = d["tsc"]
    x0 = d["x0"]
    wq_d = d["wq"]
    wk_d = d["wk"]
    wv_d = d["wv"]
    wo_d = d["wo"]
    wgates_d = d["wgates"]
    rmsc_d = d["rmsc"]
    maskIJ_d = d["maskIJ"]
    maskT_d = d["maskT"]
    identf_d = d["identf"]
    erow_d = d["erow"]
    ones_d = d["onesrow"]
    x1p = d["x1p"]
    mrow_hbm = d["mrow_scr"]
    crow_hbm = d["crow_scr"]

    sb = lambda n, s, dt=F32: ctx.enter_context(nc.sbuf_tensor(pfx + n, s, dt))
    psn = lambda n, s, dt=F32: ctx.enter_context(nc.psum_tensor(pfx + n, s, dt))
    wqb = [sb(f"wqb{c}", [128, 512], BF16) for c in range(DC)]
    wkb = [sb(f"wkb{c}", [128, 512], BF16) for c in range(DC)]
    wvb = [sb(f"wvb{c}", [128, 512], BF16) for c in range(DC)]
    wob = [sb(f"wob{p}", [128, D], BF16) for p in range(4)]
    wgb = [sb(f"wgb{c}", [128, H], BF16) for c in range(DC)]
    wld2 = sb("wld2", [128, D])
    rmsc = sb("s_rmsc", [128, DC])
    maskIJ = [sb(f"mIJ{v}", [128, 512], BF16) for v in range(4)]
    maskT = [sb(f"mT{v}", [128, 512], BF16) for v in range(4)]
    identf = sb("s_identf", [128, 128])
    erow = sb("s_erow", [1, 128])
    onesr = sb("onesr", [1, 128])
    xn_sb = sb("xn_sb", [128, N], BF16)
    t_fm = [sb(f"t_fm{c}", [128, N], BF16) for c in range(DC)]
    tscrow = sb("tscrow", [1, N])
    tscb = sb("tscb", [128, N])
    qt = [sb(f"qt{u}", [65, N], BF16) for u in range(2)]
    kt = [sb(f"kt{u}", [65, N], BF16) for u in range(2)]
    v_tm = [sb(f"v_tm{j}", [128, H * 65], BF16) for j in range(16)]
    gates = sb("gates", [128, 16 * H])
    sums = sb("sums", [128, 16 * H])
    expdump_ = [sb(f"expdump{u}", [128, N], BF16) for u in range(2)]
    p_T = [sb(f"p_T{j}", [128, 512], BF16) for j in range(16)]
    outsc = [sb(f"outsc{p}", [128, N], BF16) for p in range(4)]
    mtmp_ = [sb(f"mtmp{u}", [128, 4]) for u in range(2)]
    mcol_ = [sb(f"mcol{u}", [128, 1]) for u in range(2)]
    mb16_ = [sb(f"mb16{u}", [128, 1], BF16) for u in range(2)]
    mcols = sb("mcols", [128, 16])
    negm_ = [sb(f"negm{u}", [128, 1]) for u in range(2)]
    ctmp = sb("ctmp", [128, 16])
    rs = sb("rs", [128, 1])
    mrow = [sb("mrow0", [1, N])] * 2
    crow = [sb(f"crow{u}", [1, N]) for u in range(2)]
    mrows16 = sb("mrows16", [16, 128])
    cb2 = sb("cb2", [128, 512])
    x0t = sb("x0t", [128, D])
    outt = sb("outt", [128, D])
    PS1 = psn("PS1", [128, 2048])
    PS2 = psn("PS2", [128, 2048])

    bank = lambda ps, k: ps[:, k * 512:(k + 1) * 512]

    for c in range(DC):
        for (wd_, dst, key, scale) in ((wq_d, wqb, "wq", 0.125),
                                       (wk_d, wkb, "wk", None),
                                       (wv_d, wvb, "wv", None)):
            P.dma("qs", "sync", lambda e, cc=c, w_=wd_: e.dma_start(
                out=wld2[:, 0:512], in_=w_[cc * 128:(cc + 1) * 128, :]),
                writes=["wld2"])
            if scale is None:
                P.op("vector", lambda v, cc=c, d_=dst: v.tensor_copy(
                    d_[cc][:], wld2[:, 0:512]), reads=["wld2"], writes=[(key, c)])
            else:
                P.op("vector", lambda v, cc=c, d_=dst, sc=scale:
                     v.tensor_scalar(d_[cc][:], wld2[:, 0:512], sc, None, ALU.mult),
                     reads=["wld2"], writes=[(key, c)])
        P.dma("qs", "sync", lambda e, cc=c: e.dma_start(
            out=wld2[:, 0:H], in_=wgates_d[cc * 128:(cc + 1) * 128, :]),
            writes=["wld2"])
        P.op("vector", lambda v, cc=c: v.tensor_copy(wgb[cc][:], wld2[:, 0:H]),
             reads=["wld2"], writes=[("wg", c)])
    for p in range(4):
        P.dma("qs", "sync", lambda e, pp=p: e.dma_start(
            out=wld2[:], in_=wo_d[pp]), writes=["wld2"])
        P.op("vector", lambda v, pp=p: v.tensor_copy(wob[pp][:], wld2[:]),
             reads=["wld2"], writes=[("wo", p)])
    P.dma("qs", "sync", lambda e: e.dma_start(out=rmsc[:], in_=rmsc_d),
          writes=["rmsc"])
    for v_ in range(4):
        P.dma("qs", "sync", lambda e, vv=v_: e.dma_start(
            out=maskIJ[vv][:], in_=maskIJ_d[vv]), writes=[("mIJ", v_)])
        P.dma("qs", "sync", lambda e, vv=v_: e.dma_start(
            out=maskT[vv][:], in_=maskT_d[vv]), writes=[("mT", v_)])
    P.dma("qs", "sync", lambda e: e.dma_start(out=identf[:], in_=identf_d),
          writes=["identf"])
    P.dma("qs", "sync", lambda e: e.dma_start(out=erow[:], in_=erow_d),
          writes=["erow"])
    P.dma("qs", "sync", lambda e: e.dma_start(out=onesr[:], in_=ones_d),
          writes=["onesr"])

    wqk = [("wq", c) for c in range(DC)]
    wkk = [("wk", c) for c in range(DC)]
    wvk = [("wv", c) for c in range(DC)]
    wgk = [("wg", c) for c in range(DC)]
    tfk = [("t", c) for c in range(DC)]

    for s in range(SEQ):
        P.dma("qs", "sync", lambda e, ss=s: e.dma_start(
            out=tscrow[:],
            in_=tsc.rearrange("a b -> (a b)")[None, ss * N:(ss + 1) * N]),
            writes=["tscrow"])
        for k in range(4):
            P.op("tensor", lambda t, kk=k: t.matmul(
                bank(PS2, 3), onesr[:], tscrow[0:1, kk * 512:(kk + 1) * 512],
                start=True, stop=True),
                reads=["tscrow", "onesr"],
                waits=P.peek_war(["B23"]), writes=["B23"])
            P.op("vector", lambda v, kk=k: v.tensor_copy(
                tscb[:, kk * 512:(kk + 1) * 512], bank(PS2, 3)),
                reads=["B23"], writes=[("tscb", k)])
        tscbk = [("tscb", k) for k in range(4)]
        for c in range(DC):
            P.dma("qs", "sync", lambda e, cc=c, ss=s: e.dma_start(
                out=xn_sb[:], in_=xnfm[cc][:, ss * N:(ss + 1) * N]),
                writes=["xn_sb"])
            P.op("vector", lambda v, cc=c: v.scalar_tensor_tensor(
                t_fm[cc][:], xn_sb[:], rmsc[:, cc:cc + 1], tscb[:],
                ALU.mult, ALU.mult),
                reads=["xn_sb", "rmsc"] + tscbk, writes=[("t", c)])
        for jt in range(16):
            k_loop(P, bank(PS2, 3),
                   [(t_fm[c][:, jt * 128:(jt + 1) * 128], wvb[c][:])
                    for c in range(DC)],
                   reads=tfk + wvk, writes=["B23"])
            P.op("vector", lambda v, j=jt: v.tensor_copy(
                v_tm[j][:].rearrange("p (h n) -> p h n", n=65)[:, :, 0:64],
                bank(PS2, 3).rearrange("p (h n) -> p h n", n=64)),
                reads=["B23"], writes=[("v", jt)])
            P.op("vector", lambda v, j=jt: v.memset(
                v_tm[j][:].rearrange("p (h n) -> p h n", n=65)[:, :, 64:65],
                1.0), writes=[("v1", jt)])
            k_loop(P, PS2[:, 1536:1544],
                   [(t_fm[c][:, jt * 128:(jt + 1) * 128], wgb[c][:])
                    for c in range(DC)],
                   reads=tfk + wgk, writes=["B23"])
            P.op("scalar", lambda a, j=jt: a.activation(
                gates[:, j * H:(j + 1) * H], PS2[:, 1536:1544], AF.Sigmoid),
                reads=["B23"], writes=[("gates", jt)])
        P.op("vector", lambda v: v.memset(sums[:], 0.0),
             writes=[("sums", h_, i_) for h_ in range(H) for i_ in range(16)])

        for pair in range(4):
            for hh in range(2):
                h = 2 * pair + hh
                Q, Kt = qt[hh], kt[hh]
                qk_r = [("qt", hh, I) for I in range(4)] + [("qtaug", hh)]
                kk_r = [("kt", hh, I) for I in range(4)] + [("kt1", hh)]
                for I in range(4):
                    k_loop(P, bank(PS2, 2)[0:64, :],
                           [(wqb[c][:, h * 64:(h + 1) * 64],
                             t_fm[c][:, I * 512:(I + 1) * 512])
                            for c in range(DC)],
                           reads=wqk + tfk, writes=["B22"])
                    P.op("vector", lambda v, II=I, Q=Q: v.tensor_copy(
                        Q[0:64, II * 512:(II + 1) * 512],
                        bank(PS2, 2)[0:64, :]),
                        reads=["B22"], writes=[("qt", hh, I)])
                    k_loop(P, bank(PS2, 2)[0:64, :],
                           [(wkb[c][:, h * 64:(h + 1) * 64],
                             t_fm[c][:, I * 512:(I + 1) * 512])
                            for c in range(DC)],
                           reads=wkk + tfk, writes=["B22"])
                    P.op("vector", lambda v, II=I, Kt=Kt: v.tensor_copy(
                        Kt[0:64, II * 512:(II + 1) * 512],
                        bank(PS2, 2)[0:64, :]),
                        reads=["B22"], writes=[("kt", hh, I)])
                P.op("vector", lambda v, Kt=Kt: v.memset(Kt[64:65, :], 1.0),
                     writes=[("kt1", hh)])
                for it in range(16):
                    W4 = it // 4 + 1
                    up = it % 2
                    PSS = PS1 if up == 0 else PS2
                    skey = (lambda j, up=up: ("B1", j) if up == 0
                            else ["B20", "B21", "B22", "B23"][j])
                    mtmp = mtmp_[up]; mcol = mcol_[up]
                    mb16 = mb16_[up]; negm = negm_[up]
                    expdump = expdump_[up]
                    for jb4 in range(W4):
                        P.op("tensor", lambda t, i_=it, j_=jb4, Q=Q, Kt=Kt,
                                    PSS=PSS:
                             t.matmul(bank(PSS, j_),
                                      Q[0:64, i_ * 128:(i_ + 1) * 128],
                                      Kt[0:64, j_ * 512:(j_ + 1) * 512],
                                      start=True, stop=True),
                             reads=qk_r[:4] + kk_r[:4],
                             waits=P.peek_war([skey(jb4)]),
                             writes=[skey(jb4)])
                    P.op("vector", lambda v, i_=it, j_=W4 - 1, PSS=PSS:
                         v.tensor_tensor(
                             bank(PSS, j_), bank(PSS, j_), maskIJ[i_ % 4][:],
                             ALU.add),
                         reads=[skey(W4 - 1), ("mIJ", it % 4)],
                         writes=[skey(W4 - 1)])
                    for jb4 in range(W4):
                        P.op("vector", lambda v, j_=jb4, PSS=PSS, mtmp=mtmp:
                             v.reduce_max(
                                 mtmp[:, j_:j_ + 1], bank(PSS, j_), axis=AX.X),
                             reads=[skey(jb4)], writes=[("mtmp", up, jb4)])
                    P.op("vector", lambda v, W=W4, mtmp=mtmp, mcol=mcol:
                         v.reduce_max(mcol[:], mtmp[:, 0:W], axis=AX.X),
                         reads=[("mtmp", up, j) for j in range(W4)],
                         writes=[("mcol", up)])
                    P.op("vector", lambda v, mb16=mb16, mcol=mcol:
                         v.tensor_copy(mb16[:], mcol[:]),
                         reads=[("mcol", up)], writes=[("mb16", up)])
                    P.op("vector", lambda v, i_=it, mb16=mb16: v.tensor_copy(
                        mcols[:, i_:i_ + 1], mb16[:]),
                        reads=[("mb16", up)], writes=[("mcols", it)])
                    P.op("vector", lambda v, i_=it, negm=negm: v.tensor_scalar(
                        negm[:], mcols[:, i_:i_ + 1], -1.0, None, ALU.mult),
                        reads=[("mcols", it)], writes=[("negm", up)])
                    P.op("scalar", lambda a, i_=it, W=W4, h_=h, PSS=PSS,
                                negm=negm, expdump=expdump: a.activation(
                        expdump[:, 0:W * 512], PSS[:, 0:W * 512], AF.Exp,
                        bias=negm[:], scale=1.0,
                        accum_out=sums[:, h_ * 16 + i_:h_ * 16 + i_ + 1]),
                        reads=[skey(j) for j in range(W4)] + [("negm", up)],
                        writes=[("expdump", up), ("sums", h, it)])
                P.op("tensor", lambda t: t.transpose(
                    PS2[0:16, 1536:1664], mcols[:], identf[:]),
                    reads=[("mcols", i_) for i_ in range(16)] + ["identf"],
                    writes=["B23"])
                P.op("vector", lambda v: v.tensor_copy(mrows16[:], PS2[0:16, 1536:1664]),
                     reads=["B23"], writes=["mrows16"])
                P.dma("q6", "sync", lambda e, u=hh: e.dma_start(
                    out=mrow[u][:], in_=mrows16[:]),
                    reads=["mrows16"], writes=[("mrowsb", hh)])
                for I in range(4):
                    P.op("tensor", lambda t, II=I, u=hh: t.matmul(
                        PS2[0:65, 1536:2048], erow[0:1, 0:65],
                        mrow[u][0:1, II * 512:(II + 1) * 512],
                        start=True, stop=True),
                        reads=[("mrowsb", hh), "erow"],
                        waits=P.peek_war(["B23"]), writes=["B23"])
                    P.op("vector", lambda v, II=I, Q=Q: v.tensor_scalar(
                        Q[64:65, II * 512:(II + 1) * 512],
                        PS2[64:65, 1536:2048], -1.0, None, ALU.mult),
                        reads=["B23"], writes=[("qtaug", hh)])
                for it in range(16):
                    P.op("vector", lambda v, i_=it, h_=h: v.reciprocal(
                        rs[:], sums[:, h_ * 16 + i_:h_ * 16 + i_ + 1]),
                        reads=[("sums", h, it)], writes=["rs"])
                    P.op("vector", lambda v, i_=it, h_=h: v.tensor_scalar(
                        ctmp[:, i_:i_ + 1],
                        gates[:, i_ * H + h_:i_ * H + h_ + 1],
                        rs[:], None, ALU.mult),
                        reads=["rs", ("gates", it)], writes=[("ctmp", it)])
                P.op("tensor", lambda t: t.transpose(
                    PS2[0:16, 1536:1664], ctmp[:], identf[:]),
                    reads=[("ctmp", i_) for i_ in range(16)] + ["identf"],
                    writes=["B23"])
                P.op("vector", lambda v: v.tensor_copy(mrows16[:], PS2[0:16, 1536:1664]),
                     reads=["B23"], writes=["mrows16"])
                P.dma("q6", "sync", lambda e, u=hh: e.dma_start(
                    out=crow[u][:], in_=mrows16[:]),
                    reads=["mrows16"], writes=[("crow", hh)])
            for I in range(4):
                nj = 4 * I + 4
                for hh in range(2):
                    h = 2 * pair + hh
                    Q, Kt = qt[hh], kt[hh]
                    qk_r = [("qt", hh, II) for II in range(4)] + [("qtaug", hh)]
                    kk_r = [("kt", hh, II) for II in range(4)] + [("kt1", hh)]
                    for jb in range(nj):
                        u2_ = jb % 2
                        pbank = bank(PS2, 0) if u2_ == 0 else bank(PS1, 0)
                        pkey = "B20" if u2_ == 0 else ("B1", 0)
                        P.op("tensor", lambda t, j_=jb, II=I, Q=Q, Kt=Kt,
                                    pb=pbank:
                             t.matmul(pb,
                                      Kt[:, j_ * 128:(j_ + 1) * 128],
                                      Q[:, II * 512:(II + 1) * 512],
                                      start=True, stop=True),
                             reads=kk_r + qk_r,
                             waits=P.peek_war([pkey]),
                             writes=[pkey])
                        dpos = jb - 4 * I
                        if dpos >= 0:
                            P.op("vector", lambda v, d_=dpos, pb=pbank:
                                 v.tensor_tensor(pb, pb, maskT[d_][:],
                                                 ALU.add),
                                 reads=[pkey, ("mT", dpos)],
                                 writes=[pkey])
                        P.op("scalar", lambda a, j_=jb, pb=pbank: a.activation(
                            p_T[j_][:], pb, AF.Exp),
                            reads=[pkey], writes=[("pT", jb)])
                    tp = (0, 0) if hh == 0 else (0, 64)
                    lo = 0 if hh == 0 else 64
                    pvw = P.peek_raw([("pT", j) for j in range(nj)]
                                     + [("v", j) for j in range(nj)]
                                     + [("v1", j) for j in range(nj)]) \
                        + P.peek_war(["B21"])
                    for jb in range(nj):
                        P.op("tensor",
                             lambda t, j_=jb, h_=h, tp=tp, lo=lo,
                                    st=(jb == 0), sp=(jb == nj - 1):
                             t.matmul(bank(PS2, 1)[lo:lo + 64, :],
                                      v_tm[j_][:, h_ * 65:h_ * 65 + 64],
                                      p_T[j_][:], start=st, stop=sp,
                                      tile_position=tp),
                             reads=([("pT", j) for j in range(nj)]
                                    if jb == nj - 1 else ()),
                             writes=(["B21"] if jb == nj - 1 else ()),
                             waits=(pvw if jb == 0 else ()))
                    P.op("tensor", lambda t, II=I, tp=tp, lo=lo, u=hh:
                         t.matmul(PS2[lo:lo + 64, 1024:1536],
                                  onesr[0:1, 0:64],
                                  crow[u][0:1, II * 512:(II + 1) * 512],
                                  start=True, stop=True, tile_position=tp),
                         reads=[("crow", hh), "onesr"],
                         waits=P.peek_war(["B22"]),
                         writes=["B22"])
                P.op("vector", lambda v: v.tensor_copy(cb2[:], bank(PS2, 2)),
                     reads=["B22"], writes=["cb2"])
                P.op("vector", lambda v, II=I, p_=pair: v.tensor_tensor(
                    outsc[p_][:, II * 512:(II + 1) * 512], bank(PS2, 1),
                    cb2[:], ALU.mult),
                    reads=["cb2", ("PSpv", 0), ("PSpv", 1)],
                    writes=[("outsc", pair, I)])
        osk = [("outsc", p_, I) for p_ in range(4) for I in range(4)]
        for tile in range(16):
            rr = s * N + tile * 128
            P.dma("q5", "sync", lambda e, r0=rr: e.dma_start(
                out=x0t[:], in_=x0[r0:r0 + 128, :]), writes=["x0t"])
            for dstrip in range(2):
                k_loop(P, bank(PS1, dstrip),
                       [(outsc[p_][:, tile * 128:(tile + 1) * 128],
                         wob[p_][:, dstrip * 512:(dstrip + 1) * 512])
                        for p_ in range(4)],
                       reads=osk + [("wo", p_) for p_ in range(4)],
                       writes=[("B1", dstrip)])
                P.op("vector", lambda v, ds=dstrip: v.tensor_tensor(
                    outt[:, ds * 512:(ds + 1) * 512], bank(PS1, ds),
                    x0t[:, ds * 512:(ds + 1) * 512], ALU.add),
                    reads=[("B1", dstrip), "x0t"], writes=[("outt", dstrip)])
            P.dma("qg", "gpsimd", lambda e, r0=rr: e.dma_start(
                out=x1p[r0:r0 + 128, :], in_=outt[:]),
                reads=[("outt", 0), ("outt", 1)])
    P.op("gpsimd", lambda g_: g_.engine_nop(), waits=[P.dma_token("qg")])
    return


def emit_ph3(nc, P, ctx, d, pfx="ph3_"):
    x1 = d["x1"]; lngc_d = d["lngc"]; lnbc_d = d["lnbc"]
    identb_d = d["identb"]; identf_d = d["identf"]
    xn1 = d["xn1"]; normfm = d["normfm"]; x2p = d["x2p"]; lm1 = d["lm1"]
    sb = lambda n, s, dt=F32: ctx.enter_context(nc.sbuf_tensor(pfx + n, s, dt))
    psn = lambda n, s, dt=F32: ctx.enter_context(nc.psum_tensor(pfx + n, s, dt))
    xt = [sb(f"xt{i}", [128, D]) for i in range(2)]
    sq_ = [sb(f"sq{u}", [128, D]) for u in range(2)]
    mcol_ = [sb(f"mcol{u}", [128, 1]) for u in range(2)]
    vcol_ = [sb(f"vcol{u}", [128, 1]) for u in range(2)]
    rstd_ = [sb(f"rstd{u}", [128, 1]) for u in range(2)]
    xnt_ = [sb(f"xnt{u}", [128, D]) for u in range(2)]
    u2_ = [sb(f"u2{u}", [128, D]) for u in range(2)]
    ub_ = [sb(f"ub{u}", [128, D], BF16) for u in range(2)]
    x2t_ = [sb(f"x2t{u}", [128, D]) for u in range(2)]
    tmax_ = [sb(f"tmax{u}", [128, 1]) for u in range(2)]
    idb = sb("idb", [128, 128], BF16)
    idf = sb("idf", [128, 128], F32)
    lngc = sb("s_lngc", [128, DC]); lnbc = sb("s_lnbc", [128, DC])
    fm = [sb(f"fm{c}", [128, R], BF16) for c in range(DC)]
    runmax = sb("runmax", [128, 1])
    mrow_s = sb("mrow_s", [1, 128]); mone = sb("mone", [1, 1])
    pstr_ = [psn(f"pstr{u}", [128, 128], BF16) for u in range(2)]
    ps_r = psn("ps_r", [1, 128], F32)
    K = ConstBank(P, nc, ctx, pfx=pfx)
    b_ln = K.get(EPS_LN); k_ln = K.key(EPS_LN)

    P.dma("qs", "sync", lambda e: e.dma_start(out=idb[:], in_=identb_d),
          writes=["idb"])
    P.dma("qs", "sync", lambda e: e.dma_start(out=idf[:], in_=identf_d),
          writes=["idf"])
    P.dma("qs", "sync", lambda e: e.dma_start(out=lngc[:], in_=lngc_d),
          writes=["lngc"])
    P.dma("qs", "sync", lambda e: e.dma_start(out=lnbc[:], in_=lnbc_d),
          writes=["lnbc"])
    P.op("vector", lambda v: v.memset(runmax[:], 0.0), writes=["runmax"])

    def ln_pass(s, src_key, srcb, dst, dst_key):
        sq = sq_[s]; mcol = mcol_[s]; vcol = vcol_[s]; rstd = rstd_[s]
        _k = lambda nm: (nm, s)
        P.op("vector", lambda v: v.reduce_sum(mcol[:], srcb[:], axis=AX.X),
             reads=[src_key], writes=[_k("mcol")])
        P.op("vector", lambda v: v.tensor_scalar(
            mcol[:], mcol[:], 1.0 / D, None, ALU.mult),
            reads=[_k("mcol")], writes=[_k("mcol")])
        P.op("vector", lambda v: v.tensor_scalar(
            dst[:], srcb[:], mcol[:], None, ALU.subtract),
            reads=[src_key, _k("mcol")], writes=[dst_key])
        P.op("scalar", lambda a: a.activation(sq[:], dst[:], AF.Square),
             reads=[dst_key], writes=[_k("sq")])
        P.op("vector", lambda v: v.reduce_sum(vcol[:], sq[:], axis=AX.X),
             reads=[_k("sq")], writes=[_k("vcol")])
        P.op("scalar", lambda a: a.activation(
            rstd[:], vcol[:], AF.Sqrt, bias=b_ln, scale=1.0 / D),
            reads=[_k("vcol"), k_ln], writes=[_k("rstd")])
        P.op("vector", lambda v: v.reciprocal(rstd[:], rstd[:]),
             reads=[_k("rstd")], writes=[_k("rstd")])
        P.op("vector", lambda v: v.tensor_scalar(
            dst[:], dst[:], rstd[:], None, ALU.mult),
            reads=[dst_key, _k("rstd")], writes=[dst_key])

    def tile(r, s):
        b = xt[s]; xnt = xnt_[s]; u2 = u2_[s]; ub = ub_[s]
        x2t = x2t_[s]; tmax = tmax_[s]; pstr = pstr_[s]
        _k = lambda nm: (nm, s)
        P.dma("qs" if s == 0 else "qa", "sync",
              lambda e: e.dma_start(out=b[:], in_=x1[r * 128:(r + 1) * 128, :]),
              writes=[f"xt{s}"])
        ln_pass(s, f"xt{s}", b, xnt, _k("xnt"))
        P.dma("qg", "gpsimd", lambda e: e.dma_start(
            out=xn1[r * 128:(r + 1) * 128, :], in_=xnt[:]), reads=[_k("xnt")])
        P.op("gpsimd", lambda g_: g_.tensor_add(
            x2t[:], b[:], xnt[:]),
            reads=[f"xt{s}", _k("xnt")], writes=[_k("x2t")])
        P.dma("qg", "gpsimd", lambda e: e.dma_start(
            out=x2p[r * 128:(r + 1) * 128, :], in_=x2t[:]), reads=[_k("x2t")])
        P.op("vector", lambda v: v.reduce_max(
            tmax[:], xnt[:], axis=AX.X, apply_absolute_value=True),
            reads=[_k("xnt")], writes=[_k("tmax")])
        P.op("vector", lambda v: v.tensor_tensor(
            runmax[:], runmax[:], tmax[:], ALU.max),
            reads=["runmax", _k("tmax")], writes=["runmax"])
        ln_pass(s, _k("xnt"), xnt, u2, _k("u2"))
        P.op("scalar", lambda a: a.copy(ub[:], u2[:]),
             reads=[_k("u2")], writes=[_k("ub")])
        for c in range(DC):
            P.op("tensor", lambda t, cc=c: t.transpose(
                pstr[:], ub[:, cc * 128:(cc + 1) * 128], idb[:]),
                reads=[_k("ub"), "idb"], writes=[_k("pstr")])
            P.op("vector", lambda v, cc=c: v.tensor_scalar(
                fm[cc][:, r * 128:(r + 1) * 128], pstr[:],
                lngc[:, cc:cc + 1], lnbc[:, cc:cc + 1], ALU.mult, ALU.add),
                reads=[_k("pstr"), "lngc", "lnbc"], writes=[("fm", c, r)])

    for r in range(R // 128):
        tile(r, r % 2)
    P.op("tensor", lambda t: t.matmul(
        ps_r[:], runmax[:, 0:1], idf[:], start=True, stop=True),
        reads=["runmax", "idf"], writes=["ps_r"])
    P.op("vector", lambda v: v.tensor_copy(mrow_s[:], ps_r[:]),
         reads=["ps_r"], writes=["mrow_s"])
    P.op("vector", lambda v: v.reduce_max(mone[:], mrow_s[:], axis=AX.X),
         reads=["mrow_s"], writes=["mone"])
    P.dma("qg", "gpsimd", lambda e: e.dma_start(out=lm1, in_=mone[:]),
          reads=["mone"], writes=[("lm1_hbm",)])
    for c in range(DC):
        P.dma("qg", "gpsimd",
              lambda e, cc=c: e.dma_start(out=normfm[cc], in_=fm[cc][:]),
              reads=[("fm", c, r) for r in range(R // 128)])
    P.op("gpsimd", lambda g_: g_.engine_nop(), waits=[P.dma_token("qg")])
    return

def emit_ph4(nc, P, ctx, d, pfx="ph4_"):
    normfm = d["normfm"]; Whb = d["Whb"]; Wqk_d = d["Wqk"]
    bhv_row = d["bhv"]; bhg_c = d["bhgc"]; bqk_c = d["bqkc"]
    gb_d = d["gb"]; Wob_d = d["Wob"]; bo_row = d["bor"]
    x2p = d["x2p"]; maskT_d = d["maskT"]; ones_d = d["onesrow"]
    x2q = d["x2q"]; g_scr = d["g_scr"]; gd_scr = d["gd_scr"]

    sb = lambda n, s, dt=F32: ctx.enter_context(nc.sbuf_tensor(pfx + n, s, dt))
    psn = lambda n, s, dt=F32: ctx.enter_context(nc.psum_tensor(pfx + n, s, dt))
    nrm = [sb(f"nrm{c}", [128, N], BF16) for c in range(DC)]
    whs = [sb(f"whs{c}", [128, 512], BF16) for c in range(DC)]
    wgate_ = [sb(f"wgate{u}", [128, DC * 128], BF16) for u in range(2)]
    wqkb = [sb(f"wqkb{c}", [128, QK], BF16) for c in range(DC)]
    wob = [sb(f"wob{c}", [128, D], BF16) for c in range(HC)]
    qv = sb("qv", [128, N], BF16)
    kv = sb("kv", [128, N], BF16)
    gbc = sb("s_gb", [128, 4])
    bqkc = sb("s_bqk", [128, 1])
    bhgc = sb("s_bhg", [128, HC])
    bhvb = sb("bhvb", [128, HID])
    bob = sb("bob", [128, D])
    onesr = sb("onesr", [1, 128])
    maskT = [sb(f"mT{v}", [128, 512], BF16) for v in range(4)]
    v_tm = [sb(f"v_tm{j}", [128, HID], BF16) for j in range(16)]
    p_T = [sb(f"p_T{j}", [128, 512], BF16) for j in range(16)]
    rsb_ = [sb(f"rsb{u}", [128, 512]) for u in range(2)]
    gsl_ = [sb(f"gsl{u}", [128, 512], BF16) for u in range(2)]
    gout_ = [sb(f"gout{u}", [128, 512], BF16) for u in range(2)]
    gtile_ = [sb(f"gtile{u}", [128, HC * 128], BF16) for u in range(2)]
    x2t = sb("x2t", [128, D])
    outt = sb("outt", [128, D])
    PS1 = psn("PS1", [128, 2048])
    PS2 = psn("PS2", [128, 2048])
    bank = lambda ps, k: ps[:, k * 512:(k + 1) * 512]

    P.dma("qs", "sync", lambda e: e.dma_start(out=gbc[:], in_=gb_d),
          writes=["gbc"])
    P.dma("qs", "sync", lambda e: e.dma_start(out=bqkc[:], in_=bqk_c),
          writes=["bqkc"])
    P.dma("qs", "sync", lambda e: e.dma_start(out=bhgc[:], in_=bhg_c),
          writes=["bhgc"])
    P.dma("qs", "sync", lambda e: e.dma_start(out=onesr[:], in_=ones_d),
          writes=["onesr"])
    for v_ in range(4):
        P.dma("qs", "sync", lambda e, vv=v_: e.dma_start(
            out=maskT[vv][:], in_=maskT_d[vv]), writes=[("mT", v_)])
    for c in range(DC):
        P.dma("qs", "sync", lambda e, cc=c: e.dma_start(
            out=x2t[:, 0:QK], in_=Wqk_d[cc * 128:(cc + 1) * 128, :]),
            writes=["x2t"])
        P.op("vector", lambda v, cc=c: v.tensor_copy(
            wqkb[cc][:], x2t[:, 0:QK]), reads=["x2t"], writes=[("wqk", c)])
    for c in range(HC):
        P.dma("qs", "sync", lambda e, cc=c: e.dma_start(
            out=wob[cc][:], in_=Wob_d[cc * 128:(cc + 1) * 128, :]),
            writes=[("wob", c)])
    P.dma("qs", "sync", lambda e: e.dma_start(out=bhvb[0:1, :], in_=bhv_row),
          writes=["bhv_r"])
    P.dma("qs", "sync", lambda e: e.dma_start(out=bob[0:1, :], in_=bo_row),
          writes=["bo_r"])
    for k in range(4):
        P.op("tensor", lambda t, kk=k: t.matmul(
            bank(PS2, 2), onesr[:], bhvb[0:1, kk * 512:(kk + 1) * 512],
            start=True, stop=True),
            reads=["bhv_r", "onesr"], waits=P.peek_war(["B22"]),
            writes=["B22"])
        P.op("vector", lambda v, kk=k: v.tensor_copy(
            bhvb[:, kk * 512:(kk + 1) * 512], bank(PS2, 2)),
            reads=["B22"], writes=[("bhvb", k)])
    for k in range(2):
        P.op("tensor", lambda t, kk=k: t.matmul(
            bank(PS2, 2), onesr[:], bob[0:1, kk * 512:(kk + 1) * 512],
            start=True, stop=True),
            reads=["bo_r", "onesr"], waits=P.peek_war(["B22"]),
            writes=["B22"])
        P.op("vector", lambda v, kk=k: v.tensor_copy(
            bob[:, kk * 512:(kk + 1) * 512], bank(PS2, 2)),
            reads=["B22"], writes=[("bob", k)])

    nk = [("nrm", c) for c in range(DC)]
    for s in range(SEQ):
        for c in range(DC):
            P.dma("qs", "sync", lambda e, cc=c, ss=s: e.dma_start(
                out=nrm[cc][:], in_=normfm[cc][:, ss * N:(ss + 1) * N]),
                writes=[("nrm", c)])
        for I in range(4):
            k_loop(P, bank(PS1, 2),
                   [(wqkb[c][:], nrm[c][:, I * 512:(I + 1) * 512])
                    for c in range(DC)],
                   reads=[("wqk", c) for c in range(DC)] + nk,
                   writes=[("B12", 0)])
            P.op("scalar", lambda a, II=I: a.activation(
                qv[:, II * 512:(II + 1) * 512], bank(PS1, 2), AF.Silu,
                bias=bqkc[:]),
                reads=[("B12", 0), "bqkc"], writes=[("qks", I)])
        qkk = [("qks", I) for I in range(4)]
        P.op("vector", lambda v: v.tensor_scalar(
            kv[:], qv[:], gbc[:, 1:2], gbc[:, 3:4], ALU.mult, ALU.add),
            reads=qkk + ["gbc"], writes=["kv"])
        P.op("vector", lambda v: v.tensor_scalar(
            qv[:], qv[:], gbc[:, 0:1], gbc[:, 2:3], ALU.mult, ALU.add),
            reads=qkk + ["gbc", "kv"], writes=["qv"] + qkk)
        for vs in range(4):
            P.dma("qa", "sync", lambda e, v0=vs: e.dma_start(
                out=whs[0][:].tensor.full_ap() if False else whs[0][:],
                in_=Whb[0:128, v0 * 512:(v0 + 1) * 512]),
                writes=[("whs", 0)])
            for c in range(1, DC):
                P.dma("qa", "sync", lambda e, cc=c, v0=vs: e.dma_start(
                    out=whs[cc][:],
                    in_=Whb[cc * 128:(cc + 1) * 128, v0 * 512:(v0 + 1) * 512]),
                    writes=[("whs", c)])
            for jt in range(16):
                u = jt % 2
                u4 = jt % 4
                pb = bank(PS1, 2 + u) if u4 < 2 else bank(PS2, u)
                pk = ("B12", u) if u4 < 2 else ("B20", u)
                k_loop(P, pb,
                       [(nrm[c][:, jt * 128:(jt + 1) * 128], whs[c][:])
                        for c in range(DC)],
                       reads=nk + [("whs", c) for c in range(DC)],
                       writes=[pk])
                P.op("vector", lambda v, v0=vs, u=u, pb=pb, pk=pk:
                     v.tensor_tensor(
                         rsb_[u][:], pb, bhvb[:, v0 * 512:(v0 + 1) * 512],
                         ALU.add),
                     reads=[pk, ("bhvb", vs)], writes=[("rsb", u)])
                P.op("scalar", lambda a, j=jt, v0=vs, u=u: a.activation(
                    v_tm[j][:, v0 * 512:(v0 + 1) * 512], rsb_[u][:], AF.Silu),
                    reads=[("rsb", u)], writes=[("v", jt, vs)])
        for hc in range(HC):
            ug = hc % 2
            wg_t = wgate_[ug]
            P.dma("qa", "sync", lambda e, h0=hc, w_=wg_t: e.dma_start(
                out=w_[:].rearrange("p (c n) -> p c n", c=DC),
                in_=Whb[:, HID + h0 * 128:HID + (h0 + 1) * 128].rearrange(
                    "(c p) n -> p c n", p=128)),
                writes=[("wgate", ug)])
            for I in range(4):
                u = I % 2
                pb = bank(PS1, 2 + u) if I < 2 else bank(PS2, u)
                pk = ("B12", u) if I < 2 else ("B20", u)
                k_loop(P, pb,
                       [(wg_t[:, c * 128:(c + 1) * 128],
                         nrm[c][:, I * 512:(I + 1) * 512])
                        for c in range(DC)],
                       reads=nk + [("wgate", ug)],
                       writes=[pk])
                P.op("scalar", lambda a, h0=hc, u=u, pb=pb, pk=pk: a.activation(
                    gsl_[u][:], pb, AF.Silu, bias=bhgc[:, h0:h0 + 1]),
                    reads=[pk, "bhgc"], writes=[("gsl", u)])
                P.dma("qg", "gpsimd", lambda e, h0=hc, II=I, u=u: e.dma_start(
                    out=g_scr[h0][:, II * 512:(II + 1) * 512], in_=gsl_[u][:]),
                    reads=[("gsl", u)], writes=[("gscr", hc, I)])
        for I in range(4):
            nj = 4 * I + 4
            for jb in range(nj):
                u = jb % 2
                pb = bank(PS2, u)
                P.op("tensor", lambda t, j_=jb, II=I, pb=pb: t.matmul(
                    pb, kv[:, j_ * 128:(j_ + 1) * 128],
                    qv[:, II * 512:(II + 1) * 512], start=True, stop=True),
                    reads=["kv", "qv"], waits=P.peek_war([("B20", u)]),
                    writes=[("B20", u)])
                dpos = jb - 4 * I
                if dpos >= 0:
                    P.op("vector", lambda v, d_=dpos, pb=pb: v.tensor_tensor(
                        pb, pb, maskT[d_][:], ALU.add),
                        reads=[("B20", u), ("mT", dpos)], writes=[("B20", u)])
                P.op("scalar", lambda a, u=u, pb=pb: a.activation(
                    rsb_[u][:], pb, AF.Relu),
                    reads=[("B20", u)], writes=[("rsb", u)])
                P.op("scalar", lambda a, j_=jb, u=u: a.activation(
                    p_T[j_][:], rsb_[u][:], AF.Square),
                    reads=[("rsb", u)], writes=[("pT", jb)])
            vtk = [("v", jb, vs) for jb in range(nj) for vs in range(4)]
            ptk = [("pT", jb) for jb in range(nj)]
            for hc in range(HC):
                u = hc % 2
                pb = bank(PS1, u)
                k_loop(P, pb,
                       [(v_tm[jb][:, hc * 128:(hc + 1) * 128], p_T[jb][:])
                        for jb in range(nj)],
                       reads=vtk + ptk, writes=[("B1", u)])
                P.dma("qa", "sync", lambda e, h0=hc, II=I, u=u: e.dma_start(
                    out=gsl_[u][:], in_=g_scr[h0][:, II * 512:(II + 1) * 512]),
                    reads=[("gscr", hc, I)], writes=[("gsl", u)])
                P.op("vector", lambda v, u=u, pb=pb: v.scalar_tensor_tensor(
                    gout_[u][:], pb, 1.0 / (2048.0 * 2048.0),
                    gsl_[u][:], ALU.mult, ALU.mult),
                    reads=[("B1", u), ("gsl", u)], writes=[("gout", u)])
                P.dma("qg", "gpsimd", lambda e, h0=hc, II=I, u=u: e.dma_start(
                    out=gd_scr[h0][:, II * 512:(II + 1) * 512],
                    in_=gout_[u][:]), reads=[("gout", u)],
                    writes=[("gd", hc, I)])
        for tile in range(16):
            rr = s * N + tile * 128
            ut = tile % 2
            gt_t = gtile_[ut]
            P.dma("q5", "sync", lambda e, t0=tile, g_=gt_t: e.dma_start(
                out=g_[:].rearrange("p (h n) -> p h n", h=HC),
                in_=gd_scr.rearrange("h p n -> p h n")[
                    :, :, t0 * 128:(t0 + 1) * 128]),
                reads=[("gd", hc, tile // 4) for hc in range(HC)],
                writes=[("gt", ut)])
            P.dma("q6", "gpsimd", lambda e, r0=rr: e.dma_start(
                out=x2t[:], in_=x2p[r0:r0 + 128, :]), writes=["x2t"])
            for ds in range(2):
                k_loop(P, bank(PS2, 2 + ds),
                       [(gt_t[:, hc * 128:(hc + 1) * 128],
                         wob[hc][:, ds * 512:(ds + 1) * 512])
                        for hc in range(HC)],
                       reads=[("gt", ut)]
                       + [("wob", hc) for hc in range(HC)],
                       writes=["B22" if ds == 0 else "B23"])
                P.op("vector", lambda v, d_=ds: v.tensor_tensor(
                    outt[:, d_ * 512:(d_ + 1) * 512], bank(PS2, 2 + d_),
                    bob[:, d_ * 512:(d_ + 1) * 512], ALU.add),
                    reads=[("B22" if ds == 0 else "B23"), ("bob", ds)],
                    writes=[("outt", ds)])
                P.op("vector", lambda v, d_=ds: v.tensor_tensor(
                    outt[:, d_ * 512:(d_ + 1) * 512],
                    outt[:, d_ * 512:(d_ + 1) * 512],
                    x2t[:, d_ * 512:(d_ + 1) * 512], ALU.add),
                    reads=[("outt", ds), "x2t"], writes=[("outt", ds)])
            P.dma("qg", "gpsimd", lambda e, r0=rr: e.dma_start(
                out=x2q[r0:r0 + 128, :], in_=outt[:]),
                reads=[("outt", 0), ("outt", 1)])
    P.op("gpsimd", lambda g_: g_.engine_nop(), waits=[P.dma_token("qg")])
    return

def P_vscr_keys(s, j, hg):
    return [("vscr", j, vs) for vs in range(4)]


def _emit_scalar_prep(P, nc, ctx, maxes, clip, onesr_key, onesr, PSb, psb_key,
                      pfx=""):
    sb = lambda n, s_, dt=F32: ctx.enter_context(
        nc.sbuf_tensor(pfx + n, s_, dt))
    mx8 = sb("mx8", [1, 8])
    clp = sb("clp", [1, 1])
    am = sb("am", [1, 1])
    sg = sb("sg", [1, 1])
    sv = sb("sv", [1, 1])
    rv = sb("rv", [1, 1])
    scol = sb("scol_b", [128, 1])
    rcol = sb("rcol_b", [128, 1])
    P.dma("qs", "sync", lambda e: e.dma_start(out=mx8[:], in_=maxes),
          writes=["mx8"])
    P.dma("qs", "sync", lambda e: e.dma_start(out=clp[:], in_=clip),
          writes=["clp"])
    P.op("vector", lambda v: v.reduce_max(am[:], mx8[:], axis=AX.X),
         reads=["mx8"], writes=["am"])
    P.op("scalar", lambda a: a.activation(sg[:], clp[:], AF.Sigmoid),
         reads=["clp"], writes=["sg"])
    P.op("vector", lambda v: v.tensor_tensor(sv[:], sg[:], am[:], ALU.mult),
         reads=["sg", "am"], writes=["sv"])
    P.op("vector", lambda v: v.tensor_scalar(
        sv[:], sv[:], 1.0 / QMAX, None, ALU.mult), reads=["sv"], writes=["sv"])
    P.op("vector", lambda v: v.reciprocal(rv[:], sv[:]),
         reads=["sv"], writes=["rv"])
    for (src, dst, kk) in ((sv, scol, "scol_b"), (rv, rcol, "rcol_b")):
        P.op("tensor", lambda t, s_=src: t.matmul(
            PSb[:, 0:1], onesr[:], s_[:], start=True, stop=True),
            reads=[("sv" if src is sv else "rv"), onesr_key],
            waits=P.peek_war([psb_key]), writes=[psb_key])
        P.op("vector", lambda v, d_=dst: v.tensor_copy(d_[:], PSb[:, 0:1]),
             reads=[psb_key], writes=[kk])
    return scol, rcol


def emit_ph5a(nc, P, ctx, d, pfx="ph5a_"):

    xn1 = d["xn1"]
    maxes = d["maxes"]
    clipx = d["clipx"]
    wgTi = d["wgTi"]
    wuTi = d["wuTi"]
    sgc_d = d["sgc"]
    suc_d = d["suc"]
    identb_d = d["identb"]
    identf_d = d["identf"]
    ones_d = d["onesrow"]
    hmid = d["hmid"]
    lm2 = d["lm2"]
    nfm_scr = d["nfm_scr"]

    sb = lambda n, s, dt=F32: ctx.enter_context(nc.sbuf_tensor(pfx + n, s, dt))
    psn = lambda n, s, dt=F32: ctx.enter_context(nc.psum_tensor(pfx + n, s, dt))
    PS2 = psn("PS2", [128, 2048])
    bank = lambda ps, k: ps[:, k * 512:(k + 1) * 512]
    onesr = sb("onesr", [1, 128])
    P.dma("qs", "sync", lambda e: e.dma_start(out=onesr[:], in_=ones_d),
          writes=["onesr"])
    scol, rcol = _emit_scalar_prep(P, nc, ctx, maxes, clipx, "onesr",
                                   onesr, PS2[:, 1536:2048], "B23", pfx)
    idb = sb("idb", [128, 128], BF16)
    idf = sb("idf", [128, 128], F32)
    P.dma("qs", "sync", lambda e: e.dma_start(out=idb[:], in_=identb_d),
          writes=["idb"])
    P.dma("qs", "sync", lambda e: e.dma_start(out=idf[:], in_=identf_d),
          writes=["idf"])
    sgc = sb("s_sgc", [128, IC1]); suc = sb("s_suc", [128, IC1])
    P.dma("qs", "sync", lambda e: e.dma_start(out=sgc[:], in_=sgc_d),
          writes=["sgc0"])
    P.dma("qs", "sync", lambda e: e.dma_start(out=suc[:], in_=suc_d),
          writes=["suc0"])
    P.op("vector", lambda v: v.tensor_scalar(
        sgc[:], sgc[:], scol[:], None, ALU.mult),
        reads=["sgc0", "scol_b"], writes=["sgc"])
    P.op("vector", lambda v: v.tensor_scalar(
        suc[:], suc[:], scol[:], None, ALU.mult),
        reads=["suc0", "scol_b"], writes=["suc"])

    xt = sb("xt", [128, D])
    vq = sb("vq", [128, D])
    vqb = sb("vqb", [128, D], BF16)
    ntile = sb("ntile", [128, 128], BF16)
    pstr = psn("pstr", [128, 128], BF16)
    for r in range(R // 128):
        P.dma("q5", "sync", lambda e, r0=r * 128: e.dma_start(
            out=xt[:], in_=xn1[r0:r0 + 128, :]), writes=["xt"])
        P.op("vector", lambda v: v.tensor_scalar(
            vq[:], xt[:], rcol[:], None, ALU.mult),
            reads=["xt", "rcol_b"], writes=["vq"])
        P.op("vector", lambda v: v.tensor_scalar(
            vq[:], vq[:], MAGIC, None, ALU.add), reads=["vq"], writes=["vq"])
        P.op("vector", lambda v: v.tensor_scalar(
            vq[:], vq[:], MAGIC, None, ALU.subtract),
            reads=["vq"], writes=["vq"])
        P.op("vector", lambda v: v.tensor_scalar(
            vq[:], vq[:], -QMAX - 1.0, QMAX, ALU.max, ALU.min),
            reads=["vq"], writes=["vq"])
        P.op("vector", lambda v: v.tensor_copy(vqb[:], vq[:]),
             reads=["vq"], writes=["vqb"])
        for c in range(DC):
            P.op("tensor", lambda t, cc=c: t.transpose(
                pstr[:], vqb[:, cc * 128:(cc + 1) * 128], idb[:]),
                reads=["vqb", "idb"], writes=["pstr"])
            P.op("vector", lambda v: v.tensor_copy(ntile[:], pstr[:]),
                 reads=["pstr"], writes=["ntile"])
            P.dma("q6", "gpsimd", lambda e, cc=c, r0=r * 128: e.dma_start(
                out=nfm_scr[cc][:, r0:r0 + 128], in_=ntile[:]),
                reads=["ntile"], writes=[("nfm", c, r)])
    nst = [sb(f"nst{c}", [128, 512], BF16) for c in range(DC)]
    wgs = [sb(f"wgs{c}", [128, I1], BF16) for c in range(DC)]
    wus = [sb(f"wus{c}", [128, I1], BF16) for c in range(DC)]
    sil_ = [sb(f"sil{u}", [128, 512]) for u in range(2)]
    upv_ = [sb(f"upv{u}", [128, 512]) for u in range(2)]
    hm_ = [sb(f"hm{u}", [128, 512]) for u in range(2)]
    runmax = sb("runmax", [128, 1])
    tmax_ = [sb(f"tmax{u}", [128, 1]) for u in range(2)]
    mrow_s = sb("mrow_s", [1, 128]); mone = sb("mone", [1, 1])
    ps_r = psn("ps_r", [1, 128], F32)
    P.op("vector", lambda v: v.memset(runmax[:], 0.0), writes=["runmax"])
    for c in range(DC):
        P.dma("qa", "sync", lambda e, cc=c: e.dma_start(
            out=wgs[cc][:], in_=wgTi[cc * 128:(cc + 1) * 128, :]),
            writes=[("wgs", c)])
        P.dma("qa", "sync", lambda e, cc=c: e.dma_start(
            out=wus[cc][:], in_=wuTi[cc * 128:(cc + 1) * 128, :]),
            writes=[("wus", c)])
    for st in range(8):
        for c in range(DC):
            P.dma("qs", "sync", lambda e, cc=c, s0=st * 512: e.dma_start(
                out=nst[cc][:], in_=nfm_scr[cc][:, s0:s0 + 512]),
                reads=[("nfm", c, rr) for rr in range(st * 4, st * 4 + 4)],
                writes=[("nst", c)])
        for ic in range(IC1):
            u = ic % 2
            sil = sil_[u]; upv = upv_[u]; hm = hm_[u]
            psg = bank(PS2, u)
            psu = bank(PS2, 2 + u)
            k_loop(P, psg,
                   [(wgs[c][:, ic * 128:(ic + 1) * 128], nst[c][:])
                    for c in range(DC)],
                   reads=[("wgs", c) for c in range(DC)]
                   + [("nst", c) for c in range(DC)], writes=[("B2", u)])
            k_loop(P, psu,
                   [(wus[c][:, ic * 128:(ic + 1) * 128], nst[c][:])
                    for c in range(DC)],
                   reads=[("wus", c) for c in range(DC)]
                   + [("nst", c) for c in range(DC)], writes=[("B2", 2 + u)])
            P.op("scalar", lambda a_, i0=ic, sil=sil, psg=psg: a_.activation(
                sil[:], psg, AF.Silu, scale=sgc[:, i0:i0 + 1]),
                reads=[("B2", u), "sgc"], writes=[("sil", u)])
            P.op("vector", lambda v, i0=ic, upv=upv, psu=psu: v.tensor_scalar(
                upv[:], psu, suc[:, i0:i0 + 1], None, ALU.mult),
                reads=[("B2", 2 + u), "suc"], writes=[("upv", u)])
            P.op("vector", lambda v, hm=hm, sil=sil, upv=upv: v.tensor_tensor(
                hm[:], sil[:], upv[:], ALU.mult),
                reads=[("sil", u), ("upv", u)], writes=[("hm", u)])
            P.op("vector", lambda v, hm=hm, u=u: v.reduce_max(
                tmax_[u][:], hm[:], axis=AX.X, apply_absolute_value=True),
                reads=[("hm", u)], writes=[("tmax", u)])
            P.op("vector", lambda v, u=u: v.tensor_tensor(
                runmax[:], runmax[:], tmax_[u][:], ALU.max),
                reads=["runmax", ("tmax", u)], writes=["runmax"])
            P.dma("qg", "gpsimd", lambda e, i0=ic, s0=st * 512, hm=hm:
                  e.dma_start(out=hmid[i0][:, s0:s0 + 512], in_=hm[:]),
                  reads=[("hm", u)])
    P.op("tensor", lambda t: t.matmul(
        ps_r[:], runmax[:, 0:1], idf[:], start=True, stop=True),
        reads=["runmax", "idf"], writes=["ps_r"])
    P.op("vector", lambda v: v.tensor_copy(mrow_s[:], ps_r[:]),
         reads=["ps_r"], writes=["mrow_s"])
    P.op("vector", lambda v: v.reduce_max(mone[:], mrow_s[:], axis=AX.X),
         reads=["mrow_s"], writes=["mone"])
    P.dma("qg", "gpsimd", lambda e: e.dma_start(out=lm2, in_=mone[:]),
          reads=["mone"], writes=[("lm2_hbm",)])
    P.op("gpsimd", lambda g_: g_.engine_nop(), waits=[P.dma_token("qg")])
    return


def emit_ph5b(nc, P, ctx, d, pfx="ph5b_"):

    hmid = d["hmid"]
    maxes = d["maxes"]
    clipd = d["clipd"]
    wdTq_d = d["wdTq"]
    x2q = d["x2q"]
    ones_d = d["onesrow"]
    OUT = d["OUT"]

    sb = lambda n, s, dt=F32: ctx.enter_context(nc.sbuf_tensor(pfx + n, s, dt))
    psn = lambda n, s, dt=F32: ctx.enter_context(nc.psum_tensor(pfx + n, s, dt))
    PS2 = psn("PS2", [128, 2048])
    bank = lambda ps, k: ps[:, k * 512:(k + 1) * 512]
    onesr = sb("onesr", [1, 128])
    P.dma("qs", "sync", lambda e: e.dma_start(out=onesr[:], in_=ones_d),
          writes=["onesr"])
    scol, rcol = _emit_scalar_prep(P, nc, ctx, maxes, clipd, "onesr",
                                   onesr, PS2[:, 1536:2048], "B23", pfx)
    wdq = [sb(f"wdq{i}", [128, D], BF16) for i in range(IC1)]
    for i in range(IC1):
        P.dma("qs", "sync", lambda e, i0=i: e.dma_start(
            out=wdq[i0][:], in_=wdTq_d[i0 * 128:(i0 + 1) * 128, :]),
            writes=[("wdq", i)])
    hl = [sb(f"hl{u}", [128, 512]) for u in range(2)]
    mfm = [sb(f"mfm{i}", [128, 512], BF16) for i in range(IC1)]
    x2t = sb("x2t", [128, D])
    outt = sb("outt", [128, D])
    for st in range(8):
        for ic in range(IC1):
            u = ic % 2
            P.dma("qa" if u == 0 else "qv", "sync",
                  lambda e, i0=ic, s0=st * 512, b=hl[u]:
                  e.dma_start(out=b[:], in_=hmid[i0][:, s0:s0 + 512]),
                  writes=[("hl", u)])
            P.op("vector", lambda v, b=hl[u]: v.tensor_scalar(
                b[:], b[:], rcol[:], None, ALU.mult),
                reads=[("hl", u), "rcol_b"], writes=[("hl", u)])
            P.op("vector", lambda v, b=hl[u]: v.tensor_scalar(
                b[:], b[:], MAGIC, None, ALU.add),
                reads=[("hl", u)], writes=[("hl", u)])
            P.op("vector", lambda v, b=hl[u]: v.tensor_scalar(
                b[:], b[:], MAGIC, None, ALU.subtract),
                reads=[("hl", u)], writes=[("hl", u)])
            P.op("vector", lambda v, b=hl[u]: v.tensor_scalar(
                b[:], b[:], -QMAX - 1.0, QMAX, ALU.max, ALU.min),
                reads=[("hl", u)], writes=[("hl", u)])
            P.op("vector", lambda v, b=hl[u], i0=ic: v.tensor_copy(
                mfm[i0][:], b[:]), reads=[("hl", u)], writes=[("mfm", ic)])
        for tile in range(4):
            rr = st * 512 + tile * 128
            P.dma("q5", "sync", lambda e, r0=rr: e.dma_start(
                out=x2t[:], in_=x2q[r0:r0 + 128, :]), writes=["x2t"])
            for ds in range(2):
                k_loop(P, bank(PS2, ds),
                       [(mfm[ic][:, tile * 128:(tile + 1) * 128],
                         wdq[ic][:, ds * 512:(ds + 1) * 512])
                        for ic in range(IC1)],
                       reads=[("mfm", ic) for ic in range(IC1)]
                       + [("wdq", ic) for ic in range(IC1)],
                       writes=[("B2", ds)])
                P.op("vector", lambda v, d_=ds: v.tensor_scalar(
                    outt[:, d_ * 512:(d_ + 1) * 512], bank(PS2, d_),
                    scol[:], None, ALU.mult),
                    reads=[("B2", ds), "scol_b"], writes=[("outt", ds)])
                P.op("vector", lambda v, d_=ds: v.tensor_tensor(
                    outt[:, d_ * 512:(d_ + 1) * 512],
                    outt[:, d_ * 512:(d_ + 1) * 512],
                    x2t[:, d_ * 512:(d_ + 1) * 512], ALU.add),
                    reads=[("outt", ds), "x2t"], writes=[("outt", ds)])
            P.dma("qg", "gpsimd", lambda e, r0=rr: e.dma_start(
                out=OUT[r0:r0 + 128, :], in_=outt[:]),
                reads=[("outt", 0), ("outt", 1)])
    P.op("gpsimd", lambda g_: g_.engine_nop(), waits=[P.dma_token("qg")])
    return



def build_fused():
    nc = bass.Bass("TRN2", target_bir_lowering=False, debug=False)
    d = {}

    def din(name, shape, dt=F32):
        d[name] = nc.dram_tensor(name, shape, dt, kind="ExternalInput").ap()

    def dint(name, shape, dt=F32):
        d[name] = nc.dram_tensor(name, shape, dt, kind="Internal").ap()

    din("x0", [R, D]); din("wq", [D, 512]); din("wk", [D, 512])
    din("wv", [D, 512]); din("wo", [4, 128, D]); din("wgates", [D, H])
    din("rmsc", [128, DC]); din("maskIJ", [4, 128, 512], BF16)
    din("maskT", [4, 128, 512], BF16); din("identf", [128, 128])
    din("identb", [128, 128], BF16); din("erow", [1, 128])
    din("onesrow", [1, 128])
    din("w1", [D, F0]); din("w2", [F0, D]); din("Wh", [D, 2 * HID])
    din("Wo", [HID, D]); din("wg", [I1, D]); din("wgT", [D, I1])
    din("wu", [I1, D]); din("wuT", [D, I1]); din("wd", [D, I1])
    din("wdT", [I1, D])
    din("b1c", [128, FC0]); din("b2", [1, D])
    din("lngc", [128, DC]); din("lnbc", [128, DC])
    din("Wqk", [D, QK]); din("bhv", [1, HID]); din("bhgc", [128, HC])
    din("bqkc", [128, 1]); din("gb", [128, 4]); din("bor", [1, D])
    din("clipx", [1, 1]); din("clipd", [1, 1])
    dint("xnfm", [DC, 128, R], BF16); dint("tsc", [32, 128])
    dint("x1p", [R, D]); dint("x1", [R, D]); dint("xn1", [R, D])
    dint("normfm", [DC, 128, R], BF16); dint("x2p", [R, D])
    dint("x2q", [R, D]); dint("hmid", [IC1, 128, R])
    dint("lm1", [1, 1]); dint("lm2", [1, 1])
    dint("w1b", [D, F0], BF16); dint("w2b", [F0, D], BF16)
    dint("Whb", [D, 2 * HID], BF16); dint("Wob", [HID, D], BF16)
    dint("wgTi", [D, I1], BF16); dint("wuTi", [D, I1], BF16)
    dint("wdTq", [I1, D], BF16)
    dint("sgc", [128, IC1]); dint("suc", [128, IC1])
    dint("mrow_scr", [16, 128]); dint("crow_scr", [2, 16, 128])
    dint("v_scr", [16, 128, HID], BF16)
    dint("g_scr", [HC, 128, N], BF16); dint("gd_scr", [HC, 128, N], BF16)
    d["OUT"] = nc.dram_tensor("OUT", [R, D], F32, kind="ExternalOutput").ap()
    dint("nfm_scr", [DC, 128, R], BF16)
    cc1_in = nc.dram_tensor("cc1_in", [128, 128], F32, kind="Internal")
    cc1_out = nc.dram_tensor("cc1_out", [128, 128], F32, kind="Internal")
    cc2_in = nc.dram_tensor("cc2_in", [128, 128], F32, kind="Internal")
    cc2_out = nc.dram_tensor("cc2_out", [128, 128], F32, kind="Internal")

    with ExitStack() as octx:
        P = Prog(nc, octx)

        def phase(emit, dd, pfx):
            with ExitStack() as ctx:
                emit(nc, P, ctx, dd, pfx)
            P.join()

        phase(emit_phW, d, "W_")
        phase(emit_ph0, d, "A_")
        phase(emit_ph1, d, "B_")
        phase(emit_ph2, dict(d, x1p=d["x1p"]), "C_")
        phase(emit_ph3, d, "D_")

        def collective(src_ap, cin, cout, tag):
            with ExitStack() as ctx:
                pad = ctx.enter_context(
                    nc.sbuf_tensor(tag + "pad", [128, 128], F32))
                P.op("vector", lambda v: v.memset(pad[:], 0.0),
                     writes=[tag + "pad"])
                P.dma("qg", "gpsimd", lambda e: e.dma_start(
                    out=pad[0:1, 0:1], in_=src_ap),
                    reads=[(tag + "lm",)], writes=[tag + "pad0"])
                P.dma("qg", "gpsimd", lambda e: e.dma_start(
                    out=cin[:, :], in_=pad[:]),
                    reads=[tag + "pad", tag + "pad0"], writes=[tag + "in"])
                P.op("gpsimd", lambda g_: g_.collective_compute(
                    "AllReduce", ALU.max,
                    replica_groups=[list(range(NCORES))],
                    ins=[cin.ap().opt()], outs=[cout.ap().opt()]),
                    reads=[tag + "in"], writes=[tag + "done"])

        P.last_write[("cc1lm",)] = P.last_write.get(("lm1_hbm",))
        collective(d["lm1"], cc1_in, cc1_out, "cc1")
        phase(emit_ph4, d, "E_")
        d5a = dict(d, maxes=cc1_out.ap()[0:1, 0:8])
        phase(emit_ph5a, d5a, "F_")
        P.last_write[("cc2lm",)] = P.last_write.get(("lm2_hbm",))
        collective(d["lm2"], cc2_in, cc2_out, "cc2")
        P.join()
        d5b = dict(d, maxes=cc2_out.ap()[0:1, 0:8], x2q=d["x2q"])
        d5b["OUT"] = d["OUT"]
        phase(emit_ph5b, d5b, "G_")
        block = octx.enter_context(nc.Block())
        P.emit(block)
    return nc


_FUSED_NC = None


def kernel(**inp):
    global _FUSED_NC
    x = np.ascontiguousarray(np.asarray(inp["L0_attn_tokens"], np.float32)
                             .reshape(NCORES, SEQ * N, D))
    wq = np.ascontiguousarray(np.transpose(
        np.asarray(inp["L0_attn_wq"], np.float32), (1, 0, 2)).reshape(D, 512))
    wk = np.ascontiguousarray(np.transpose(
        np.asarray(inp["L0_attn_wk"], np.float32), (1, 0, 2)).reshape(D, 512))
    wv = np.ascontiguousarray(np.transpose(
        np.asarray(inp["L0_attn_wv"], np.float32), (1, 0, 2)).reshape(D, 512))
    wo = np.ascontiguousarray(np.asarray(inp["L0_attn_wo"], np.float32)
                              .reshape(4, 128, D))
    wgates = np.ascontiguousarray(np.asarray(inp["L0_attn_w_gates"], np.float32))
    rmsc = _cols(inp["L0_attn_rms_w"], DC)
    mask_T, mask_IJ = _host_masks()
    mask_T = mask_T.astype(ml_dtypes.bfloat16)
    mask_IJ = mask_IJ.astype(ml_dtypes.bfloat16)
    erow = np.zeros((1, 128), np.float32); erow[0, 64] = 1.0
    onesrow = np.ones((1, 128), np.float32)
    identf = np.eye(128, dtype=np.float32)
    identb = np.eye(128).astype(ml_dtypes.bfloat16)
    w1 = np.ascontiguousarray(np.asarray(inp["L0_ffn_w1"], np.float32))
    b1c = _cols(inp["L0_ffn_b1"], FC0)
    w2 = np.ascontiguousarray(np.asarray(inp["L0_ffn_w2"], np.float32))
    b2 = np.asarray(inp["L0_ffn_b2"], np.float32).reshape(1, D)
    Wh = np.ascontiguousarray(np.asarray(inp["L1_attn_Wh"], np.float32))
    Wo1 = np.ascontiguousarray(np.asarray(inp["L1_attn_Wo"], np.float32))
    wg = np.ascontiguousarray(np.asarray(inp["L1_ffn_w_gate"], np.float32))
    wu = np.ascontiguousarray(np.asarray(inp["L1_ffn_w_up"], np.float32))
    wd = np.ascontiguousarray(np.asarray(inp["L1_ffn_w_down"], np.float32))
    wgT = np.ascontiguousarray(wg.T)
    wuT = np.ascontiguousarray(wu.T)
    wdT = np.ascontiguousarray(wd.T)
    lngc = _cols(inp["L1_attn_ln_g"], DC)
    lnbc = _cols(inp["L1_attn_ln_b"], DC)
    bh = np.asarray(inp["L1_attn_bh"], np.float32)
    bhv = bh[:HID].reshape(1, HID)
    bhgc = _cols(bh[HID:], HC)
    bqkc = np.ascontiguousarray(np.asarray(inp["L1_attn_bqk"], np.float32)
                                .reshape(QK, 1))
    gamma = np.asarray(inp["L1_attn_gamma"], np.float32)
    beta = np.asarray(inp["L1_attn_beta"], np.float32)
    gb = np.ascontiguousarray(np.stack(
        [gamma[0], gamma[1], beta[0], beta[1]], axis=1))
    Wqk = np.ascontiguousarray(np.asarray(inp["L1_attn_Wqk"], np.float32))
    bo = np.asarray(inp["L1_attn_bo"], np.float32).reshape(1, D)
    clipx = np.asarray(inp["L1_ffn_clip_x"], np.float32).reshape(1, 1)
    clipd = np.asarray(inp["L1_ffn_clip_down"], np.float32).reshape(1, 1)

    if _FUSED_NC is None:
        _FUSED_NC = build_fused()
    nc = _FUSED_NC
    common = {"wq": wq, "wk": wk, "wv": wv, "wo": wo, "wgates": wgates,
              "rmsc": rmsc, "maskIJ": mask_IJ, "maskT": mask_T,
              "identf": identf, "identb": identb, "erow": erow,
              "onesrow": onesrow, "w1": w1, "w2": w2, "Wh": Wh, "Wo": Wo1,
              "wg": wg, "wgT": wgT, "wu": wu, "wuT": wuT, "wd": wd,
              "wdT": wdT, "b1c": b1c, "b2": b2, "lngc": lngc, "lnbc": lnbc,
              "Wqk": Wqk, "bhv": bhv, "bhgc": bhgc, "bqkc": bqkc, "gb": gb,
              "bor": bo, "clipx": clipx, "clipd": clipd}
    cores = list(range(NCORES))
    in_maps = [dict(common, x0=x[c]) for c in cores]
    res = run_bass_kernel_spmd(nc, in_maps, core_ids=cores)
    out = np.stack([res.results[c]["OUT"] for c in cores]).reshape(16, N, D)
    return out
